# version 07
# baseline (speedup 1.0000x reference)
import contextlib
import math
import numpy as np
import concourse.bass as bass
import concourse.mybir as mybir
from concourse.bass_utils import run_bass_kernel_spmd

F32 = mybir.dt.float32
BF16 = mybir.dt.bfloat16
AF = mybir.ActivationFunctionType
ALU = mybir.AluOpType
AX = mybir.AxisListType.X

NCORES = 8
T = 8192
C = 4096
HD = 128
HPC = 4
HC = HPC * HD
TOWN = T // NCORES
S = 512
NTS = T // S
NCC = C // 128
SCALE_MAIN = 1.0 / math.sqrt(HD)
SCALE_AD = 1.0 / math.sqrt(C)
EPS = 1e-5
NEG = -1.0e30

DEBUG_STAGE = 0


class _Trk:
    __slots__ = ("name", "step", "n")

    def __init__(self, name, step):
        self.name, self.step, self.n = name, step, 0

    def bump(self):
        self.n += self.step
        return self.n


class _Fut:
    __slots__ = ("trk", "val")

    def __init__(self):
        self.trk, self.val = None, None


class _Prog:
    ENGS = ("pe", "act", "dve", "pool", "sp")

    def __init__(self):
        self.ops = {e: [] for e in self.ENGS}
        self.tracks = {}

    def trk(self, name, step=1):
        if name not in self.tracks:
            self.tracks[name] = _Trk(name, step)
        return self.tracks[name]

    SERIAL = ("dve", "act", "pool")

    def emit(self, eng, fn, inc=None, waits=()):
        ws = [w for w in waits
              if w is not None and (isinstance(w, _Fut) or w[1])]
        swait = None
        if eng in self.SERIAL:
            et = self.trk(eng)
            if inc is None:
                inc = et
            if et.n > 0:
                swait = (et, et.n)
        if swait is not None:
            ws.append(swait)
        self.ops[eng].append((fn, inc, inc.bump() if inc is not None else None, ws))
        return (inc, inc.n) if inc is not None else None

    def realize(self, block, sems):
        def runner(ename):
            def go(e):
                lw = {}
                for fn, trk, _val, waits in self.ops[ename]:
                    need = {}
                    for w in waits:
                        if isinstance(w, _Fut):
                            if w.trk is None or not w.val:
                                continue
                            t, v = w.trk, w.val
                        else:
                            t, v = w
                        if lw.get(t.name, 0) < v:
                            need[t.name] = max(need.get(t.name, 0), v)
                    for tn, v in need.items():
                        e.wait_ge(sems[tn], v)
                        lw[tn] = v
                    ins = fn(e)
                    if trk is not None:
                        ins.then_inc(sems[trk.name], trk.step)
            return go
        block.tensor(runner("pe"))
        block.scalar(runner("act"))
        block.vector(runner("dve"))
        block.gpsimd(runner("pool"))
        block.sync(runner("sp"))


def _build(debug_stage=0, shared_cc=True):
    nc = bass.Bass("TRN2", target_bir_lowering=False, num_devices=NCORES)
    RG = [list(range(NCORES))]

    xT = nc.dram_tensor("xT", [C, TOWN], F32, kind="ExternalInput").ap()
    wqkvT = nc.dram_tensor("wqkvT", [128, 12 * 4096], F32, kind="ExternalInput").ap()
    wprojT = nc.dram_tensor("wprojT", [128, 32 * 4096], F32, kind="ExternalInput").ap()
    worgT = nc.dram_tensor("worgT", [128, 16 * 8192], F32, kind="ExternalInput").ap()
    downT = nc.dram_tensor("downT", [128, 8 * 2048], F32, kind="ExternalInput").ap()
    upw = nc.dram_tensor("upw", [64, 16 * 2048], F32, kind="ExternalInput").ap()
    wqT = nc.dram_tensor("wqT", [128, 4096], F32, kind="ExternalInput").ap()
    keysT = nc.dram_tensor("keysT", [128, 8], F32, kind="ExternalInput").ap()
    adT = nc.dram_tensor("adT", [C, 10], F32, kind="ExternalInput").ap()
    gat = nc.dram_tensor("gat", [1, 1], F32, kind="ExternalInput").ap()
    cosT = nc.dram_tensor("cosT", [128, T], F32, kind="ExternalInput").ap()
    sinS = nc.dram_tensor("sinS", [128, T], F32, kind="ExternalInput").ap()
    masks = nc.dram_tensor("masks", [4, 128, S], F32, kind="ExternalInput").ap()
    out = nc.dram_tensor("out", [TOWN, C], F32, kind="ExternalOutput").ap()

    h1own_d = nc.dram_tensor("h1own_d", [128, 32768], BF16, kind="Internal").ap()
    h1all_d = nc.dram_tensor("h1all_d", [NCORES * 128, 32768], BF16, kind="Internal", addr_space="Shared" if shared_cc else "Local").ap()
    wqkv_bf_d = nc.dram_tensor("wqkv_bf_d", [128, 12 * 4096], BF16, kind="Internal").ap()
    q_d = nc.dram_tensor("q_d", [HC, T], BF16, kind="Internal").ap()
    k_d = nc.dram_tensor("k_d", [HC, T], BF16, kind="Internal").ap()
    v_d = nc.dram_tensor("v_d", [128, 32768], BF16, kind="Internal").ap()
    yown_d = nc.dram_tensor("yown_d", [HC, T], BF16, kind="Internal").ap()
    yall_d = nc.dram_tensor("yall_d", [NCORES * HC, T], BF16, kind="Internal", addr_space="Shared" if shared_cc else "Local").ap()
    x2fm_d = nc.dram_tensor("x2fm_d", [128, 32768], F32, kind="Internal").ap()
    x2tm_d = nc.dram_tensor("x2tm_d", [TOWN, C], F32, kind="Internal").ap()
    bc_d = nc.dram_tensor("bc_d", [4, 512], BF16, kind="Internal").ap()
    hsin_d = nc.dram_tensor("hsin_d", [128, NCC], F32, kind="Internal").ap()
    hsout_d = nc.dram_tensor("hsout_d", [128, NCC], F32, kind="Internal").ap()

    dbg = {}
    if debug_stage >= 1:
        dbg["h1all"] = nc.dram_tensor("dbg_h1all", [NCORES * C, TOWN], BF16, kind="ExternalOutput").ap()
    if debug_stage >= 2:
        dbg["q"] = nc.dram_tensor("dbg_q", [HC, T], BF16, kind="ExternalOutput").ap()
        dbg["k"] = nc.dram_tensor("dbg_k", [HC, T], BF16, kind="ExternalOutput").ap()
        dbg["v"] = nc.dram_tensor("dbg_v", [128, 32768], BF16, kind="ExternalOutput").ap()
        dbg["ak"] = nc.dram_tensor("dbg_ak", [128, 40], BF16, kind="ExternalOutput").ap()
        dbg["av"] = nc.dram_tensor("dbg_av", [16, 512], BF16, kind="ExternalOutput").ap()
        dbg["wbf"] = nc.dram_tensor("dbg_wbf", [C, 3 * HC], BF16, kind="ExternalOutput").ap()
    if debug_stage >= 3:
        dbg["yall"] = nc.dram_tensor("dbg_yall", [NCORES * HC, T], BF16, kind="ExternalOutput").ap()
        dbg["P0"] = nc.dram_tensor("dbg_P0", [128, 1024], BF16, kind="ExternalOutput").ap()
        dbg["rows"] = nc.dram_tensor("dbg_rows", [8, 512], F32, kind="ExternalOutput").ap()
        dbg["expA"] = nc.dram_tensor("dbg_expA", [16, 512], BF16, kind="ExternalOutput").ap()
        dbg["expA2"] = nc.dram_tensor("dbg_expA2", [16, 512], BF16, kind="ExternalOutput").ap()
        dbg["oev"] = nc.dram_tensor("dbg_oev", [128, 512], F32, kind="ExternalOutput").ap()
    if debug_stage >= 4:
        dbg["x2"] = nc.dram_tensor("dbg_x2", [TOWN, C], F32, kind="ExternalOutput").ap()
        dbg["h2"] = nc.dram_tensor("dbg_h2", [C, TOWN], BF16, kind="ExternalOutput").ap()
        dbg["coeffs"] = nc.dram_tensor("dbg_coeffs", [1, 8], F32, kind="ExternalOutput").ap()

    ident_d = nc.inline_tensor(np.eye(128, dtype=np.float32), name="ident_c").ap()

    P = _Prog()
    t_pe = P.trk("pe")
    t_act = P.trk("act")
    t_dve = P.trk("dve")
    t_pool = P.trk("pool")
    L = {i: P.trk(f"L{i}", 16) for i in range(10)}
    Strk = {i: P.trk(f"S{i}", 16) for i in range(6)}
    t_li = P.trk("Li", 16)
    t_fr = [P.trk("FR0", 16), P.trk("FR1", 16)]
    t_rr = [P.trk("RR0", 16), P.trk("RR1", 16)]
    t_fb = [P.trk("FB0", 16), P.trk("FB1", 16)]
    t_rb = [P.trk("RB0", 16), P.trk("RB1", 16)]
    t_ys = [P.trk("YS0", 16), P.trk("YS1", 16)]
    t_ld = P.trk("Ld", 16)
    t_ld2 = P.trk("Ld2", 16)
    t_cc = P.trk("cc", 1)

    stack = contextlib.ExitStack()
    ar1 = stack.enter_context(nc.sbuf_tensor("ar1", [128, 32768], BF16))
    ar2 = stack.enter_context(nc.sbuf_tensor("ar2", [128, 32768], BF16))
    ar3 = stack.enter_context(nc.sbuf_tensor("ar3", [128, 16384], BF16))
    ones_bf = stack.enter_context(nc.sbuf_tensor("ones_bf", [128, 1], BF16))
    onesr_bf = stack.enter_context(nc.sbuf_tensor("onesr_bf", [1, 128], BF16))
    onesr_f = stack.enter_context(nc.sbuf_tensor("onesr_f", [1, 128], F32))
    ident_sb = stack.enter_context(nc.sbuf_tensor("ident_sb", [128, 128], F32))
    akT_sb = stack.enter_context(nc.sbuf_tensor("akT_sb", [128, 40], BF16))
    av_sb = stack.enter_context(nc.sbuf_tensor("av_sb", [16, 512], BF16))
    adT_sb = stack.enter_context(nc.sbuf_tensor("adT_sb", [128, 320], BF16))
    g_sb = stack.enter_context(nc.sbuf_tensor("g_sb", [1, 1], F32))
    ce_sb = stack.enter_context(nc.sbuf_tensor("ce_sb", [128, 8], F32))
    keys_sb = stack.enter_context(nc.sbuf_tensor("keys_sb", [128, 8], F32))
    qT_sb = stack.enter_context(nc.sbuf_tensor("qT_sb", [128, 1], F32))
    hsum_sb = stack.enter_context(nc.sbuf_tensor("hsum_sb", [128, NCC], F32))
    mrow_sb = stack.enter_context(nc.sbuf_tensor("mrow_sb", [1, 4096], F32))
    srow_sb = stack.enter_context(nc.sbuf_tensor("srow_sb", [1, 1024], F32))
    mask_sb = stack.enter_context(nc.sbuf_tensor("mask_sb", [128, 4 * S], F32))

    a1 = ar1.ap()
    a2 = ar2.ap()
    a3 = ar3.ap()
    f1 = ar1.ap().bitcast(F32)
    f2 = ar2.ap().bitcast(F32)
    f3 = ar3.ap().bitcast(F32)

    SA = stack.enter_context(nc.psum_tensor("SA", [128, 1024], F32))
    SB = stack.enter_context(nc.psum_tensor("SB", [128, 1024], F32))
    OA = stack.enter_context(nc.psum_tensor("OA", [128, 512], F32))
    OB = stack.enter_context(nc.psum_tensor("OB", [128, 512], F32))
    SUMS = stack.enter_context(nc.psum_tensor("SUMS", [128, 512], F32))
    AUX = stack.enter_context(nc.psum_tensor("AUX", [128, 512], F32))

    E = P.emit

    E("pool", lambda e: e.memset(ones_bf[:, :], 1.0))
    E("pool", lambda e: e.memset(onesr_bf[:, :], 1.0))
    E("pool", lambda e: e.memset(onesr_f[:, :], 1.0))
    E("pool", lambda e: e.memset(av_sb[:, :], 0.0), inc=t_pool)
    mi_ones = (t_pool, t_pool.n)
    E("sp", lambda e: e.dma_start(out=ident_sb[:, :], in_=ident_d), inc=t_li)
    E("sp", lambda e: e.dma_start(
        out=mask_sb[:, :].rearrange("p (m s) -> p m s", m=4),
        in_=masks.rearrange("m p s -> p m s")), inc=t_li)
    E("sp", lambda e: e.dma_start(out=g_sb[:, :], in_=gat), inc=t_li)
    E("sp", lambda e: e.dma_start(out=keys_sb[:, :], in_=keysT), inc=t_li)
    E("sp", lambda e: e.dma_start(
        out=f3[:, 0:320].rearrange("p (cc a) -> p cc a", a=10),
        in_=adT.rearrange("(cc p) a -> p cc a", p=128)), inc=t_li)
    mi_init = (t_li, t_li.n)
    mi_ident = mi_masks = mi_gat = mi_keys = mi0 = mi_init
    E("pool", lambda e: e.tensor_copy(out=adT_sb[:, :], in_=f3[:, 0:320]),
      inc=t_pool, waits=[mi0])
    mi_adT = (t_pool, t_pool.n)

    free = {}
    for cc in range(NCC):
        sl = cc % 2
        xf = f3[:, 512 + 1024 * sl: 512 + 1024 * sl + 1024]
        xb = a1[:, 1024 * sl:1024 * sl + 1024]
        sq = a1[:, 2048 + 1024 * sl:2048 + 1024 * sl + 1024]
        E("sp", lambda e, d=xf, s=xT[128 * cc:128 * cc + 128, :]:
          e.dma_start(out=d, in_=s), inc=L[sl],
          waits=[free.get(("a0x", sl)), mi_adT if cc < 2 else None])
        mi_x = (L[sl], L[sl].n)
        E("pool", lambda e, d=xb, s=xf: e.tensor_copy(out=d, in_=s),
          inc=t_pool, waits=[mi_x, free.get(("a0b", sl))])
        mi_xb = (t_pool, t_pool.n)
        free[("a0x", sl)] = mi_xb
        E("act", lambda e, d=sq, s=xb: e.activation(out=d, in_=s, func=AF.Square),
          inc=t_act, waits=[mi_xb, free.get(("a0s", sl))])
        mi_sq = (t_act, t_act.n)
        st, sp_ = (cc == 0), (cc == NCC - 1)
        for h in range(2):
            bank = SUMS if h == 0 else AUX
            E("pe", lambda e, b=bank, r=xb, h_=h, st_=st, sp__=sp_:
              e.matmul(b[0:1, :], ones_bf[:, :], r[:, 512 * h_:512 * h_ + 512],
                       start=st_, stop=sp__),
              waits=[mi_xb, mi_ones] if (h == 0) else [])
            E("pe", lambda e, b=bank, r=sq, h_=h, st_=st, sp__=sp_:
              e.matmul(b[32:33, :], ones_bf[:, :], r[:, 512 * h_:512 * h_ + 512],
                       start=st_, stop=sp__),
              inc=t_pe if h == 1 else None, waits=[mi_sq] if h == 0 else [])
        free[("a0b", sl)] = (t_pe, t_pe.n)
        free[("a0s", sl)] = (t_pe, t_pe.n)
    mi_stats1 = (t_pe, t_pe.n)

    def ln_rows(mi_done, roff):
        out_mis = []
        for h in range(2):
            bank = SUMS if h == 0 else AUX
            m_h = mrow_sb[0:1, roff + 512 * h: roff + 512 * h + 512]
            r_h = mrow_sb[0:1, roff + 1024 + 512 * h: roff + 1024 + 512 * h + 512]
            tmp = srow_sb[0:1, 512 * h:512 * h + 512]
            E("act", lambda e, d=m_h, b=bank: e.mul(out=d, in_=b[0:1, :], mul=1.0 / C),
              inc=t_act, waits=[mi_done])
            mi_m = (t_act, t_act.n)
            E("act", lambda e, d=r_h, b=bank: e.mul(out=d, in_=b[32:33, :], mul=1.0 / C),
              inc=t_act)
            mi_e2 = (t_act, t_act.n)
            E("dve", lambda e, d=tmp, s=m_h: e.tensor_mul(d, s, s),
              inc=t_dve, waits=[mi_m])
            E("dve", lambda e, d=r_h, s=tmp: e.tensor_sub(d, d, s),
              inc=t_dve, waits=[mi_e2])
            E("dve", lambda e, d=r_h: e.tensor_scalar_add(d, d, EPS), inc=t_dve)
            mi_v = (t_dve, t_dve.n)
            E("act", lambda e, d=r_h: e.activation(out=d, in_=d, func=AF.Sqrt),
              inc=t_act, waits=[mi_v])
            mi_s = (t_act, t_act.n)
            E("dve", lambda e, d=r_h: e.reciprocal(d, d), inc=t_dve, waits=[mi_s])
            out_mis.append((t_dve, t_dve.n))
        return out_mis[-1]

    mi_rows1 = ln_rows(mi_stats1, 0)

    h1_mi = None
    prev_half_done = None
    for h in range(2):
        m_h = mrow_sb[0:1, 512 * h:512 * h + 512]
        r_h = mrow_sb[0:1, 1024 + 512 * h:1024 + 512 * h + 512]
        E("pe", lambda e, r=m_h: e.matmul(OA[:, :], onesr_f[:, :], r, start=True, stop=True),
          inc=t_pe, waits=[mi_rows1, prev_half_done])
        E("pe", lambda e, r=r_h: e.matmul(OB[:, :], onesr_f[:, :], r, start=True, stop=True),
          inc=t_pe)
        mi_bc = (t_pe, t_pe.n)
        for cc in range(NCC):
            sl = cc % 2
            xf = f3[:, 512 + 512 * sl:512 + 512 * sl + 512]
            tm = f3[:, 1536 + 512 * sl:1536 + 512 * sl + 512]
            hb = a1[:, 4096 + 512 * sl:4096 + 512 * sl + 512]
            E("sp", lambda e, d=xf, s=xT[128 * cc:128 * cc + 128, 512 * h:512 * h + 512]:
              e.dma_start(out=d, in_=s), inc=L[sl],
              waits=[free.get(("nx", sl)), mi_stats1 if cc < 2 else None])
            mi_x = (L[sl], L[sl].n)
            E("dve", lambda e, d=tm, s=xf: e.tensor_sub(d, s, OA[:, :]),
              inc=t_dve, waits=[mi_x, mi_bc, free.get(("ntm", sl))])
            mi_sub = (t_dve, t_dve.n)
            free[("nx", sl)] = mi_sub
            E("dve", lambda e, d=hb, s=tm: e.tensor_mul(d, s, OB[:, :]),
              inc=t_dve, waits=[free.get(("nh", sl))])
            mi_h1 = (t_dve, t_dve.n)
            free[("ntm", sl)] = mi_h1
            E("act", lambda e, s=hb, cc_=cc, h_=h:
              e.dma_start(out=h1own_d[:, 16384 * h_ + 512 * cc_:16384 * h_ + 512 * cc_ + 512], in_=s),
              inc=Strk[sl], waits=[mi_h1])
            free[("nh", sl)] = (Strk[sl], Strk[sl].n)
        prev_half_done = (t_dve, t_dve.n)
    h1_mi = [(Strk[0], Strk[0].n), (Strk[1], Strk[1].n)]

    E("pool", lambda e: e.collective_compute(
        "AllGather", ALU.bypass, replica_groups=RG, ins=[h1own_d.opt()], outs=[h1all_d.opt()]),
      inc=t_cc, waits=h1_mi)
    mi_ag_h1 = (t_cc, t_cc.n)

    mi_wspill = None
    for i in range(12):
        sl = i % 2
        wf = f1[:, 4096 + 4096 * sl:4096 + 4096 * sl + 4096]
        wb = a1[:, 24576 + 4096 * sl:24576 + 4096 * sl + 4096]
        E("sp", lambda e, d=wf, s=wqkvT[:, 4096 * i:4096 * i + 4096]:
          e.dma_start(out=d, in_=s),
          inc=L[2 + sl], waits=[free.get(("wsf", sl))])
        mi_w = (L[2 + sl], L[2 + sl].n)
        E("dve", lambda e, d=wb, s=wf: e.tensor_copy(out=d, in_=s),
          inc=t_dve, waits=[mi_w, free.get(("wsb", sl))])
        mi_wb = (t_dve, t_dve.n)
        free[("wsf", sl)] = mi_wb
        E("act", lambda e, s=wb, i_=i:
          e.dma_start(out=wqkv_bf_d[:, 4096 * i_:4096 * i_ + 4096], in_=s),
          inc=Strk[2 + sl], waits=[mi_wb])
        free[("wsb", sl)] = (Strk[2 + sl], Strk[2 + sl].n)
    mi_wspill = [(Strk[2], Strk[2].n), (Strk[3], Strk[3].n)]

    if debug_stage >= 1:
        E("pool", lambda e: e.dma_start(out=dbg["h1all"], in_=h1all_d),
          inc=t_ld, waits=[mi_ag_h1])
    if debug_stage == 1:
        return _finish(nc, P, stack)

    qkv_banks = [OA[:, :], OB[:, :], SA[:, 0:512], SA[:, 512:1024]]
    h1_slots = [a1[:, 0:16384], a1[:, 16384:32768], a2[:, 13312:29696]]
    mi_adchain = mi_adT
    mi_h1slot = {}
    for tp in range(8):
        mi_h1s = {}
        for hf in range(2):
            ts = 2 * tp + hf
            slot = (2 * tp + hf) % 3
            h1s = h1_slots[slot]
            E("sp", lambda e, d=h1s, b=ts // 2, f=ts % 2:
              e.dma_start(out=d, in_=h1all_d[128 * b:128 * b + 128,
                                             16384 * f:16384 * f + 16384]),
              inc=L[slot], waits=[mi_ag_h1, mi_h1slot.get(slot)]
              + (mi_wspill if ts < 3 else []))
            mi_h1s[hf] = ((L[slot], L[slot].n), h1s, slot)
        csl = {}
        cs_trk = {(0, 0): L[4], (0, 1): L[5], (1, 0): L[8], (1, 1): L[9]}
        for hf in range(2):
            ts = 2 * tp + hf
            base = 3072 + 2048 * (tp % 2) + 1024 * hf
            ctk = cs_trk[(tp % 2, hf)]
            E("sp", lambda e, d=f3[:, base:base + 512], s_=cosT[:, 512 * ts:512 * ts + 512]:
              e.dma_start(out=d, in_=s_), inc=ctk, waits=[free.get(("cs", tp % 2, hf))])
            E("sp", lambda e, d=f3[:, base + 512:base + 1024], s_=sinS[:, 512 * ts:512 * ts + 512]:
              e.dma_start(out=d, in_=s_), inc=ctk)
            csl[hf] = ((ctk, ctk.n), f3[:, base:base + 512], f3[:, base + 512:base + 1024])
        for ot in range(12):
            wsl = ot % 2
            wb = a2[:, 4096 * wsl:4096 * wsl + 4096]
            E("sp", lambda e, d=wb, o=ot:
              e.dma_start(out=d, in_=wqkv_bf_d[:, 4096 * o:4096 * o + 4096]),
              inc=L[6 + wsl], waits=(mi_wspill if tp == 0 else []) + [free.get(("w", wsl))])
            mi_w = (L[6 + wsl], L[6 + wsl].n)
            bk0, bk1 = (0, 1) if ot % 2 == 0 else (2, 3)
            for cc in range(NCC):
                for hf in range(2):
                    bank = qkv_banks[bk0 + hf]
                    E("pe", lambda e, b=bank, w=wb, h_=mi_h1s[hf][1], cc_=cc,
                      st=(cc == 0), sp_=(cc == NCC - 1):
                      e.matmul(b, w[:, 128 * cc_:128 * cc_ + 128],
                               h_[:, 512 * cc_:512 * cc_ + 512], start=st, stop=sp_),
                      inc=t_pe if (cc == NCC - 1 and hf == 1) else None,
                      waits=[mi_w, mi_h1s[0][0], mi_h1s[1][0],
                             free.get(("qb", bk0)), free.get(("qb", bk0 + 1))]
                      if (cc == 0 and hf == 0) else [])
            mi_mm = (t_pe, t_pe.n)
            free[("w", wsl)] = mi_mm
            mi_h1slot[mi_h1s[0][2]] = mi_mm
            mi_h1slot[mi_h1s[1][2]] = mi_mm
            if tp == 0 and ot >= 4:
                for cc in range(NCC):
                    E("pe", lambda e, w=wb, cc_=cc, st=(cc == 0), sp_=(cc == NCC - 1):
                      e.matmul(SUMS[:, 0:10], w[:, 128 * cc_:128 * cc_ + 128],
                               adT_sb[:, 10 * cc_:10 * cc_ + 10], start=st, stop=sp_),
                      inc=t_pe if cc == NCC - 1 else None,
                      waits=[mi_adchain] if cc == 0 else [])
                mi_amm = (t_pe, t_pe.n)
                free[("w", wsl)] = mi_amm
                if ot < 8:
                    E("dve", lambda e, o=ot: e.tensor_copy(
                        out=akT_sb[:, 10 * (o - 4):10 * (o - 4) + 10], in_=SUMS[:, 0:10]),
                      inc=t_dve, waits=[mi_amm])
                    mi_adchain = (t_dve, t_dve.n)
                else:
                    avt = f3[:, 7168:7184]
                    E("dve", lambda e, d=avt: e.tensor_copy(out=d[:, 0:10], in_=SUMS[:, 0:10]),
                      inc=t_dve, waits=[mi_amm])
                    mi_ae = (t_dve, t_dve.n)
                    E("pe", lambda e, s_=avt: e.transpose(AUX[0:10, 0:128], s_[:, 0:10], ident_sb[:, :]),
                      inc=t_pe, waits=[mi_ae, mi_ident, free.get("aux_a1")])
                    mi_tr = (t_pe, t_pe.n)
                    E("act", lambda e, o=ot: e.copy(
                        out=av_sb[0:10, 128 * (o - 8):128 * (o - 8) + 128], in_=AUX[0:10, 0:128]),
                      inc=t_act, waits=[mi_tr])
                    mi_adchain = (t_act, t_act.n)
                    free["aux_a1"] = mi_adchain
            for hf in range(2):
                ts = 2 * tp + hf
                bank = qkv_banks[bk0 + hf]
                par = bk0 + hf
                if ot < 8:
                    ev = f3[:, 512 * hf:512 * hf + 512]
                    tt1 = f3[:, 1024 + 1024 * hf:1024 + 1024 * hf + 512]
                    ro = f3[:, 1024 + 1024 * hf + 512:1024 + 1024 * hf + 1024]
                    ob = a2[:, 8192 + 512 * hf:8192 + 512 * hf + 512]
                    E("dve", lambda e, d=ev, b=bank: e.tensor_copy(out=d, in_=b),
                      inc=t_dve, waits=[mi_mm, free.get(("ev", hf))])
                    mi_ev = (t_dve, t_dve.n)
                    free[("qb", par)] = mi_ev
                    cs_mi, cs_c, cs_s = csl[hf]
                    E("pool", lambda e, d=tt1, s_=ev, c_=cs_c: e.tensor_mul(d, s_, c_),
                      inc=t_pool, waits=[mi_ev, cs_mi, free.get(("t1", hf))])
                    E("pool", lambda e, d=ro, s_=ev: e.tensor_copy(out=d[0:64, :], in_=s_[64:128, :]))
                    E("pool", lambda e, d=ro, s_=ev: e.tensor_copy(out=d[64:128, :], in_=s_[0:64, :]),
                      inc=t_pool)
                    mi_rot = (t_pool, t_pool.n)
                    free[("ev", hf)] = mi_rot
                    E("pool", lambda e, d=ro, s_=cs_s: e.tensor_mul(d, d, s_), inc=t_pool)
                    E("pool", lambda e, d=ob, a_=tt1, b_=ro: e.tensor_add(d, a_, b_),
                      inc=t_pool, waits=[free.get(("ob", hf))])
                    mi_ob = (t_pool, t_pool.n)
                    free[("t1", hf)] = mi_ob
                    free[("cs", tp % 2, hf)] = mi_ob
                    dst = q_d if ot < 4 else k_d
                    E("act", lambda e, s_=ob, d=dst, r=128 * (ot % 4), t0=512 * ts:
                      e.dma_start(out=d[r:r + 128, t0:t0 + 512], in_=s_),
                      inc=Strk[hf], waits=[mi_ob])
                    free[("ob", hf)] = (Strk[hf], Strk[hf].n)
                else:
                    ev = f3[:, 512 * hf:512 * hf + 512]
                    vsb = a2[:, 9216 + 2048 * hf:9216 + 2048 * hf + 2048]
                    E("dve", lambda e, d=ev, b=bank: e.tensor_copy(out=d, in_=b),
                      inc=t_dve, waits=[mi_mm, free.get(("ev", hf))])
                    mi_ev = (t_dve, t_dve.n)
                    free[("qb", par)] = mi_ev
                    for j in range(4):
                        E("pe", lambda e, s_=ev, j_=j: e.transpose(
                            AUX[:, 128 * j_:128 * j_ + 128], s_[:, 128 * j_:128 * j_ + 128],
                            ident_sb[:, :]),
                          inc=t_pe if j == 3 else None,
                          waits=[mi_ev, mi_ident, free.get("aux_a1")] if j == 0 else [])
                    mi_tr = (t_pe, t_pe.n)
                    free[("ev", hf)] = mi_tr
                    for j in range(4):
                        E("act", lambda e, d=vsb, j_=j, c0=128 * (ot - 8):
                          e.copy(out=d[:, 512 * j_ + c0:512 * j_ + c0 + 128],
                                 in_=AUX[:, 128 * j_:128 * j_ + 128]),
                          inc=t_act if j == 3 else None,
                          waits=[mi_tr, free.get(("vsb", hf)) if j == 0 else None]
                          if j == 0 else [])
                    mi_vs = (t_act, t_act.n)
                    free["aux_a1"] = mi_vs
                    if ot == 11:
                        E("act", lambda e, s_=vsb, t_=ts:
                          e.dma_start(out=v_d[:, 2048 * t_:2048 * t_ + 2048], in_=s_),
                          inc=Strk[2 + hf], waits=[mi_vs])
                        free[("vsb", hf)] = (Strk[2 + hf], Strk[2 + hf].n)
    mi_a1_st = [(Strk[i], Strk[i].n) for i in range(4)]
    mi_a1 = mi_a1_st + [(t_pe, t_pe.n), (t_pool, t_pool.n), (t_dve, t_dve.n)]

    if debug_stage >= 2:
        for nm, src in (("q", q_d), ("k", k_d), ("v", v_d)):
            E("pool", lambda e, n=nm, s_=src: e.dma_start(out=dbg[n], in_=s_), inc=t_ld, waits=mi_a1)
        E("pool", lambda e: e.dma_start(out=dbg["ak"], in_=akT_sb[:, :]), inc=t_ld, waits=mi_a1)
        E("pool", lambda e: e.dma_start(out=dbg["av"], in_=av_sb[:, :]), inc=t_ld, waits=mi_a1)
        E("pool", lambda e: e.dma_start(out=dbg["wbf"], in_=wqkv_bf_d), inc=t_ld, waits=mi_a1)
    if debug_stage == 2:
        return _finish(nc, P, stack)

    E("sp", lambda e: e.dma_start(out=a1[:, :].rearrange("p (h t) -> p h t", h=4),
                                  in_=k_d.rearrange("(h p) t -> p h t", p=128)),
      inc=L[2], waits=mi_a1)
    mi_K = (L[2], L[2].n)
    E("sp", lambda e: e.dma_start(out=a2[:, :], in_=v_d),
      inc=L[3], waits=mi_a1)
    mi_V = (L[3], L[3].n)

    mi_sgrp = {}
    mi_pslot = {}
    mi_oband = {}
    mi_mfin = {}
    mi_expA = {}
    mi_expA2 = {}
    mi_fbc = {}
    mi_rbc = {}
    mi_ybf = {}
    mi_oev = {}
    deferred = []

    def flush_defer():
        for eng, fn, inc, waits, fut in deferred:
            mi = E(eng, fn, inc=inc, waits=waits)
            if fut is not None:
                fut.trk, fut.val = mi
        deferred.clear()

    def defer(eng, fn, inc, waits, fut=None):
        deferred.append((eng, fn, inc, waits, fut))

    for c in range(NTS):
        sl = c % 2
        qs = a3[:, 2048 * sl:2048 * sl + 2048]
        E("sp", lambda e, d=qs, c_=c:
          e.dma_start(out=d.rearrange("p (h t) -> p h t", h=4),
                      in_=q_d[:, 512 * c_:512 * c_ + 512].rearrange("(h p) t -> p h t", p=128)),
          inc=L[4 + sl], waits=mi_a1_st[:2] + [free.get(("qs", sl))])
        mi_q = (L[4 + sl], L[4 + sl].n)
        nkt = 4 * (c + 1)
        npair = nkt // 2
        for h in range(HPC):
            ph = h % 2
            qv = qs[:, 512 * h:512 * h + 512]
            oband = OA if ph == 0 else OB
            sbank = SUMS if ph == 0 else AUX

            def emit_S(p_, first=False):
                sgrp = SA if p_ % 2 == 0 else SB
                for jj in range(2):
                    j = 2 * p_ + jj
                    E("pe", lambda e, g=sgrp, h_=h, j_=j, o=512 * jj, q_=qv:
                      e.matmul(g[:, o:o + 512],
                               a1[:, 8192 * h_ + 128 * j_:8192 * h_ + 128 * j_ + 128],
                               q_, start=True, stop=True),
                      inc=t_pe if jj == 1 else None,
                      waits=[mi_K, mi_q, mi_sgrp.get(p_ % 2)] if jj == 0 else [])
                return (t_pe, t_pe.n)

            mi_s = {}
            mi_s[0] = emit_S(0)
            if npair > 1:
                mi_s[1] = emit_S(1)
            dp = min(2, npair - 1)
            for p_ in range(npair):
                sgrp = SA if p_ % 2 == 0 else SB
                mi_mask = None
                for jj in range(2):
                    o = 2 * p_ + jj - 4 * c
                    if 0 <= o < 4:
                        E("dve", lambda e, g=sgrp, off=512 * jj, o_=o:
                          e.tensor_add(g[:, off:off + 512], g[:, off:off + 512],
                                       mask_sb[:, 512 * o_:512 * o_ + 512]),
                          inc=t_dve, waits=[mi_s[p_], mi_masks])
                        mi_mask = (t_dve, t_dve.n)
                psl = p_ % 3
                pb = a3[:, 4096 + 1024 * psl:4096 + 1024 * psl + 1024]
                E("act", lambda e, d=pb, g=sgrp:
                  e.activation(out=d, in_=g[:, :], func=AF.Exp, scale=SCALE_MAIN),
                  inc=t_act, waits=[mi_s[p_], mi_mask, mi_pslot.get(psl)])
                mi_e = (t_act, t_act.n)
                mi_sgrp[p_ % 2] = mi_e
                if p_ + 2 < npair:
                    mi_s[p_ + 2] = emit_S(p_ + 2)
                if p_ == dp:
                    flush_defer()
                for jj in range(2):
                    j = 2 * p_ + jj
                    E("pe", lambda e, ob=oband, j_=j, h_=h, p__=pb:
                      e.matmul(ob[:, :], a2[:, 512 * j_ + 128 * h_:512 * j_ + 128 * h_ + 128],
                               p__[:, 512 * (j_ % 2):512 * (j_ % 2) + 512],
                               start=(j_ == 0), stop=False),
                      waits=([mi_e, mi_V] + ([mi_oband.get(ph)] if j == 0 else []))
                      if jj == 0 else [])
                    E("pe", lambda e, j_=j, p__=pb, b=sbank, nk=nkt:
                      e.matmul(b[0:1, :], ones_bf[:, :],
                               p__[:, 512 * (j_ % 2):512 * (j_ % 2) + 512],
                               start=(j_ == 0), stop=(j_ == nk - 1)),
                      inc=t_pe if jj == 1 else None,
                      waits=[mi_mfin.get(ph)] if j == 0 else [])
                mi_pslot[psl] = (t_pe, t_pe.n)
            if deferred:
                flush_defer()
            mi_kloop = (t_pe, t_pe.n)
            E("pe", lambda e, h_=h, q_=qv, b=sbank:
              e.matmul(b[32:42, :], akT_sb[:, 10 * h_:10 * h_ + 10], q_, start=True, stop=True),
              inc=t_pe, waits=[mi_expA.get(ph)])
            mi_as = (t_pe, t_pe.n)
            expA = a3[:, 7168 + 512 * ph:7168 + 512 * ph + 512]
            E("act", lambda e, d=expA, b=sbank:
              e.activation(out=d[0:10, :], in_=b[32:42, :], func=AF.Exp, scale=SCALE_AD),
              inc=t_act, waits=[mi_as, mi_expA2.get(ph)])
            mi_ea = (t_act, t_act.n)
            mi_expA[ph] = mi_ea
            E("pe", lambda e, s_=expA, b=sbank:
              e.matmul(b[64:65, :], ones_bf[0:10, :], s_[0:10, :], start=True, stop=True),
              inc=t_pe, waits=[mi_ea])
            mi_asum = (t_pe, t_pe.n)
            lnr = srow_sb[0:1, 512 * ph:512 * ph + 512]
            E("act", lambda e, d=lnr, b=sbank:
              e.activation(out=d, in_=b[64:65, :], func=AF.Ln),
              inc=t_act, waits=[mi_asum])
            E("act", lambda e, d=lnr:
              e.activation(out=d, in_=d, func=AF.Exp, scale=-1.0),
              inc=t_act)
            mi_ras = (t_act, t_act.n)
            frow = a3[0:1, 13312 + 512 * ph:13312 + 512 * ph + 512]
            E("dve", lambda e, d=frow, r=lnr, b=sbank:
              e.scalar_tensor_tensor(d, b[0:1, :], g_sb[0:1, 0:1], r,
                                     op0=ALU.mult, op1=ALU.mult),
              inc=t_dve, waits=[mi_ras, mi_gat, mi_fbc.get(ph)])
            mi_f = (t_dve, t_dve.n)
            E("act", lambda e, s_=frow, p=ph: e.dma_start(out=bc_d[p:p + 1, :], in_=s_),
              inc=t_fr[ph], waits=[mi_f, mi_fbc.get(ph)])
            mi_fd = (t_fr[ph], t_fr[ph].n)
            fbc = a3[0:10, 9216 + 512 * ph:9216 + 512 * ph + 512]
            E("sp", lambda e, d=fbc, p=ph:
              e.dma_start(out=d, in_=bc_d[p:p + 1, :].broadcast_to([10, 512])),
              inc=t_fb[ph], waits=[mi_fd, mi_expA2.get(ph)])
            mi_fb = (t_fb[ph], t_fb[ph].n)
            mi_fbc[ph] = mi_fb
            expA2 = a3[0:10, 8192 + 512 * ph:8192 + 512 * ph + 512]
            E("pool", lambda e, d=expA2, a=expA, f_=fbc:
              e.tensor_mul(d, a[0:10, :], f_),
              inc=t_pool, waits=[mi_fb, mi_ea])
            mi_e2 = (t_pool, t_pool.n)
            mi_expA2[ph] = mi_e2
            rrow = a3[0:1, 14336 + 512 * ph:14336 + 512 * ph + 512]
            lnm = srow_sb[0:1, 512 * ph:512 * ph + 512]
            E("act", lambda e, d=lnm, b=sbank:
              e.activation(out=d, in_=b[0:1, :], func=AF.Ln),
              inc=t_act, waits=[mi_f])
            E("act", lambda e, d=rrow, s_=lnm:
              e.activation(out=d, in_=s_, func=AF.Exp, scale=-1.0),
              inc=t_act)
            mi_r = (t_act, t_act.n)
            mi_mfin[ph] = mi_r
            E("act", lambda e, s_=rrow, p=ph: e.dma_start(out=bc_d[2 + p:3 + p, :], in_=s_),
              inc=t_rr[ph], waits=[mi_rbc.get(ph)])
            mi_rd = (t_rr[ph], t_rr[ph].n)
            rbc = a3[:, 10240 + 512 * ph:10240 + 512 * ph + 512]
            E("sp", lambda e, d=rbc, p=ph:
              e.dma_start(out=d, in_=bc_d[2 + p:3 + p, :].broadcast_to([128, 512])),
              inc=t_rb[ph], waits=[mi_rd, mi_rbc.get(ph)])
            mi_rb = (t_rb[ph], t_rb[ph].n)
            fut_ao = _Fut()
            defer("pe",
                  (lambda h_, s_, ob: lambda e:
                   e.matmul(ob[:, :], av_sb[0:10, 128 * h_:128 * h_ + 128], s_[0:10, :],
                            start=False, stop=True))(h, expA2, oband),
                  t_pe, [mi_e2], fut_ao)
            oev = f3[:, 6144 + 512 * ph:6144 + 512 * ph + 512]
            fut_oe = _Fut()
            defer("act",
                  (lambda d, ob: lambda e: e.copy(out=d, in_=ob[:, :]))(oev, oband),
                  t_act, [fut_ao, mi_oev.get(ph)], fut_oe)
            ybf = a3[:, 11264 + 512 * ph:11264 + 512 * ph + 512]
            fut_ytt = _Fut()
            defer("pool",
                  (lambda d, s_, r_: lambda e: e.tensor_mul(d, s_, r_))(ybf, oev, rbc),
                  t_pool, [mi_rb, fut_oe, mi_ybf.get(ph)], fut_ytt)
            fut_yst = _Fut()
            defer("act",
                  (lambda s_, h_, c_: lambda e:
                   e.dma_start(out=yown_d[128 * h_:128 * h_ + 128, 512 * c_:512 * c_ + 512],
                               in_=s_))(ybf, h, c),
                  t_ys[ph], [fut_ytt], fut_yst)
            mi_oband[ph] = fut_ytt
            mi_ybf[ph] = fut_yst
            mi_oev[ph] = fut_ytt
            mi_rbc[ph] = fut_ytt
        free[("qs", sl)] = (t_pe, t_pe.n)
    flush_defer()
    mi_b = [(t_ys[0], t_ys[0].n), (t_ys[1], t_ys[1].n), (t_pe, t_pe.n), (t_dve, t_dve.n)]

    E("pool", lambda e: e.collective_compute(
        "AllGather", ALU.bypass, replica_groups=RG, ins=[yown_d.opt()], outs=[yall_d.opt()]),
      inc=t_cc, waits=mi_b)
    mi_ag_y = (t_cc, t_cc.n)

    if debug_stage >= 3:
        E("pool", lambda e: e.dma_start(out=dbg["yall"], in_=yall_d),
          inc=t_ld, waits=[mi_ag_y])
    if debug_stage == 3:
        return _finish(nc, P, stack)

    zbanks = [[OA[:, :], OB[:, :]], [SA[:, 0:512], SA[:, 512:1024]]]
    for th in range(2):
        def _ldy(e, th_=th):
            pid = e.partition_id()
            return e.dma_start(out=a1[:, 16384 * th_:16384 * th_ + 16384]
                               .rearrange("p (cc t) -> p cc t", t=512),
                               in_=yall_d.rearrange("(cc p) t -> p cc t", p=128)
                               [:, :, bass.ts(pid * 2 + th_, 512)])
        E("sp", _ldy, inc=L[th], waits=[mi_ag_y])
        free[("yld", th)] = (L[th], L[th].n)
    mi_y = [free[("yld", 0)], free[("yld", 1)]]
    for ot in range(NCC):
        wsl = ot % 2
        wf = f2[:, 8192 + 4096 * wsl:8192 + 4096 * wsl + 4096]
        wb = a3[:, 4096 * wsl:4096 * wsl + 4096]
        E("sp", lambda e, d=wf, o=ot:
          e.dma_start(out=d, in_=wprojT[:, 4096 * o:4096 * o + 4096]),
          inc=L[6 + wsl], waits=[free.get(("cwf", wsl)),
                                 mi_ag_y if ot < 2 else None,
                                 mi_V if ot < 2 else None])
        mi_wf = (L[6 + wsl], L[6 + wsl].n)
        E("dve", lambda e, d=wb, s_=wf: e.tensor_copy(out=d, in_=s_),
          inc=t_dve, waits=[mi_wf, free.get(("cwb", wsl))])
        mi_wb = (t_dve, t_dve.n)
        free[("cwf", wsl)] = mi_wb
        bks = zbanks[ot % 2]
        for cc in range(NCC):
            for th in range(2):
                E("pe", lambda e, b=bks[th], w=wb, cc_=cc, t_=th,
                  st=(cc == 0), sp_=(cc == NCC - 1):
                  e.matmul(b, w[:, 128 * cc_:128 * cc_ + 128],
                           a1[:, 16384 * t_ + 512 * cc_:16384 * t_ + 512 * cc_ + 512],
                           start=st, stop=sp_),
                  inc=t_pe if (cc == NCC - 1 and th == 1) else None,
                  waits=[mi_wb, mi_y[0], mi_y[1],
                         free.get(("zb", ot % 2))] if (cc == 0 and th == 0) else [])
        mi_mm = (t_pe, t_pe.n)
        free[("cwb", wsl)] = mi_mm
        free["yh"] = mi_mm
        for th in range(2):
            par = 2 * (ot % 2) + th
            xsl = f3[:, 4096 + 512 * par:4096 + 512 * par + 512]
            x2t = f3[:, 6144 + 512 * par:6144 + 512 * par + 512]
            x2b = a2[:, 512 * par:512 * par + 512]
            E("sp", lambda e, d=xsl, o=ot, t_=th:
              e.dma_start(out=d, in_=xT[128 * o:128 * o + 128, 512 * t_:512 * t_ + 512]),
              inc=L[2 + par], waits=[free.get(("cx", par))])
            mi_x = (L[2 + par], L[2 + par].n)
            E("dve", lambda e, d=x2t, b=bks[th], x=xsl: e.tensor_add(d, b, x),
              inc=t_dve, waits=[mi_mm, mi_x, free.get(("cx2", par)), free.get(("cx2a", par))])
            mi_x2 = (t_dve, t_dve.n)
            free[("cx", par)] = mi_x2
            E("act", lambda e, s_=x2t, o=ot, t_=th:
              e.dma_start(out=x2fm_d[:, 1024 * o + 512 * t_:1024 * o + 512 * t_ + 512], in_=s_),
              inc=Strk[1 + par], waits=[mi_x2])
            free[("cx2a", par)] = (Strk[1 + par], Strk[1 + par].n)
            E("pool", lambda e, d=x2b, s_=x2t: e.tensor_copy(out=d, in_=s_),
              inc=t_pool, waits=[mi_x2, free.get(("cxb", par))])
            mi_xb = (t_pool, t_pool.n)
            free[("cx2", par)] = mi_xb
            sqb = a2[:, 2048 + 512 * th:2048 + 512 * th + 512]
            E("act", lambda e, d=sqb, s_=x2b: e.activation(out=d, in_=s_, func=AF.Square),
              inc=t_act, waits=[mi_xb, free.get(("csq", th))])
            mi_sq = (t_act, t_act.n)
            st, sp_ = (ot == 0), (ot == NCC - 1)
            sbank = SUMS if th == 0 else AUX
            E("pe", lambda e, b=sbank, r=x2b, st_=st, sp__=sp_:
              e.matmul(b[0:1, :], ones_bf[:, :], r, start=st_, stop=sp__),
              waits=[mi_xb])
            E("pe", lambda e, b=sbank, r=sqb, st_=st, sp__=sp_:
              e.matmul(b[32:33, :], ones_bf[:, :], r, start=st_, stop=sp__),
              inc=t_pe, waits=[mi_sq])
            free[("cxb", par)] = (t_pe, t_pe.n)
            free[("csq", th)] = (t_pe, t_pe.n)
        free[("zb", ot % 2)] = (t_dve, t_dve.n)
    mi_cstats = (t_pe, t_pe.n)
    mi_x2fm = [(Strk[i], Strk[i].n) for i in range(1, 5)]

    mi_rows2 = ln_rows(mi_cstats, 2048)

    bcm = f3[:, 0:1024]
    bcr = f3[:, 1024:2048]
    mi_bc = None
    for h in range(2):
        m_h = mrow_sb[0:1, 2048 + 512 * h:2048 + 512 * h + 512]
        r_h = mrow_sb[0:1, 3072 + 512 * h:3072 + 512 * h + 512]
        E("pe", lambda e, r=m_h: e.matmul(OA[:, :], onesr_f[:, :], r, start=True, stop=True),
          inc=t_pe, waits=[mi_rows2, free.get(("zb", 0)), free.get(("zb", 1))])
        mi1 = (t_pe, t_pe.n)
        E("pe", lambda e, r=r_h: e.matmul(OB[:, :], onesr_f[:, :], r, start=True, stop=True),
          inc=t_pe)
        mi2 = (t_pe, t_pe.n)
        E("dve", lambda e, d=bcm, h_=h: e.tensor_copy(out=d[:, 512 * h_:512 * h_ + 512],
                                                      in_=OA[:, :]),
          inc=t_dve, waits=[mi1])
        E("dve", lambda e, d=bcr, h_=h: e.tensor_copy(out=d[:, 512 * h_:512 * h_ + 512],
                                                      in_=OB[:, :]),
          inc=t_dve, waits=[mi2])
        mi_bc = (t_dve, t_dve.n)
        free[("zb", 0)] = mi_bc
        free[("zb", 1)] = mi_bc

    for cc in range(NCC):
        sl = cc % 2
        x2s = f3[:, 4096 + 1024 * sl:4096 + 1024 * sl + 1024]
        tmp = f3[:, 2048 + 1024 * sl:2048 + 1024 * sl + 1024]
        E("sp", lambda e, d=x2s, c_=cc:
          e.dma_start(out=d, in_=x2fm_d[:, 1024 * c_:1024 * c_ + 1024]),
          inc=L[2 + sl], waits=(mi_x2fm + [mi_cstats] if cc < 2 else []) + [free.get(("h2x", sl))])
        mi_x = (L[2 + sl], L[2 + sl].n)
        E("pool", lambda e, d=tmp, s=x2s, m=bcm: e.tensor_sub(d, s, m),
          inc=t_pool, waits=[mi_x, mi_bc, free.get(("h2t", sl))])
        mi_t = (t_pool, t_pool.n)
        free[("h2x", sl)] = mi_t
        h2v = a2[:, 1024 * cc:1024 * cc + 1024]
        E("pool", lambda e, d=h2v, s=tmp, r=bcr: e.tensor_mul(d, s, r),
          inc=t_pool, waits=[mi_V if cc == 0 else None])
        mi_h = (t_pool, t_pool.n)
        free[("h2t", sl)] = mi_h
        E("dve", lambda e, d=hsum_sb[:, cc:cc + 1], s=h2v: e.reduce_sum(d, s, axis=AX),
          inc=t_dve, waits=[mi_h])
    mi_h2 = (t_pool, t_pool.n)
    mi_hsum = (t_dve, t_dve.n)

    for tt in range(8):
        sl = tt % 2
        xsl = f1[:, 4096 * sl:4096 * sl + 4096]
        stg = f1[:, 8192 + 4096 * sl:8192 + 4096 * sl + 4096]
        E("sp", lambda e, d=xsl, t_=tt:
          e.dma_start(out=d.rearrange("p (cc t) -> p cc t", t=128),
                      in_=x2fm_d.rearrange("p (cc t) -> p cc t", t=1024)
                      [:, 0:32, 128 * t_:128 * t_ + 128]),
          inc=L[8 + sl], waits=mi_x2fm + [free.get(("xt", sl)),
                            free.get("yh") if tt < 2 else None])
        mi_x = (L[8 + sl], L[8 + sl].n)
        for g in range(8):
            for j in range(4):
                cc = 4 * g + j
                E("pe", lambda e, s=xsl, c_=cc, j_=j:
                  e.transpose(AUX[:, 128 * j_:128 * j_ + 128],
                              s[:, 128 * c_:128 * c_ + 128], ident_sb[:, :]),
                  inc=t_pe if j == 3 else None,
                  waits=[mi_x, mi_rows2, free.get("xaux")] if j == 0 else [])
            mi_tr = (t_pe, t_pe.n)
            E("act", lambda e, d=stg, g_=g: e.copy(out=d[:, 512 * g_:512 * g_ + 512],
                                                   in_=AUX[:, :]),
              inc=t_act, waits=[mi_tr, free.get(("xstg", sl)) if g == 0 else None])
            free["xaux"] = (t_act, t_act.n)
        mi_stg = (t_act, t_act.n)
        free[("xt", sl)] = mi_stg
        E("act", lambda e, s=stg, t_=tt:
          e.dma_start(out=x2tm_d[128 * t_:128 * t_ + 128, :], in_=s),
          inc=Strk[3 + sl], waits=[mi_stg])
        free[("xstg", sl)] = (Strk[3 + sl], Strk[3 + sl].n)
    mi_x2tm = [(Strk[3], Strk[3].n), (Strk[4], Strk[4].n)]

    E("act", lambda e: e.dma_start(out=hsin_d, in_=hsum_sb[:, :]),
      inc=Strk[5], waits=[mi_hsum])
    mi_hs = (Strk[5], Strk[5].n)
    E("pool", lambda e: e.collective_compute(
        "AllReduce", ALU.add, replica_groups=RG, ins=[hsin_d.opt()], outs=[hsout_d.opt()]),
      inc=t_cc, waits=[mi_hs])
    mi_ar = (t_cc, t_cc.n)
    mcol = f3[:, 6144:6176]
    E("sp", lambda e: e.dma_start(out=mcol, in_=hsout_d), inc=L[1], waits=[mi_ar])
    mi_m = (L[1], L[1].n)
    E("dve", lambda e: e.tensor_scalar_mul(mcol, mcol, 1.0 / T), inc=t_dve, waits=[mi_m])
    mi_m2 = (t_dve, t_dve.n)
    wqs = f1[:, 0:4096]
    E("sp", lambda e: e.dma_start(out=wqs, in_=wqT),
      inc=L[3], waits=[free.get(("xt", 0))])
    mi_wq = (L[3], L[3].n)
    for cc in range(NCC):
        E("pe", lambda e, c_=cc, st=(cc == 0), sp_=(cc == NCC - 1):
          e.matmul(AUX[0:1, 0:128], mcol[:, c_:c_ + 1], wqs[:, 128 * c_:128 * c_ + 128],
                   start=st, stop=sp_),
          inc=t_pe if cc == NCC - 1 else None,
          waits=[mi_m2, mi_wq, free.get("xaux"), (t_act, P.tracks["act"].n)] if cc == 0 else [])
    mi_qmm = (t_pe, t_pe.n)
    qrow = f3[:, 6176:6304]
    E("dve", lambda e: e.tensor_copy(out=qrow[0:1, :], in_=AUX[0:1, 0:128]),
      inc=t_dve, waits=[mi_qmm])
    mi_qr = (t_dve, t_dve.n)
    E("pe", lambda e: e.transpose(AUX[:, 256:257], qrow[0:1, :], ident_sb[0:1, 0:1]),
      inc=t_pe, waits=[mi_qr])
    mi_qt = (t_pe, t_pe.n)
    E("dve", lambda e: e.tensor_copy(out=qT_sb[:, :], in_=AUX[:, 256:257]),
      inc=t_dve, waits=[mi_qt])
    mi_qT = (t_dve, t_dve.n)
    E("pe", lambda e: e.matmul(AUX[0:1, 0:8], qT_sb[:, :], keys_sb[:, :],
                               start=True, stop=True),
      inc=t_pe, waits=[mi_qT, mi_keys])
    mi_sc = (t_pe, t_pe.n)
    e8 = f3[:, 6304:6312]
    rt = f3[:, 6312:6320]
    E("dve", lambda e: e.reduce_max(rt[0:1, 0:1], AUX[0:1, 0:8], axis=AX, negate=True),
      inc=t_dve, waits=[mi_sc])
    E("dve", lambda e: e.tensor_scalar_mul(rt[0:1, 0:1], rt[0:1, 0:1], SCALE_MAIN),
      inc=t_dve)
    mi_ng = (t_dve, t_dve.n)
    E("act", lambda e: e.activation(out=e8[0:1, :], in_=AUX[0:1, 0:8], func=AF.Exp,
                                    scale=SCALE_MAIN, bias=rt[0:1, 0:1]),
      inc=t_act, waits=[mi_ng])
    mi_e8 = (t_act, t_act.n)
    E("dve", lambda e: e.reduce_sum(rt[0:1, 1:2], e8[0:1, :], axis=AX),
      inc=t_dve, waits=[mi_e8])
    E("dve", lambda e: e.reciprocal(rt[0:1, 2:3], rt[0:1, 1:2]), inc=t_dve)
    E("dve", lambda e: e.tensor_scalar_mul(e8[0:1, :], e8[0:1, :], rt[0:1, 2:3]),
      inc=t_dve)
    mi_cf = (t_dve, t_dve.n)
    E("pe", lambda e: e.matmul(AUX[:, 0:8], onesr_f[:, :], e8[0:1, :], start=True, stop=True),
      inc=t_pe, waits=[mi_cf])
    mi_cemm = (t_pe, t_pe.n)
    E("dve", lambda e: e.tensor_copy(out=ce_sb[:, :], in_=AUX[:, 0:8]),
      inc=t_dve, waits=[mi_cemm])
    mi_ce = (t_dve, t_dve.n)

    if debug_stage >= 4:
        E("pool", lambda e: e.dma_start(out=dbg["x2"], in_=x2tm_d), inc=t_ld,
          waits=mi_x2tm)
        E("act", lambda e: e.dma_start(
            out=dbg["h2"].rearrange("(cc p) t -> p cc t", p=128),
            in_=a2[:, :].rearrange("p (cc t) -> p cc t", t=1024)), inc=t_ld2,
          waits=[mi_h2])
        E("act", lambda e: e.dma_start(out=dbg["coeffs"], in_=e8[0:1, :]),
          inc=t_ld2, waits=[mi_cf])
    mi_dbg4 = (t_ld2, t_ld2.n) if debug_stage >= 4 else None
    if debug_stage == 4:
        return _finish(nc, P, stack)

    gpack = a3[0:64, 0:8192]
    gbanks = [OA, OB, SUMS, AUX]
    mi_gev = {}
    for e_ in range(8):
        sl = e_ % 2
        dwf = f1[:, 2048 * sl:2048 * sl + 2048]
        dwb = a3[:, 8192 + 2048 * sl:8192 + 2048 * sl + 2048]
        E("sp", lambda e, d=dwf, i=e_:
          e.dma_start(out=d, in_=downT[:, 2048 * i:2048 * i + 2048]),
          inc=L[4 + sl], waits=[free.get(("df", sl)), mi_qmm if e_ < 2 else None])
        mi_w = (L[4 + sl], L[4 + sl].n)
        E("pool", lambda e, d=dwb, s=dwf: e.tensor_copy(out=d, in_=s),
          inc=t_pool, waits=[mi_w, free.get(("db", sl))])
        mi_wb = (t_pool, t_pool.n)
        free[("df", sl)] = mi_wb
        for th in range(2):
            gi = (e_ * 2 + th) % 4
            bank = gbanks[gi]
            E_waits = [mi_wb, mi_h2, mi_gev.get(gi),
                       mi_ce if e_ == 0 else None]
            for cc in range(NCC):
                E("pe", lambda e, b=bank, w=dwb, cc_=cc, t_=th,
                  st=(cc == 0), sp_=(cc == NCC - 1):
                  e.matmul(b[0:64, :], w[:, 64 * cc_:64 * cc_ + 64],
                           a2[:, 1024 * cc_ + 512 * t_:1024 * cc_ + 512 * t_ + 512],
                           start=st, stop=sp_),
                  inc=t_pe if cc == NCC - 1 else None,
                  waits=E_waits if cc == 0 else [])
            mi_mm = (t_pe, t_pe.n)
            E("dve", lambda e, b=bank, i=e_, t_=th:
              e.tensor_scalar_mul(
                  gpack[0:64, 1024 * i + 512 * t_:1024 * i + 512 * t_ + 512],
                  b[0:64, :], ce_sb[0:64, i:i + 1]),
              inc=t_dve, waits=[mi_mm, mi_ce])
            mi_gev[gi] = (t_dve, t_dve.n)
        free[("db", sl)] = (t_pe, t_pe.n)
    mi_g = (t_dve, t_dve.n)
    if debug_stage == 41:
        return _finish(nc, P, stack)

    outg = {0: [(SA, 0), (SA, 256), (SA, 512), (SA, 768),
                (SB, 0), (SB, 256), (SB, 512), (SB, 768)],
            1: [(OA, 0), (OA, 256), (OB, 0), (OB, 256),
                (SUMS, 0), (SUMS, 256), (AUX, 0), (AUX, 256)]}
    mi_outev = {}
    n_ot = 2 if debug_stage in (42, 43, 44) else 16
    skip_up = (debug_stage == 43)
    mi_outall = None
    mi_wprev = None
    mi_upprev = None
    wbf = a3[:, 8192:16384]
    upb = a1[0:64, 10240:12288]
    upb2 = a1[0:64, 12288:14336]
    for ot in range(n_ot):
        wff = f1[:, 8192:16384]
        E("sp", lambda e, d=wff, o=ot:
          e.dma_start(out=d, in_=worgT[:, 8192 * o:8192 * o + 8192]),
          inc=L[6], waits=[mi_wprev, mi_g if ot == 0 else None,
                           mi_dbg4 if ot == 0 else None]
          + (mi_x2tm if ot == 0 else []))
        mi_wf = (L[6], L[6].n)
        E("dve", lambda e, d=wbf, s=wff: e.tensor_copy(out=d, in_=s),
          inc=t_dve, waits=[mi_wf, mi_wprev])
        mi_wcast = (t_dve, t_dve.n)
        usl = ot % 2
        upx = upb if usl == 0 else upb2
        upff = f1[0:64, 2048 + 2048 * usl:2048 + 2048 * usl + 2048]
        E("sp", lambda e, d=upff, o=ot:
          e.dma_start(out=d, in_=upw[:, 2048 * o:2048 * o + 2048]),
          inc=L[8 + usl], waits=[free.get(("uf", usl)), mi_g if ot < 2 else None])
        mi_uf = (L[8 + usl], L[8 + usl].n)
        E("pool", lambda e, d=upx, s=upff: e.tensor_copy(out=d, in_=s),
          inc=t_pool, waits=[mi_uf, free.get(("ub", usl))])
        mi_up = (t_pool, t_pool.n)
        free[("uf", usl)] = mi_up
        grp = outg[ot % 2]
        mi_tt = {}
        for tt in range(8):
            bnk, off = grp[tt]
            for cc in range(NCC):
                E("pe", lambda e, b=bnk, o_=off, cc_=cc, t_=tt,
                  st=(cc == 0), sp_=(skip_up and cc == NCC - 1):
                  e.matmul(b[:, o_:o_ + 256],
                           a2[:, 1024 * cc_ + 128 * t_:1024 * cc_ + 128 * t_ + 128],
                           wbf[:, 256 * cc_:256 * cc_ + 256], start=st, stop=sp_),
                  inc=t_pe if (skip_up and cc == NCC - 1) else None,
                  waits=[mi_wcast, mi_g, mi_outev.get((ot % 2, tt))] if cc == 0 else [])
            if not skip_up:
                for e_ in range(8):
                    E("pe", lambda e, b=bnk, o_=off, i=e_, t_=tt, sp_=(e_ == 7), u=upx:
                      e.matmul(b[:, o_:o_ + 256],
                               gpack[0:64, 1024 * i + 128 * t_:1024 * i + 128 * t_ + 128],
                               u[0:64, 256 * i:256 * i + 256],
                               start=False, stop=sp_),
                      inc=t_pe if e_ == 7 else None,
                      waits=[mi_up] if e_ == 0 else [])
            mi_tt[tt] = (t_pe, t_pe.n)
        for tt in range(8):
            bnk, off = grp[tt]
            mm_gate = mi_tt[3] if tt < 4 else mi_tt[7]
            xsl = f1[:, 7168 + 256 * (tt % 2):7168 + 256 * (tt % 2) + 256]
            och = f1[:, 7680 + 256 * (tt % 2):7680 + 256 * (tt % 2) + 256]
            E("sp", lambda e, d=xsl, o=ot, t_=tt:
              e.dma_start(out=d, in_=x2tm_d[128 * t_:128 * t_ + 128, 256 * o:256 * o + 256]),
              inc=L[tt % 2], waits=(mi_x2tm if ot == 0 else []) + [free.get(("ox", tt % 2))])
            mi_x = (L[tt % 2], L[tt % 2].n)
            E("dve", lambda e, d=och, b=bnk, o_=off, x=xsl:
              e.tensor_add(d, b[:, o_:o_ + 256], x),
              inc=t_dve, waits=[mm_gate, mi_x, free.get(("oo", tt % 2))])
            mi_o = (t_dve, t_dve.n)
            free[("ox", tt % 2)] = mi_o
            mi_outev[(ot % 2, tt)] = mi_o
            E("act", lambda e, s=och, o=ot, t_=tt:
              e.dma_start(out=out[128 * t_:128 * t_ + 128, 256 * o:256 * o + 256], in_=s),
              inc=Strk[tt % 2], waits=[mi_o])
            free[("oo", tt % 2)] = (Strk[tt % 2], Strk[tt % 2].n)
        mi_wprev = (t_pe, t_pe.n)
        free[("ub", usl)] = (t_pe, t_pe.n)

    return _finish(nc, P, stack)


def _finish(nc, P, stack):
    semstack = contextlib.ExitStack()
    sems = {name: semstack.enter_context(nc.semaphore(f"sem_{name}"))
            for name in P.tracks}
    final = [(t, t.n) for t in P.tracks.values() if t.n > 0]
    P.emit("sp", lambda e: e.nop(), waits=final)
    with nc.Block() as block:
        P.realize(block, sems)
    semstack.close()
    stack.close()
    return nc



def _prep(x, w_attn, w_proj, adapter_wte, gating, W_org, lora_down, lora_up,
          lora_keys, Wq):
    x = np.asarray(x, dtype=np.float32).reshape(T, C)
    xT = np.ascontiguousarray(x.T)
    w_attn = np.asarray(w_attn, dtype=np.float32)
    theta = 1.0 / (10000.0 ** (np.arange(0, HD, 2, dtype=np.float32) / HD))
    idx = np.outer(np.arange(T, dtype=np.float32), theta).astype(np.float32)
    idx2 = np.concatenate([idx, idx], axis=-1)
    cosT = np.ascontiguousarray(np.cos(idx2).astype(np.float32).T)
    sinST = np.ascontiguousarray(np.sin(idx2).astype(np.float32).T).copy()
    sinST[0:64, :] *= -1.0
    mk = np.zeros((4, 128, S), dtype=np.float32)
    qq = np.arange(S)[None, :]
    kk = np.arange(128)[:, None]
    for i, o in enumerate((0, 128, 256, 384)):
        mk[i] = np.where(qq >= kk + o, 0.0, NEG).astype(np.float32)
    shared = {
        "wprojT": np.ascontiguousarray(
            np.asarray(w_proj, np.float32).T.reshape(32, 128, 32, 128)
            .transpose(1, 2, 0, 3).reshape(128, 32 * 4096)),
        "worgT": np.ascontiguousarray(
            np.asarray(W_org, np.float32).T.reshape(32, 128, 16, 256)
            .transpose(1, 2, 0, 3).reshape(128, 16 * 8192)),
        "downT": np.ascontiguousarray(
            np.asarray(lora_down, np.float32).reshape(8, 32, 128, 64)
            .transpose(2, 0, 1, 3).reshape(128, 8 * 2048)),
        "upw": np.ascontiguousarray(
            np.asarray(lora_up, np.float32).reshape(8, 64, 16, 256)
            .transpose(1, 2, 0, 3).reshape(64, 16 * 2048)),
        "wqT": np.ascontiguousarray(
            np.asarray(Wq, np.float32).T.reshape(32, 128, 128)
            .transpose(1, 0, 2).reshape(128, 4096)),
        "keysT": np.ascontiguousarray(np.asarray(lora_keys, np.float32).T),
        "adT": np.ascontiguousarray(np.asarray(adapter_wte, np.float32).T),
        "gat": np.asarray(gating, np.float32).reshape(1, 1),
        "cosT": cosT,
        "sinS": sinST,
        "masks": mk,
    }
    in_maps = []
    for r in range(NCORES):
        im = dict(shared)
        im["xT"] = np.ascontiguousarray(xT[:, TOWN * r:TOWN * (r + 1)])
        rows = np.concatenate([w_attn[HC * r:HC * (r + 1)],
                               w_attn[C + HC * r:C + HC * (r + 1)],
                               w_attn[2 * C + HC * r:2 * C + HC * (r + 1)]], axis=0)
        im["wqkvT"] = np.ascontiguousarray(
            rows.T.reshape(32, 128, 12, 128).transpose(1, 2, 0, 3).reshape(128, 12 * 4096))
        in_maps.append(im)
    return in_maps


def kernel(L0_attn_x, L0_attn_w_attn, L0_attn_w_proj, L0_attn_adapter_wte,
           L0_attn_gating, L0_ffn_W_org, L0_ffn_lora_down, L0_ffn_lora_up,
           L0_ffn_lora_keys, L0_ffn_Wq):
    in_maps = _prep(L0_attn_x, L0_attn_w_attn, L0_attn_w_proj, L0_attn_adapter_wte,
                    L0_attn_gating, L0_ffn_W_org, L0_ffn_lora_down, L0_ffn_lora_up,
                    L0_ffn_lora_keys, L0_ffn_Wq)
    nc = _build(DEBUG_STAGE)
    res = run_bass_kernel_spmd(nc, in_maps, core_ids=list(range(NCORES)))
    if DEBUG_STAGE:
        kernel.last_results = res
        if "out" not in res.results[0]:
            return None
    outs = [res.results[r]["out"] for r in range(NCORES)]
    return np.concatenate(outs, axis=0).reshape(1, T, C)


# version 08
# speedup vs baseline: 1.0348x; 1.0348x over previous
import contextlib
import math
import numpy as np
import concourse.bass as bass
import concourse.mybir as mybir
from concourse.bass_utils import run_bass_kernel_spmd

F32 = mybir.dt.float32
BF16 = mybir.dt.bfloat16
AF = mybir.ActivationFunctionType
ALU = mybir.AluOpType
AX = mybir.AxisListType.X

NCORES = 8
T = 8192
C = 4096
HD = 128
HPC = 4
HC = HPC * HD
TOWN = T // NCORES
S = 512
NTS = T // S
NCC = C // 128
SCALE_MAIN = 1.0 / math.sqrt(HD)
SCALE_AD = 1.0 / math.sqrt(C)
EPS = 1e-5
NEG = -1.0e30

DEBUG_STAGE = 0


class _Trk:
    __slots__ = ("name", "step", "n")

    def __init__(self, name, step):
        self.name, self.step, self.n = name, step, 0

    def bump(self):
        self.n += self.step
        return self.n


class _Fut:
    __slots__ = ("trk", "val")

    def __init__(self):
        self.trk, self.val = None, None


class _Prog:
    ENGS = ("pe", "act", "dve", "pool", "sp")

    def __init__(self):
        self.ops = {e: [] for e in self.ENGS}
        self.tracks = {}

    def trk(self, name, step=1):
        if name not in self.tracks:
            self.tracks[name] = _Trk(name, step)
        return self.tracks[name]

    SERIAL = ("dve", "act", "pool")

    def emit(self, eng, fn, inc=None, waits=()):
        ws = [w for w in waits
              if w is not None and (isinstance(w, _Fut) or w[1])]
        swait = None
        if eng in self.SERIAL:
            et = self.trk(eng)
            if inc is None:
                inc = et
            if et.n > 0:
                swait = (et, et.n)
        if swait is not None:
            ws.append(swait)
        self.ops[eng].append((fn, inc, inc.bump() if inc is not None else None, ws))
        return (inc, inc.n) if inc is not None else None

    def realize(self, block, sems):
        def runner(ename):
            def go(e):
                lw = {}
                for fn, trk, _val, waits in self.ops[ename]:
                    need = {}
                    for w in waits:
                        if isinstance(w, _Fut):
                            if w.trk is None or not w.val:
                                continue
                            t, v = w.trk, w.val
                        else:
                            t, v = w
                        if lw.get(t.name, 0) < v:
                            need[t.name] = max(need.get(t.name, 0), v)
                    for tn, v in need.items():
                        e.wait_ge(sems[tn], v)
                        lw[tn] = v
                    ins = fn(e)
                    if trk is not None:
                        ins.then_inc(sems[trk.name], trk.step)
            return go
        block.tensor(runner("pe"))
        block.scalar(runner("act"))
        block.vector(runner("dve"))
        block.gpsimd(runner("pool"))
        block.sync(runner("sp"))


def _build(debug_stage=0, shared_cc=True):
    nc = bass.Bass("TRN2", target_bir_lowering=False, num_devices=NCORES)
    RG = [list(range(NCORES))]

    xT = nc.dram_tensor("xT", [C, TOWN], F32, kind="ExternalInput").ap()
    wqkvT = nc.dram_tensor("wqkvT", [128, 12 * 4096], F32, kind="ExternalInput").ap()
    wprojT = nc.dram_tensor("wprojT", [128, 32 * 4096], F32, kind="ExternalInput").ap()
    worgT = nc.dram_tensor("worgT", [128, 16 * 8192], F32, kind="ExternalInput").ap()
    downT = nc.dram_tensor("downT", [128, 8 * 2048], F32, kind="ExternalInput").ap()
    upw = nc.dram_tensor("upw", [64, 16 * 2048], F32, kind="ExternalInput").ap()
    wqT = nc.dram_tensor("wqT", [128, 4096], F32, kind="ExternalInput").ap()
    keysT = nc.dram_tensor("keysT", [128, 8], F32, kind="ExternalInput").ap()
    adT = nc.dram_tensor("adT", [C, 10], F32, kind="ExternalInput").ap()
    gat = nc.dram_tensor("gat", [1, 1], F32, kind="ExternalInput").ap()
    cosT = nc.dram_tensor("cosT", [128, T], F32, kind="ExternalInput").ap()
    sinS = nc.dram_tensor("sinS", [128, T], F32, kind="ExternalInput").ap()
    masks = nc.dram_tensor("masks", [4, 128, S], F32, kind="ExternalInput").ap()
    out = nc.dram_tensor("out", [TOWN, C], F32, kind="ExternalOutput").ap()

    h1own_d = nc.dram_tensor("h1own_d", [128, 32768], BF16, kind="Internal").ap()
    h1all_d = nc.dram_tensor("h1all_d", [NCORES * 128, 32768], BF16, kind="Internal", addr_space="Shared" if shared_cc else "Local").ap()
    wqkv_bf_d = nc.dram_tensor("wqkv_bf_d", [128, 12 * 4096], BF16, kind="Internal").ap()
    q_d = nc.dram_tensor("q_d", [HC, T], BF16, kind="Internal").ap()
    k_d = nc.dram_tensor("k_d", [HC, T], BF16, kind="Internal").ap()
    v_d = nc.dram_tensor("v_d", [128, 32768], BF16, kind="Internal").ap()
    yown_d = nc.dram_tensor("yown_d", [HC, T], BF16, kind="Internal").ap()
    yall_d = nc.dram_tensor("yall_d", [NCORES * HC, T], BF16, kind="Internal", addr_space="Shared" if shared_cc else "Local").ap()
    x2fm_d = nc.dram_tensor("x2fm_d", [128, 32768], F32, kind="Internal").ap()
    x2tm_d = nc.dram_tensor("x2tm_d", [TOWN, C], F32, kind="Internal").ap()
    bc_d = nc.dram_tensor("bc_d", [4, 512], BF16, kind="Internal").ap()
    hsin_d = nc.dram_tensor("hsin_d", [128, NCC], F32, kind="Internal").ap()
    hsout_d = nc.dram_tensor("hsout_d", [128, NCC], F32, kind="Internal").ap()

    dbg = {}
    if debug_stage >= 1:
        dbg["h1all"] = nc.dram_tensor("dbg_h1all", [NCORES * C, TOWN], BF16, kind="ExternalOutput").ap()
    if debug_stage >= 2:
        dbg["q"] = nc.dram_tensor("dbg_q", [HC, T], BF16, kind="ExternalOutput").ap()
        dbg["k"] = nc.dram_tensor("dbg_k", [HC, T], BF16, kind="ExternalOutput").ap()
        dbg["v"] = nc.dram_tensor("dbg_v", [128, 32768], BF16, kind="ExternalOutput").ap()
        dbg["ak"] = nc.dram_tensor("dbg_ak", [128, 40], BF16, kind="ExternalOutput").ap()
        dbg["av"] = nc.dram_tensor("dbg_av", [16, 512], BF16, kind="ExternalOutput").ap()
        dbg["wbf"] = nc.dram_tensor("dbg_wbf", [C, 3 * HC], BF16, kind="ExternalOutput").ap()
    if debug_stage >= 3:
        dbg["yall"] = nc.dram_tensor("dbg_yall", [NCORES * HC, T], BF16, kind="ExternalOutput").ap()
        dbg["P0"] = nc.dram_tensor("dbg_P0", [128, 1024], BF16, kind="ExternalOutput").ap()
        dbg["rows"] = nc.dram_tensor("dbg_rows", [8, 512], F32, kind="ExternalOutput").ap()
        dbg["expA"] = nc.dram_tensor("dbg_expA", [16, 512], BF16, kind="ExternalOutput").ap()
        dbg["expA2"] = nc.dram_tensor("dbg_expA2", [16, 512], BF16, kind="ExternalOutput").ap()
        dbg["oev"] = nc.dram_tensor("dbg_oev", [128, 512], F32, kind="ExternalOutput").ap()
    if debug_stage >= 4:
        dbg["x2"] = nc.dram_tensor("dbg_x2", [TOWN, C], F32, kind="ExternalOutput").ap()
        dbg["h2"] = nc.dram_tensor("dbg_h2", [C, TOWN], BF16, kind="ExternalOutput").ap()
        dbg["coeffs"] = nc.dram_tensor("dbg_coeffs", [1, 8], F32, kind="ExternalOutput").ap()

    ident_d = nc.inline_tensor(np.eye(128, dtype=np.float32), name="ident_c").ap()

    P = _Prog()
    t_pe = P.trk("pe")
    t_act = P.trk("act")
    t_dve = P.trk("dve")
    t_pool = P.trk("pool")
    L = {i: P.trk(f"L{i}", 16) for i in range(10)}
    Strk = {i: P.trk(f"S{i}", 16) for i in range(6)}
    t_li = P.trk("Li", 16)
    t_fr = [P.trk("FR0", 16), P.trk("FR1", 16)]
    t_rr = [P.trk("RR0", 16), P.trk("RR1", 16)]
    t_fb = [P.trk("FB0", 16), P.trk("FB1", 16)]
    t_rb = [P.trk("RB0", 16), P.trk("RB1", 16)]
    t_ys = [P.trk("YS0", 16), P.trk("YS1", 16)]
    t_ld = P.trk("Ld", 16)
    t_ld2 = P.trk("Ld2", 16)
    t_cc = P.trk("cc", 1)

    stack = contextlib.ExitStack()
    ar1 = stack.enter_context(nc.sbuf_tensor("ar1", [128, 32768], BF16))
    ar2 = stack.enter_context(nc.sbuf_tensor("ar2", [128, 32768], BF16))
    ar3 = stack.enter_context(nc.sbuf_tensor("ar3", [128, 16384], BF16))
    ones_bf = stack.enter_context(nc.sbuf_tensor("ones_bf", [128, 1], BF16))
    onesr_bf = stack.enter_context(nc.sbuf_tensor("onesr_bf", [1, 128], BF16))
    onesr_f = stack.enter_context(nc.sbuf_tensor("onesr_f", [1, 128], F32))
    ident_sb = stack.enter_context(nc.sbuf_tensor("ident_sb", [128, 128], F32))
    akT_sb = stack.enter_context(nc.sbuf_tensor("akT_sb", [128, 40], BF16))
    av_sb = stack.enter_context(nc.sbuf_tensor("av_sb", [16, 512], BF16))
    adT_sb = stack.enter_context(nc.sbuf_tensor("adT_sb", [128, 320], BF16))
    g_sb = stack.enter_context(nc.sbuf_tensor("g_sb", [1, 1], F32))
    ce_sb = stack.enter_context(nc.sbuf_tensor("ce_sb", [128, 8], F32))
    keys_sb = stack.enter_context(nc.sbuf_tensor("keys_sb", [128, 8], F32))
    qT_sb = stack.enter_context(nc.sbuf_tensor("qT_sb", [128, 1], F32))
    hsum_sb = stack.enter_context(nc.sbuf_tensor("hsum_sb", [128, NCC], F32))
    mrow_sb = stack.enter_context(nc.sbuf_tensor("mrow_sb", [1, 4096], F32))
    srow_sb = stack.enter_context(nc.sbuf_tensor("srow_sb", [1, 1024], F32))
    mask_sb = stack.enter_context(nc.sbuf_tensor("mask_sb", [128, 4 * S], F32))

    a1 = ar1.ap()
    a2 = ar2.ap()
    a3 = ar3.ap()
    f1 = ar1.ap().bitcast(F32)
    f2 = ar2.ap().bitcast(F32)
    f3 = ar3.ap().bitcast(F32)

    SA = stack.enter_context(nc.psum_tensor("SA", [128, 1024], F32))
    SB = stack.enter_context(nc.psum_tensor("SB", [128, 1024], F32))
    OA = stack.enter_context(nc.psum_tensor("OA", [128, 512], F32))
    OB = stack.enter_context(nc.psum_tensor("OB", [128, 512], F32))
    SUMS = stack.enter_context(nc.psum_tensor("SUMS", [128, 512], F32))
    AUX = stack.enter_context(nc.psum_tensor("AUX", [128, 512], F32))

    E = P.emit

    E("pool", lambda e: e.memset(ones_bf[:, :], 1.0))
    E("pool", lambda e: e.memset(onesr_bf[:, :], 1.0))
    E("pool", lambda e: e.memset(onesr_f[:, :], 1.0))
    E("pool", lambda e: e.memset(av_sb[:, :], 0.0), inc=t_pool)
    mi_ones = (t_pool, t_pool.n)
    E("sp", lambda e: e.dma_start(out=ident_sb[:, :], in_=ident_d), inc=t_li)
    E("sp", lambda e: e.dma_start(
        out=mask_sb[:, :].rearrange("p (m s) -> p m s", m=4),
        in_=masks.rearrange("m p s -> p m s")), inc=t_li)
    E("sp", lambda e: e.dma_start(out=g_sb[:, :], in_=gat), inc=t_li)
    E("sp", lambda e: e.dma_start(out=keys_sb[:, :], in_=keysT), inc=t_li)
    E("sp", lambda e: e.dma_start(
        out=f3[:, 0:320].rearrange("p (cc a) -> p cc a", a=10),
        in_=adT.rearrange("(cc p) a -> p cc a", p=128)), inc=t_li)
    mi_init = (t_li, t_li.n)
    mi_ident = mi_masks = mi_gat = mi_keys = mi0 = mi_init
    E("pool", lambda e: e.tensor_copy(out=adT_sb[:, :], in_=f3[:, 0:320]),
      inc=t_pool, waits=[mi0])
    mi_adT = (t_pool, t_pool.n)

    free = {}
    for cc in range(NCC):
        sl = cc % 2
        xf = f3[:, 512 + 1024 * sl: 512 + 1024 * sl + 1024]
        xb = a1[:, 1024 * sl:1024 * sl + 1024]
        sq = a1[:, 2048 + 1024 * sl:2048 + 1024 * sl + 1024]
        E("sp", lambda e, d=xf, s=xT[128 * cc:128 * cc + 128, :]:
          e.dma_start(out=d, in_=s), inc=L[sl],
          waits=[free.get(("a0x", sl)), mi_adT if cc < 2 else None])
        mi_x = (L[sl], L[sl].n)
        E("pool", lambda e, d=xb, s=xf: e.tensor_copy(out=d, in_=s),
          inc=t_pool, waits=[mi_x, free.get(("a0b", sl))])
        mi_xb = (t_pool, t_pool.n)
        free[("a0x", sl)] = mi_xb
        E("act", lambda e, d=sq, s=xb: e.activation(out=d, in_=s, func=AF.Square),
          inc=t_act, waits=[mi_xb, free.get(("a0s", sl))])
        mi_sq = (t_act, t_act.n)
        st, sp_ = (cc == 0), (cc == NCC - 1)
        for h in range(2):
            bank = SUMS if h == 0 else AUX
            E("pe", lambda e, b=bank, r=xb, h_=h, st_=st, sp__=sp_:
              e.matmul(b[0:1, :], ones_bf[:, :], r[:, 512 * h_:512 * h_ + 512],
                       start=st_, stop=sp__),
              waits=[mi_xb, mi_ones] if (h == 0) else [])
            E("pe", lambda e, b=bank, r=sq, h_=h, st_=st, sp__=sp_:
              e.matmul(b[32:33, :], ones_bf[:, :], r[:, 512 * h_:512 * h_ + 512],
                       start=st_, stop=sp__),
              inc=t_pe if h == 1 else None, waits=[mi_sq] if h == 0 else [])
        free[("a0b", sl)] = (t_pe, t_pe.n)
        free[("a0s", sl)] = (t_pe, t_pe.n)
    mi_stats1 = (t_pe, t_pe.n)

    def ln_rows(mi_done, roff):
        out_mis = []
        for h in range(2):
            bank = SUMS if h == 0 else AUX
            m_h = mrow_sb[0:1, roff + 512 * h: roff + 512 * h + 512]
            r_h = mrow_sb[0:1, roff + 1024 + 512 * h: roff + 1024 + 512 * h + 512]
            tmp = srow_sb[0:1, 512 * h:512 * h + 512]
            E("act", lambda e, d=m_h, b=bank: e.mul(out=d, in_=b[0:1, :], mul=1.0 / C),
              inc=t_act, waits=[mi_done])
            mi_m = (t_act, t_act.n)
            E("act", lambda e, d=r_h, b=bank: e.mul(out=d, in_=b[32:33, :], mul=1.0 / C),
              inc=t_act)
            mi_e2 = (t_act, t_act.n)
            E("dve", lambda e, d=tmp, s=m_h: e.tensor_mul(d, s, s),
              inc=t_dve, waits=[mi_m])
            E("dve", lambda e, d=r_h, s=tmp: e.tensor_sub(d, d, s),
              inc=t_dve, waits=[mi_e2])
            E("dve", lambda e, d=r_h: e.tensor_scalar_add(d, d, EPS), inc=t_dve)
            mi_v = (t_dve, t_dve.n)
            E("act", lambda e, d=r_h: e.activation(out=d, in_=d, func=AF.Sqrt),
              inc=t_act, waits=[mi_v])
            mi_s = (t_act, t_act.n)
            E("dve", lambda e, d=r_h: e.reciprocal(d, d), inc=t_dve, waits=[mi_s])
            out_mis.append((t_dve, t_dve.n))
        return out_mis[-1]

    mi_rows1 = ln_rows(mi_stats1, 0)

    h1_mi = None
    prev_half_done = None
    for h in range(2):
        m_h = mrow_sb[0:1, 512 * h:512 * h + 512]
        r_h = mrow_sb[0:1, 1024 + 512 * h:1024 + 512 * h + 512]
        E("pe", lambda e, r=m_h: e.matmul(OA[:, :], onesr_f[:, :], r, start=True, stop=True),
          inc=t_pe, waits=[mi_rows1, prev_half_done])
        E("pe", lambda e, r=r_h: e.matmul(OB[:, :], onesr_f[:, :], r, start=True, stop=True),
          inc=t_pe)
        mi_bc = (t_pe, t_pe.n)
        for cc in range(NCC):
            sl = cc % 2
            xf = f3[:, 512 + 512 * sl:512 + 512 * sl + 512]
            tm = f3[:, 1536 + 512 * sl:1536 + 512 * sl + 512]
            hb = a1[:, 4096 + 512 * sl:4096 + 512 * sl + 512]
            E("sp", lambda e, d=xf, s=xT[128 * cc:128 * cc + 128, 512 * h:512 * h + 512]:
              e.dma_start(out=d, in_=s), inc=L[sl],
              waits=[free.get(("nx", sl)), mi_stats1 if cc < 2 else None])
            mi_x = (L[sl], L[sl].n)
            E("dve", lambda e, d=tm, s=xf: e.tensor_sub(d, s, OA[:, :]),
              inc=t_dve, waits=[mi_x, mi_bc, free.get(("ntm", sl))])
            mi_sub = (t_dve, t_dve.n)
            free[("nx", sl)] = mi_sub
            E("dve", lambda e, d=hb, s=tm: e.tensor_mul(d, s, OB[:, :]),
              inc=t_dve, waits=[free.get(("nh", sl))])
            mi_h1 = (t_dve, t_dve.n)
            free[("ntm", sl)] = mi_h1
            E("act", lambda e, s=hb, cc_=cc, h_=h:
              e.dma_start(out=h1own_d[:, 16384 * h_ + 512 * cc_:16384 * h_ + 512 * cc_ + 512], in_=s),
              inc=Strk[sl], waits=[mi_h1])
            free[("nh", sl)] = (Strk[sl], Strk[sl].n)
        prev_half_done = (t_dve, t_dve.n)
    h1_mi = [(Strk[0], Strk[0].n), (Strk[1], Strk[1].n)]

    E("pool", lambda e: e.collective_compute(
        "AllGather", ALU.bypass, replica_groups=RG, ins=[h1own_d.opt()], outs=[h1all_d.opt()]),
      inc=t_cc, waits=h1_mi)
    mi_ag_h1 = (t_cc, t_cc.n)

    mi_wspill = None
    for i in range(12):
        sl = i % 2
        wf = f1[:, 4096 + 4096 * sl:4096 + 4096 * sl + 4096]
        wb = a1[:, 24576 + 4096 * sl:24576 + 4096 * sl + 4096]
        E("sp", lambda e, d=wf, s=wqkvT[:, 4096 * i:4096 * i + 4096]:
          e.dma_start(out=d, in_=s),
          inc=L[2 + sl], waits=[free.get(("wsf", sl))])
        mi_w = (L[2 + sl], L[2 + sl].n)
        E("dve", lambda e, d=wb, s=wf: e.tensor_copy(out=d, in_=s),
          inc=t_dve, waits=[mi_w, free.get(("wsb", sl))])
        mi_wb = (t_dve, t_dve.n)
        free[("wsf", sl)] = mi_wb
        E("act", lambda e, s=wb, i_=i:
          e.dma_start(out=wqkv_bf_d[:, 4096 * i_:4096 * i_ + 4096], in_=s),
          inc=Strk[2 + sl], waits=[mi_wb])
        free[("wsb", sl)] = (Strk[2 + sl], Strk[2 + sl].n)
    mi_wspill = [(Strk[2], Strk[2].n), (Strk[3], Strk[3].n)]

    if debug_stage >= 1:
        E("pool", lambda e: e.dma_start(out=dbg["h1all"], in_=h1all_d),
          inc=t_ld, waits=[mi_ag_h1])
    if debug_stage == 1:
        return _finish(nc, P, stack)

    qkv_banks = [OA[:, :], OB[:, :], SA[:, 0:512], SA[:, 512:1024]]
    h1_slots = [a1[:, 0:16384], a1[:, 16384:32768], a2[:, 13312:29696]]
    mi_adchain = mi_adT
    mi_h1slot = {}
    for tp in range(8):
        mi_h1s = {}
        for hf in range(2):
            ts = 2 * tp + hf
            slot = (2 * tp + hf) % 3
            h1s = h1_slots[slot]
            E("sp", lambda e, d=h1s, b=ts // 2, f=ts % 2:
              e.dma_start(out=d, in_=h1all_d[128 * b:128 * b + 128,
                                             16384 * f:16384 * f + 16384]),
              inc=L[slot], waits=[mi_ag_h1, mi_h1slot.get(slot)]
              + (mi_wspill if ts < 3 else []))
            mi_h1s[hf] = ((L[slot], L[slot].n), h1s, slot)
        csl = {}
        cs_trk = {(0, 0): L[4], (0, 1): L[5], (1, 0): L[8], (1, 1): L[9]}
        for hf in range(2):
            ts = 2 * tp + hf
            base = 3072 + 2048 * (tp % 2) + 1024 * hf
            ctk = cs_trk[(tp % 2, hf)]
            E("sp", lambda e, d=f3[:, base:base + 512], s_=cosT[:, 512 * ts:512 * ts + 512]:
              e.dma_start(out=d, in_=s_), inc=ctk, waits=[free.get(("cs", tp % 2, hf))])
            E("sp", lambda e, d=f3[:, base + 512:base + 1024], s_=sinS[:, 512 * ts:512 * ts + 512]:
              e.dma_start(out=d, in_=s_), inc=ctk)
            csl[hf] = ((ctk, ctk.n), f3[:, base:base + 512], f3[:, base + 512:base + 1024])
        for ot in range(12):
            wsl = ot % 2
            wb = a2[:, 4096 * wsl:4096 * wsl + 4096]
            E("sp", lambda e, d=wb, o=ot:
              e.dma_start(out=d, in_=wqkv_bf_d[:, 4096 * o:4096 * o + 4096]),
              inc=L[6 + wsl], waits=(mi_wspill if tp == 0 else []) + [free.get(("w", wsl))])
            mi_w = (L[6 + wsl], L[6 + wsl].n)
            bk0, bk1 = (0, 1) if ot % 2 == 0 else (2, 3)
            for cc in range(NCC):
                for hf in range(2):
                    bank = qkv_banks[bk0 + hf]
                    E("pe", lambda e, b=bank, w=wb, h_=mi_h1s[hf][1], cc_=cc,
                      st=(cc == 0), sp_=(cc == NCC - 1):
                      e.matmul(b, w[:, 128 * cc_:128 * cc_ + 128],
                               h_[:, 512 * cc_:512 * cc_ + 512], start=st, stop=sp_),
                      inc=t_pe if (cc == NCC - 1 and hf == 1) else None,
                      waits=[mi_w, mi_h1s[0][0], mi_h1s[1][0],
                             free.get(("qb", bk0)), free.get(("qb", bk0 + 1))]
                      if (cc == 0 and hf == 0) else [])
            mi_mm = (t_pe, t_pe.n)
            free[("w", wsl)] = mi_mm
            mi_h1slot[mi_h1s[0][2]] = mi_mm
            mi_h1slot[mi_h1s[1][2]] = mi_mm
            if tp == 0 and ot >= 4:
                for cc in range(NCC):
                    E("pe", lambda e, w=wb, cc_=cc, st=(cc == 0), sp_=(cc == NCC - 1):
                      e.matmul(SUMS[:, 0:10], w[:, 128 * cc_:128 * cc_ + 128],
                               adT_sb[:, 10 * cc_:10 * cc_ + 10], start=st, stop=sp_),
                      inc=t_pe if cc == NCC - 1 else None,
                      waits=[mi_adchain] if cc == 0 else [])
                mi_amm = (t_pe, t_pe.n)
                free[("w", wsl)] = mi_amm
                if ot < 8:
                    E("dve", lambda e, o=ot: e.tensor_copy(
                        out=akT_sb[:, 10 * (o - 4):10 * (o - 4) + 10], in_=SUMS[:, 0:10]),
                      inc=t_dve, waits=[mi_amm])
                    mi_adchain = (t_dve, t_dve.n)
                else:
                    avt = f3[:, 7168:7184]
                    E("dve", lambda e, d=avt: e.tensor_copy(out=d[:, 0:10], in_=SUMS[:, 0:10]),
                      inc=t_dve, waits=[mi_amm])
                    mi_ae = (t_dve, t_dve.n)
                    E("pe", lambda e, s_=avt: e.transpose(AUX[0:10, 0:128], s_[:, 0:10], ident_sb[:, :]),
                      inc=t_pe, waits=[mi_ae, mi_ident, free.get("aux_a1")])
                    mi_tr = (t_pe, t_pe.n)
                    E("act", lambda e, o=ot: e.copy(
                        out=av_sb[0:10, 128 * (o - 8):128 * (o - 8) + 128], in_=AUX[0:10, 0:128]),
                      inc=t_act, waits=[mi_tr])
                    mi_adchain = (t_act, t_act.n)
                    free["aux_a1"] = mi_adchain
            for hf in range(2):
                ts = 2 * tp + hf
                bank = qkv_banks[bk0 + hf]
                par = bk0 + hf
                if ot < 8:
                    ev = f3[:, 512 * hf:512 * hf + 512]
                    tt1 = f3[:, 1024 + 1024 * hf:1024 + 1024 * hf + 512]
                    ro = f3[:, 1024 + 1024 * hf + 512:1024 + 1024 * hf + 1024]
                    ob = a2[:, 8192 + 512 * hf:8192 + 512 * hf + 512]
                    E("dve", lambda e, d=ev, b=bank: e.tensor_copy(out=d, in_=b),
                      inc=t_dve, waits=[mi_mm, free.get(("ev", hf))])
                    mi_ev = (t_dve, t_dve.n)
                    free[("qb", par)] = mi_ev
                    cs_mi, cs_c, cs_s = csl[hf]
                    E("pool", lambda e, d=tt1, s_=ev, c_=cs_c: e.tensor_mul(d, s_, c_),
                      inc=t_pool, waits=[mi_ev, cs_mi, free.get(("t1", hf))])
                    E("pool", lambda e, d=ro, s_=ev: e.tensor_copy(out=d[0:64, :], in_=s_[64:128, :]))
                    E("pool", lambda e, d=ro, s_=ev: e.tensor_copy(out=d[64:128, :], in_=s_[0:64, :]),
                      inc=t_pool)
                    mi_rot = (t_pool, t_pool.n)
                    free[("ev", hf)] = mi_rot
                    E("pool", lambda e, d=ro, s_=cs_s: e.tensor_mul(d, d, s_), inc=t_pool)
                    E("pool", lambda e, d=ob, a_=tt1, b_=ro: e.tensor_add(d, a_, b_),
                      inc=t_pool, waits=[free.get(("ob", hf))])
                    mi_ob = (t_pool, t_pool.n)
                    free[("t1", hf)] = mi_ob
                    free[("cs", tp % 2, hf)] = mi_ob
                    dst = q_d if ot < 4 else k_d
                    E("act", lambda e, s_=ob, d=dst, r=128 * (ot % 4), t0=512 * ts:
                      e.dma_start(out=d[r:r + 128, t0:t0 + 512], in_=s_),
                      inc=Strk[hf], waits=[mi_ob])
                    free[("ob", hf)] = (Strk[hf], Strk[hf].n)
                else:
                    ev = f3[:, 512 * hf:512 * hf + 512]
                    vsb = a2[:, 9216 + 2048 * hf:9216 + 2048 * hf + 2048]
                    E("dve", lambda e, d=ev, b=bank: e.tensor_copy(out=d, in_=b),
                      inc=t_dve, waits=[mi_mm, free.get(("ev", hf))])
                    mi_ev = (t_dve, t_dve.n)
                    free[("qb", par)] = mi_ev
                    for j in range(4):
                        E("pe", lambda e, s_=ev, j_=j: e.transpose(
                            AUX[:, 128 * j_:128 * j_ + 128], s_[:, 128 * j_:128 * j_ + 128],
                            ident_sb[:, :]),
                          inc=t_pe if j == 3 else None,
                          waits=[mi_ev, mi_ident, free.get("aux_a1")] if j == 0 else [])
                    mi_tr = (t_pe, t_pe.n)
                    free[("ev", hf)] = mi_tr
                    for j in range(4):
                        E("act", lambda e, d=vsb, j_=j, c0=128 * (ot - 8):
                          e.copy(out=d[:, 512 * j_ + c0:512 * j_ + c0 + 128],
                                 in_=AUX[:, 128 * j_:128 * j_ + 128]),
                          inc=t_act if j == 3 else None,
                          waits=[mi_tr, free.get(("vsb", hf)) if j == 0 else None]
                          if j == 0 else [])
                    mi_vs = (t_act, t_act.n)
                    free["aux_a1"] = mi_vs
                    if ot == 11:
                        E("act", lambda e, s_=vsb, t_=ts:
                          e.dma_start(out=v_d[:, 2048 * t_:2048 * t_ + 2048], in_=s_),
                          inc=Strk[2 + hf], waits=[mi_vs])
                        free[("vsb", hf)] = (Strk[2 + hf], Strk[2 + hf].n)
    mi_a1_st = [(Strk[i], Strk[i].n) for i in range(4)]
    mi_a1 = mi_a1_st + [(t_pe, t_pe.n), (t_pool, t_pool.n), (t_dve, t_dve.n)]

    if debug_stage >= 2:
        for nm, src in (("q", q_d), ("k", k_d), ("v", v_d)):
            E("pool", lambda e, n=nm, s_=src: e.dma_start(out=dbg[n], in_=s_), inc=t_ld, waits=mi_a1)
        E("pool", lambda e: e.dma_start(out=dbg["ak"], in_=akT_sb[:, :]), inc=t_ld, waits=mi_a1)
        E("pool", lambda e: e.dma_start(out=dbg["av"], in_=av_sb[:, :]), inc=t_ld, waits=mi_a1)
        E("pool", lambda e: e.dma_start(out=dbg["wbf"], in_=wqkv_bf_d), inc=t_ld, waits=mi_a1)
    if debug_stage == 2:
        return _finish(nc, P, stack)

    E("sp", lambda e: e.dma_start(out=a1[:, :].rearrange("p (h t) -> p h t", h=4),
                                  in_=k_d.rearrange("(h p) t -> p h t", p=128)),
      inc=L[2], waits=mi_a1)
    mi_K = (L[2], L[2].n)
    E("sp", lambda e: e.dma_start(out=a2[:, :], in_=v_d),
      inc=L[3], waits=mi_a1)
    mi_V = (L[3], L[3].n)

    mi_sgrp = {}
    mi_pslot = {}
    mi_oband = {}
    mi_mfin = {}
    mi_expA = {}
    mi_expA2 = {}
    mi_fbc = {}
    mi_rbc = {}
    mi_ybf = {}
    mi_oev = {}
    deferred = []

    def flush_defer():
        for eng, fn, inc, waits, fut in deferred:
            mi = E(eng, fn, inc=inc, waits=waits)
            if fut is not None:
                fut.trk, fut.val = mi
        deferred.clear()

    def defer(eng, fn, inc, waits, fut=None):
        deferred.append((eng, fn, inc, waits, fut))

    for c in range(NTS):
        sl = c % 2
        qs = a3[:, 2048 * sl:2048 * sl + 2048]
        E("sp", lambda e, d=qs, c_=c:
          e.dma_start(out=d.rearrange("p (h t) -> p h t", h=4),
                      in_=q_d[:, 512 * c_:512 * c_ + 512].rearrange("(h p) t -> p h t", p=128)),
          inc=L[4 + sl], waits=mi_a1_st[:2] + [free.get(("qs", sl))])
        mi_q = (L[4 + sl], L[4 + sl].n)
        nkt = 4 * (c + 1)
        npair = nkt // 2
        for h in range(HPC):
            ph = h % 2
            qv = qs[:, 512 * h:512 * h + 512]
            oband = OA if ph == 0 else OB
            sbank = SUMS if ph == 0 else AUX

            def emit_S(p_, first=False):
                sgrp = SA if p_ % 2 == 0 else SB
                for jj in range(2):
                    j = 2 * p_ + jj
                    E("pe", lambda e, g=sgrp, h_=h, j_=j, o=512 * jj, q_=qv:
                      e.matmul(g[:, o:o + 512],
                               a1[:, 8192 * h_ + 128 * j_:8192 * h_ + 128 * j_ + 128],
                               q_, start=True, stop=True),
                      inc=t_pe if jj == 1 else None,
                      waits=[mi_K, mi_q, mi_sgrp.get(p_ % 2)] if jj == 0 else [])
                return (t_pe, t_pe.n)

            mi_s = {}
            mi_s[0] = emit_S(0)
            if npair > 1:
                mi_s[1] = emit_S(1)
            dp = min(2, npair - 1)
            for p_ in range(npair):
                sgrp = SA if p_ % 2 == 0 else SB
                mi_mask = None
                for jj in range(2):
                    o = 2 * p_ + jj - 4 * c
                    if 0 <= o < 4:
                        E("dve", lambda e, g=sgrp, off=512 * jj, o_=o:
                          e.tensor_add(g[:, off:off + 512], g[:, off:off + 512],
                                       mask_sb[:, 512 * o_:512 * o_ + 512]),
                          inc=t_dve, waits=[mi_s[p_], mi_masks])
                        mi_mask = (t_dve, t_dve.n)
                psl = p_ % 3
                pb = a3[:, 4096 + 1024 * psl:4096 + 1024 * psl + 1024]
                E("act", lambda e, d=pb, g=sgrp:
                  e.activation(out=d, in_=g[:, :], func=AF.Exp, scale=SCALE_MAIN),
                  inc=t_act, waits=[mi_s[p_], mi_mask, mi_pslot.get(psl)])
                mi_e = (t_act, t_act.n)
                mi_sgrp[p_ % 2] = mi_e
                if p_ + 2 < npair:
                    mi_s[p_ + 2] = emit_S(p_ + 2)
                if p_ == dp:
                    flush_defer()
                for jj in range(2):
                    j = 2 * p_ + jj
                    E("pe", lambda e, ob=oband, j_=j, h_=h, p__=pb:
                      e.matmul(ob[:, :], a2[:, 512 * j_ + 128 * h_:512 * j_ + 128 * h_ + 128],
                               p__[:, 512 * (j_ % 2):512 * (j_ % 2) + 512],
                               start=(j_ == 0), stop=False),
                      waits=([mi_e, mi_V] + ([mi_oband.get(ph)] if j == 0 else []))
                      if jj == 0 else [])
                    E("pe", lambda e, j_=j, p__=pb, b=sbank, nk=nkt:
                      e.matmul(b[0:1, :], ones_bf[:, :],
                               p__[:, 512 * (j_ % 2):512 * (j_ % 2) + 512],
                               start=(j_ == 0), stop=(j_ == nk - 1)),
                      inc=t_pe if jj == 1 else None,
                      waits=[mi_mfin.get(ph)] if j == 0 else [])
                mi_pslot[psl] = (t_pe, t_pe.n)
            if deferred:
                flush_defer()
            mi_kloop = (t_pe, t_pe.n)
            E("pe", lambda e, h_=h, q_=qv, b=sbank:
              e.matmul(b[32:42, :], akT_sb[:, 10 * h_:10 * h_ + 10], q_, start=True, stop=True),
              inc=t_pe, waits=[mi_expA.get(ph)])
            mi_as = (t_pe, t_pe.n)
            expA = a3[:, 7168 + 512 * ph:7168 + 512 * ph + 512]
            E("act", lambda e, d=expA, b=sbank:
              e.activation(out=d[0:10, :], in_=b[32:42, :], func=AF.Exp, scale=SCALE_AD),
              inc=t_act, waits=[mi_as, mi_expA2.get(ph)])
            mi_ea = (t_act, t_act.n)
            mi_expA[ph] = mi_ea
            E("pe", lambda e, s_=expA, b=sbank:
              e.matmul(b[64:65, :], ones_bf[0:10, :], s_[0:10, :], start=True, stop=True),
              inc=t_pe, waits=[mi_ea])
            mi_asum = (t_pe, t_pe.n)
            lnr = srow_sb[0:1, 512 * ph:512 * ph + 512]
            E("act", lambda e, d=lnr, b=sbank:
              e.activation(out=d, in_=b[64:65, :], func=AF.Ln),
              inc=t_act, waits=[mi_asum])
            E("act", lambda e, d=lnr:
              e.activation(out=d, in_=d, func=AF.Exp, scale=-1.0),
              inc=t_act)
            mi_ras = (t_act, t_act.n)
            frow = a3[0:1, 14336 + 512 * ph:14336 + 512 * ph + 512]
            E("dve", lambda e, d=frow, r=lnr, b=sbank:
              e.scalar_tensor_tensor(d, b[0:1, :], g_sb[0:1, 0:1], r,
                                     op0=ALU.mult, op1=ALU.mult),
              inc=t_dve, waits=[mi_ras, mi_gat, mi_fbc.get(ph)])
            mi_f = (t_dve, t_dve.n)
            E("sp", lambda e, s_=frow, p=ph: e.dma_start(out=bc_d[p:p + 1, :], in_=s_),
              inc=t_fr[ph], waits=[mi_f, mi_fbc.get(ph)])
            mi_fd = (t_fr[ph], t_fr[ph].n)
            fbc = a3[0:10, 9216 + 512 * ph:9216 + 512 * ph + 512]
            E("sp", lambda e, d=fbc, p=ph:
              e.dma_start(out=d, in_=bc_d[p:p + 1, :].broadcast_to([10, 512])),
              inc=t_fb[ph], waits=[mi_fd, mi_expA2.get(ph)])
            mi_fb = (t_fb[ph], t_fb[ph].n)
            mi_fbc[ph] = mi_fb
            expA2 = a3[0:10, 8192 + 512 * ph:8192 + 512 * ph + 512]
            E("pool", lambda e, d=expA2, a=expA, f_=fbc:
              e.tensor_mul(d, a[0:10, :], f_),
              inc=t_pool, waits=[mi_fb, mi_ea])
            mi_e2 = (t_pool, t_pool.n)
            mi_expA2[ph] = mi_e2
            rrow = a3[0:1, 15360 + 512 * ph:15360 + 512 * ph + 512]
            lnm = srow_sb[0:1, 512 * ph:512 * ph + 512]
            E("act", lambda e, d=lnm, b=sbank:
              e.activation(out=d, in_=b[0:1, :], func=AF.Ln),
              inc=t_act, waits=[mi_f])
            E("act", lambda e, d=rrow, s_=lnm:
              e.activation(out=d, in_=s_, func=AF.Exp, scale=-1.0),
              inc=t_act)
            mi_r = (t_act, t_act.n)
            mi_mfin[ph] = mi_r
            E("sp", lambda e, s_=rrow, p=ph: e.dma_start(out=bc_d[2 + p:3 + p, :], in_=s_),
              inc=t_rr[ph], waits=[mi_r, mi_rbc.get(ph)])
            mi_rd = (t_rr[ph], t_rr[ph].n)
            rbc = a3[:, 10240 + 512 * ph:10240 + 512 * ph + 512]
            E("sp", lambda e, d=rbc, p=ph:
              e.dma_start(out=d, in_=bc_d[2 + p:3 + p, :].broadcast_to([128, 512])),
              inc=t_rb[ph], waits=[mi_rd, mi_rbc.get(ph)])
            mi_rb = (t_rb[ph], t_rb[ph].n)
            fut_ao = _Fut()
            defer("pe",
                  (lambda h_, s_, ob: lambda e:
                   e.matmul(ob[:, :], av_sb[0:10, 128 * h_:128 * h_ + 128], s_[0:10, :],
                            start=False, stop=True))(h, expA2, oband),
                  t_pe, [mi_e2], fut_ao)
            oev = f3[:, 6144 + 512 * ph:6144 + 512 * ph + 512]
            fut_oe = _Fut()
            defer("dve",
                  (lambda d, ob: lambda e: e.tensor_copy(out=d, in_=ob[:, :]))(oev, oband),
                  t_dve, [fut_ao, mi_oev.get(ph)], fut_oe)
            ybf = a3[:, 11264 + 512 * ph:11264 + 512 * ph + 512]
            fut_ytt = _Fut()
            defer("pool",
                  (lambda d, s_, r_: lambda e: e.tensor_mul(d, s_, r_))(ybf, oev, rbc),
                  t_pool, [mi_rb, fut_oe, mi_ybf.get(ph)], fut_ytt)
            fut_yst = _Fut()
            defer("sp",
                  (lambda s_, h_, c_: lambda e:
                   e.dma_start(out=yown_d[128 * h_:128 * h_ + 128, 512 * c_:512 * c_ + 512],
                               in_=s_))(ybf, h, c),
                  t_ys[ph], [fut_ytt], fut_yst)
            mi_oband[ph] = fut_ytt
            mi_ybf[ph] = fut_yst
            mi_oev[ph] = fut_ytt
            mi_rbc[ph] = fut_ytt
        free[("qs", sl)] = (t_pe, t_pe.n)
    flush_defer()
    mi_b = [(t_ys[0], t_ys[0].n), (t_ys[1], t_ys[1].n), (t_pe, t_pe.n), (t_dve, t_dve.n)]

    E("pool", lambda e: e.collective_compute(
        "AllGather", ALU.bypass, replica_groups=RG, ins=[yown_d.opt()], outs=[yall_d.opt()]),
      inc=t_cc, waits=mi_b)
    mi_ag_y = (t_cc, t_cc.n)

    if debug_stage >= 3:
        E("pool", lambda e: e.dma_start(out=dbg["yall"], in_=yall_d),
          inc=t_ld, waits=[mi_ag_y])
    if debug_stage == 3:
        return _finish(nc, P, stack)

    zbanks = [[OA[:, :], OB[:, :]], [SA[:, 0:512], SA[:, 512:1024]]]
    for th in range(2):
        def _ldy(e, th_=th):
            pid = e.partition_id()
            return e.dma_start(out=a1[:, 16384 * th_:16384 * th_ + 16384]
                               .rearrange("p (cc t) -> p cc t", t=512),
                               in_=yall_d.rearrange("(cc p) t -> p cc t", p=128)
                               [:, :, bass.ts(pid * 2 + th_, 512)])
        E("sp", _ldy, inc=L[th], waits=[mi_ag_y])
        free[("yld", th)] = (L[th], L[th].n)
    mi_y = [free[("yld", 0)], free[("yld", 1)]]
    for ot in range(NCC):
        wsl = ot % 2
        wf = f2[:, 8192 + 4096 * wsl:8192 + 4096 * wsl + 4096]
        wb = a3[:, 4096 * wsl:4096 * wsl + 4096]
        E("sp", lambda e, d=wf, o=ot:
          e.dma_start(out=d, in_=wprojT[:, 4096 * o:4096 * o + 4096]),
          inc=L[6 + wsl], waits=[free.get(("cwf", wsl)),
                                 mi_ag_y if ot < 2 else None,
                                 mi_V if ot < 2 else None])
        mi_wf = (L[6 + wsl], L[6 + wsl].n)
        E("dve", lambda e, d=wb, s_=wf: e.tensor_copy(out=d, in_=s_),
          inc=t_dve, waits=[mi_wf, free.get(("cwb", wsl))])
        mi_wb = (t_dve, t_dve.n)
        free[("cwf", wsl)] = mi_wb
        bks = zbanks[ot % 2]
        for cc in range(NCC):
            for th in range(2):
                E("pe", lambda e, b=bks[th], w=wb, cc_=cc, t_=th,
                  st=(cc == 0), sp_=(cc == NCC - 1):
                  e.matmul(b, w[:, 128 * cc_:128 * cc_ + 128],
                           a1[:, 16384 * t_ + 512 * cc_:16384 * t_ + 512 * cc_ + 512],
                           start=st, stop=sp_),
                  inc=t_pe if (cc == NCC - 1 and th == 1) else None,
                  waits=[mi_wb, mi_y[0], mi_y[1],
                         free.get(("zb", ot % 2))] if (cc == 0 and th == 0) else [])
        mi_mm = (t_pe, t_pe.n)
        free[("cwb", wsl)] = mi_mm
        free["yh"] = mi_mm
        for th in range(2):
            par = 2 * (ot % 2) + th
            xsl = f3[:, 4096 + 512 * par:4096 + 512 * par + 512]
            x2t = f3[:, 6144 + 512 * par:6144 + 512 * par + 512]
            x2b = a2[:, 512 * par:512 * par + 512]
            E("sp", lambda e, d=xsl, o=ot, t_=th:
              e.dma_start(out=d, in_=xT[128 * o:128 * o + 128, 512 * t_:512 * t_ + 512]),
              inc=L[2 + par], waits=[free.get(("cx", par))])
            mi_x = (L[2 + par], L[2 + par].n)
            E("dve", lambda e, d=x2t, b=bks[th], x=xsl: e.tensor_add(d, b, x),
              inc=t_dve, waits=[mi_mm, mi_x, free.get(("cx2", par)), free.get(("cx2a", par))])
            mi_x2 = (t_dve, t_dve.n)
            free[("cx", par)] = mi_x2
            E("act", lambda e, s_=x2t, o=ot, t_=th:
              e.dma_start(out=x2fm_d[:, 1024 * o + 512 * t_:1024 * o + 512 * t_ + 512], in_=s_),
              inc=Strk[1 + par], waits=[mi_x2])
            free[("cx2a", par)] = (Strk[1 + par], Strk[1 + par].n)
            E("pool", lambda e, d=x2b, s_=x2t: e.tensor_copy(out=d, in_=s_),
              inc=t_pool, waits=[mi_x2, free.get(("cxb", par))])
            mi_xb = (t_pool, t_pool.n)
            free[("cx2", par)] = mi_xb
            sqb = a2[:, 2048 + 512 * th:2048 + 512 * th + 512]
            E("act", lambda e, d=sqb, s_=x2b: e.activation(out=d, in_=s_, func=AF.Square),
              inc=t_act, waits=[mi_xb, free.get(("csq", th))])
            mi_sq = (t_act, t_act.n)
            st, sp_ = (ot == 0), (ot == NCC - 1)
            sbank = SUMS if th == 0 else AUX
            E("pe", lambda e, b=sbank, r=x2b, st_=st, sp__=sp_:
              e.matmul(b[0:1, :], ones_bf[:, :], r, start=st_, stop=sp__),
              waits=[mi_xb])
            E("pe", lambda e, b=sbank, r=sqb, st_=st, sp__=sp_:
              e.matmul(b[32:33, :], ones_bf[:, :], r, start=st_, stop=sp__),
              inc=t_pe, waits=[mi_sq])
            free[("cxb", par)] = (t_pe, t_pe.n)
            free[("csq", th)] = (t_pe, t_pe.n)
        free[("zb", ot % 2)] = (t_dve, t_dve.n)
    mi_cstats = (t_pe, t_pe.n)
    mi_x2fm = [(Strk[i], Strk[i].n) for i in range(1, 5)]

    mi_rows2 = ln_rows(mi_cstats, 2048)

    bcm = f3[:, 0:1024]
    bcr = f3[:, 1024:2048]
    mi_bc = None
    for h in range(2):
        m_h = mrow_sb[0:1, 2048 + 512 * h:2048 + 512 * h + 512]
        r_h = mrow_sb[0:1, 3072 + 512 * h:3072 + 512 * h + 512]
        E("pe", lambda e, r=m_h: e.matmul(OA[:, :], onesr_f[:, :], r, start=True, stop=True),
          inc=t_pe, waits=[mi_rows2, free.get(("zb", 0)), free.get(("zb", 1))])
        mi1 = (t_pe, t_pe.n)
        E("pe", lambda e, r=r_h: e.matmul(OB[:, :], onesr_f[:, :], r, start=True, stop=True),
          inc=t_pe)
        mi2 = (t_pe, t_pe.n)
        E("dve", lambda e, d=bcm, h_=h: e.tensor_copy(out=d[:, 512 * h_:512 * h_ + 512],
                                                      in_=OA[:, :]),
          inc=t_dve, waits=[mi1])
        E("dve", lambda e, d=bcr, h_=h: e.tensor_copy(out=d[:, 512 * h_:512 * h_ + 512],
                                                      in_=OB[:, :]),
          inc=t_dve, waits=[mi2])
        mi_bc = (t_dve, t_dve.n)
        free[("zb", 0)] = mi_bc
        free[("zb", 1)] = mi_bc

    for cc in range(NCC):
        sl = cc % 2
        x2s = f3[:, 4096 + 1024 * sl:4096 + 1024 * sl + 1024]
        tmp = f3[:, 2048 + 1024 * sl:2048 + 1024 * sl + 1024]
        E("sp", lambda e, d=x2s, c_=cc:
          e.dma_start(out=d, in_=x2fm_d[:, 1024 * c_:1024 * c_ + 1024]),
          inc=L[2 + sl], waits=(mi_x2fm + [mi_cstats] if cc < 2 else []) + [free.get(("h2x", sl))])
        mi_x = (L[2 + sl], L[2 + sl].n)
        E("pool", lambda e, d=tmp, s=x2s, m=bcm: e.tensor_sub(d, s, m),
          inc=t_pool, waits=[mi_x, mi_bc, free.get(("h2t", sl))])
        mi_t = (t_pool, t_pool.n)
        free[("h2x", sl)] = mi_t
        h2v = a2[:, 1024 * cc:1024 * cc + 1024]
        E("pool", lambda e, d=h2v, s=tmp, r=bcr: e.tensor_mul(d, s, r),
          inc=t_pool, waits=[mi_V if cc == 0 else None])
        mi_h = (t_pool, t_pool.n)
        free[("h2t", sl)] = mi_h
        E("dve", lambda e, d=hsum_sb[:, cc:cc + 1], s=h2v: e.reduce_sum(d, s, axis=AX),
          inc=t_dve, waits=[mi_h])
    mi_h2 = (t_pool, t_pool.n)
    mi_hsum = (t_dve, t_dve.n)

    for tt in range(8):
        sl = tt % 2
        xsl = f1[:, 4096 * sl:4096 * sl + 4096]
        stg = f1[:, 8192 + 4096 * sl:8192 + 4096 * sl + 4096]
        E("sp", lambda e, d=xsl, t_=tt:
          e.dma_start(out=d.rearrange("p (cc t) -> p cc t", t=128),
                      in_=x2fm_d.rearrange("p (cc t) -> p cc t", t=1024)
                      [:, 0:32, 128 * t_:128 * t_ + 128]),
          inc=L[8 + sl], waits=mi_x2fm + [free.get(("xt", sl)),
                            free.get("yh") if tt < 2 else None])
        mi_x = (L[8 + sl], L[8 + sl].n)
        for g in range(8):
            for j in range(4):
                cc = 4 * g + j
                E("pe", lambda e, s=xsl, c_=cc, j_=j:
                  e.transpose(AUX[:, 128 * j_:128 * j_ + 128],
                              s[:, 128 * c_:128 * c_ + 128], ident_sb[:, :]),
                  inc=t_pe if j == 3 else None,
                  waits=[mi_x, mi_rows2, free.get("xaux")] if j == 0 else [])
            mi_tr = (t_pe, t_pe.n)
            E("act", lambda e, d=stg, g_=g: e.copy(out=d[:, 512 * g_:512 * g_ + 512],
                                                   in_=AUX[:, :]),
              inc=t_act, waits=[mi_tr, free.get(("xstg", sl)) if g == 0 else None])
            free["xaux"] = (t_act, t_act.n)
        mi_stg = (t_act, t_act.n)
        free[("xt", sl)] = mi_stg
        E("act", lambda e, s=stg, t_=tt:
          e.dma_start(out=x2tm_d[128 * t_:128 * t_ + 128, :], in_=s),
          inc=Strk[3 + sl], waits=[mi_stg])
        free[("xstg", sl)] = (Strk[3 + sl], Strk[3 + sl].n)
    mi_x2tm = [(Strk[3], Strk[3].n), (Strk[4], Strk[4].n)]

    E("act", lambda e: e.dma_start(out=hsin_d, in_=hsum_sb[:, :]),
      inc=Strk[5], waits=[mi_hsum])
    mi_hs = (Strk[5], Strk[5].n)
    E("pool", lambda e: e.collective_compute(
        "AllReduce", ALU.add, replica_groups=RG, ins=[hsin_d.opt()], outs=[hsout_d.opt()]),
      inc=t_cc, waits=[mi_hs])
    mi_ar = (t_cc, t_cc.n)
    mcol = f3[:, 6144:6176]
    E("sp", lambda e: e.dma_start(out=mcol, in_=hsout_d), inc=L[1], waits=[mi_ar])
    mi_m = (L[1], L[1].n)
    E("dve", lambda e: e.tensor_scalar_mul(mcol, mcol, 1.0 / T), inc=t_dve, waits=[mi_m])
    mi_m2 = (t_dve, t_dve.n)
    wqs = f1[:, 0:4096]
    E("sp", lambda e: e.dma_start(out=wqs, in_=wqT),
      inc=L[3], waits=[free.get(("xt", 0))])
    mi_wq = (L[3], L[3].n)
    for cc in range(NCC):
        E("pe", lambda e, c_=cc, st=(cc == 0), sp_=(cc == NCC - 1):
          e.matmul(AUX[0:1, 0:128], mcol[:, c_:c_ + 1], wqs[:, 128 * c_:128 * c_ + 128],
                   start=st, stop=sp_),
          inc=t_pe if cc == NCC - 1 else None,
          waits=[mi_m2, mi_wq, free.get("xaux"), (t_act, P.tracks["act"].n)] if cc == 0 else [])
    mi_qmm = (t_pe, t_pe.n)
    qrow = f3[:, 6176:6304]
    E("dve", lambda e: e.tensor_copy(out=qrow[0:1, :], in_=AUX[0:1, 0:128]),
      inc=t_dve, waits=[mi_qmm])
    mi_qr = (t_dve, t_dve.n)
    E("pe", lambda e: e.transpose(AUX[:, 256:257], qrow[0:1, :], ident_sb[0:1, 0:1]),
      inc=t_pe, waits=[mi_qr])
    mi_qt = (t_pe, t_pe.n)
    E("dve", lambda e: e.tensor_copy(out=qT_sb[:, :], in_=AUX[:, 256:257]),
      inc=t_dve, waits=[mi_qt])
    mi_qT = (t_dve, t_dve.n)
    E("pe", lambda e: e.matmul(AUX[0:1, 0:8], qT_sb[:, :], keys_sb[:, :],
                               start=True, stop=True),
      inc=t_pe, waits=[mi_qT, mi_keys])
    mi_sc = (t_pe, t_pe.n)
    e8 = f3[:, 6304:6312]
    rt = f3[:, 6312:6320]
    E("dve", lambda e: e.reduce_max(rt[0:1, 0:1], AUX[0:1, 0:8], axis=AX, negate=True),
      inc=t_dve, waits=[mi_sc])
    E("dve", lambda e: e.tensor_scalar_mul(rt[0:1, 0:1], rt[0:1, 0:1], SCALE_MAIN),
      inc=t_dve)
    mi_ng = (t_dve, t_dve.n)
    E("act", lambda e: e.activation(out=e8[0:1, :], in_=AUX[0:1, 0:8], func=AF.Exp,
                                    scale=SCALE_MAIN, bias=rt[0:1, 0:1]),
      inc=t_act, waits=[mi_ng])
    mi_e8 = (t_act, t_act.n)
    E("dve", lambda e: e.reduce_sum(rt[0:1, 1:2], e8[0:1, :], axis=AX),
      inc=t_dve, waits=[mi_e8])
    E("dve", lambda e: e.reciprocal(rt[0:1, 2:3], rt[0:1, 1:2]), inc=t_dve)
    E("dve", lambda e: e.tensor_scalar_mul(e8[0:1, :], e8[0:1, :], rt[0:1, 2:3]),
      inc=t_dve)
    mi_cf = (t_dve, t_dve.n)
    E("pe", lambda e: e.matmul(AUX[:, 0:8], onesr_f[:, :], e8[0:1, :], start=True, stop=True),
      inc=t_pe, waits=[mi_cf])
    mi_cemm = (t_pe, t_pe.n)
    E("dve", lambda e: e.tensor_copy(out=ce_sb[:, :], in_=AUX[:, 0:8]),
      inc=t_dve, waits=[mi_cemm])
    mi_ce = (t_dve, t_dve.n)

    if debug_stage >= 4:
        E("pool", lambda e: e.dma_start(out=dbg["x2"], in_=x2tm_d), inc=t_ld,
          waits=mi_x2tm)
        E("act", lambda e: e.dma_start(
            out=dbg["h2"].rearrange("(cc p) t -> p cc t", p=128),
            in_=a2[:, :].rearrange("p (cc t) -> p cc t", t=1024)), inc=t_ld2,
          waits=[mi_h2])
        E("act", lambda e: e.dma_start(out=dbg["coeffs"], in_=e8[0:1, :]),
          inc=t_ld2, waits=[mi_cf])
    mi_dbg4 = (t_ld2, t_ld2.n) if debug_stage >= 4 else None
    if debug_stage == 4:
        return _finish(nc, P, stack)

    gpack = a3[0:64, 0:8192]
    gbanks = [OA, OB, SUMS, AUX]
    mi_gev = {}
    for e_ in range(8):
        sl = e_ % 2
        dwf = f1[:, 2048 * sl:2048 * sl + 2048]
        dwb = a3[:, 8192 + 2048 * sl:8192 + 2048 * sl + 2048]
        E("sp", lambda e, d=dwf, i=e_:
          e.dma_start(out=d, in_=downT[:, 2048 * i:2048 * i + 2048]),
          inc=L[4 + sl], waits=[free.get(("df", sl)), mi_qmm if e_ < 2 else None])
        mi_w = (L[4 + sl], L[4 + sl].n)
        E("pool", lambda e, d=dwb, s=dwf: e.tensor_copy(out=d, in_=s),
          inc=t_pool, waits=[mi_w, free.get(("db", sl))])
        mi_wb = (t_pool, t_pool.n)
        free[("df", sl)] = mi_wb
        for th in range(2):
            gi = (e_ * 2 + th) % 4
            bank = gbanks[gi]
            E_waits = [mi_wb, mi_h2, mi_gev.get(gi),
                       mi_ce if e_ == 0 else None]
            for cc in range(NCC):
                E("pe", lambda e, b=bank, w=dwb, cc_=cc, t_=th,
                  st=(cc == 0), sp_=(cc == NCC - 1):
                  e.matmul(b[0:64, :], w[:, 64 * cc_:64 * cc_ + 64],
                           a2[:, 1024 * cc_ + 512 * t_:1024 * cc_ + 512 * t_ + 512],
                           start=st, stop=sp_),
                  inc=t_pe if cc == NCC - 1 else None,
                  waits=E_waits if cc == 0 else [])
            mi_mm = (t_pe, t_pe.n)
            E("dve", lambda e, b=bank, i=e_, t_=th:
              e.tensor_scalar_mul(
                  gpack[0:64, 1024 * i + 512 * t_:1024 * i + 512 * t_ + 512],
                  b[0:64, :], ce_sb[0:64, i:i + 1]),
              inc=t_dve, waits=[mi_mm, mi_ce])
            mi_gev[gi] = (t_dve, t_dve.n)
        free[("db", sl)] = (t_pe, t_pe.n)
    mi_g = (t_dve, t_dve.n)
    if debug_stage == 41:
        return _finish(nc, P, stack)

    outg = {0: [(SA, 0), (SA, 256), (SA, 512), (SA, 768),
                (SB, 0), (SB, 256), (SB, 512), (SB, 768)],
            1: [(OA, 0), (OA, 256), (OB, 0), (OB, 256),
                (SUMS, 0), (SUMS, 256), (AUX, 0), (AUX, 256)]}
    mi_outev = {}
    n_ot = 2 if debug_stage in (42, 43, 44) else 16
    skip_up = (debug_stage == 43)
    mi_outall = None
    mi_wprev = None
    mi_upprev = None
    wbf = a3[:, 8192:16384]
    upb = a1[0:64, 10240:12288]
    upb2 = a1[0:64, 12288:14336]
    for ot in range(n_ot):
        wff = f1[:, 8192:16384]
        E("sp", lambda e, d=wff, o=ot:
          e.dma_start(out=d, in_=worgT[:, 8192 * o:8192 * o + 8192]),
          inc=L[6], waits=[mi_wprev, mi_g if ot == 0 else None,
                           mi_dbg4 if ot == 0 else None]
          + (mi_x2tm if ot == 0 else []))
        mi_wf = (L[6], L[6].n)
        E("dve", lambda e, d=wbf, s=wff: e.tensor_copy(out=d, in_=s),
          inc=t_dve, waits=[mi_wf, mi_wprev])
        mi_wcast = (t_dve, t_dve.n)
        usl = ot % 2
        upx = upb if usl == 0 else upb2
        upff = f1[0:64, 2048 + 2048 * usl:2048 + 2048 * usl + 2048]
        E("sp", lambda e, d=upff, o=ot:
          e.dma_start(out=d, in_=upw[:, 2048 * o:2048 * o + 2048]),
          inc=L[8 + usl], waits=[free.get(("uf", usl)), mi_g if ot < 2 else None])
        mi_uf = (L[8 + usl], L[8 + usl].n)
        E("pool", lambda e, d=upx, s=upff: e.tensor_copy(out=d, in_=s),
          inc=t_pool, waits=[mi_uf, free.get(("ub", usl))])
        mi_up = (t_pool, t_pool.n)
        free[("uf", usl)] = mi_up
        grp = outg[ot % 2]
        mi_tt = {}
        for tt in range(8):
            bnk, off = grp[tt]
            for cc in range(NCC):
                E("pe", lambda e, b=bnk, o_=off, cc_=cc, t_=tt,
                  st=(cc == 0), sp_=(skip_up and cc == NCC - 1):
                  e.matmul(b[:, o_:o_ + 256],
                           a2[:, 1024 * cc_ + 128 * t_:1024 * cc_ + 128 * t_ + 128],
                           wbf[:, 256 * cc_:256 * cc_ + 256], start=st, stop=sp_),
                  inc=t_pe if (skip_up and cc == NCC - 1) else None,
                  waits=[mi_wcast, mi_g, mi_outev.get((ot % 2, tt))] if cc == 0 else [])
            if not skip_up:
                for e_ in range(8):
                    E("pe", lambda e, b=bnk, o_=off, i=e_, t_=tt, sp_=(e_ == 7), u=upx:
                      e.matmul(b[:, o_:o_ + 256],
                               gpack[0:64, 1024 * i + 128 * t_:1024 * i + 128 * t_ + 128],
                               u[0:64, 256 * i:256 * i + 256],
                               start=False, stop=sp_),
                      inc=t_pe if e_ == 7 else None,
                      waits=[mi_up] if e_ == 0 else [])
            mi_tt[tt] = (t_pe, t_pe.n)
        for tt in range(8):
            bnk, off = grp[tt]
            mm_gate = mi_tt[3] if tt < 4 else mi_tt[7]
            xsl = f1[:, 7168 + 256 * (tt % 2):7168 + 256 * (tt % 2) + 256]
            och = f1[:, 7680 + 256 * (tt % 2):7680 + 256 * (tt % 2) + 256]
            E("sp", lambda e, d=xsl, o=ot, t_=tt:
              e.dma_start(out=d, in_=x2tm_d[128 * t_:128 * t_ + 128, 256 * o:256 * o + 256]),
              inc=L[tt % 2], waits=(mi_x2tm if ot == 0 else []) + [free.get(("ox", tt % 2))])
            mi_x = (L[tt % 2], L[tt % 2].n)
            E("dve", lambda e, d=och, b=bnk, o_=off, x=xsl:
              e.tensor_add(d, b[:, o_:o_ + 256], x),
              inc=t_dve, waits=[mm_gate, mi_x, free.get(("oo", tt % 2))])
            mi_o = (t_dve, t_dve.n)
            free[("ox", tt % 2)] = mi_o
            mi_outev[(ot % 2, tt)] = mi_o
            E("act", lambda e, s=och, o=ot, t_=tt:
              e.dma_start(out=out[128 * t_:128 * t_ + 128, 256 * o:256 * o + 256], in_=s),
              inc=Strk[tt % 2], waits=[mi_o])
            free[("oo", tt % 2)] = (Strk[tt % 2], Strk[tt % 2].n)
        mi_wprev = (t_pe, t_pe.n)
        free[("ub", usl)] = (t_pe, t_pe.n)

    return _finish(nc, P, stack)


def _finish(nc, P, stack):
    semstack = contextlib.ExitStack()
    sems = {name: semstack.enter_context(nc.semaphore(f"sem_{name}"))
            for name in P.tracks}
    final = [(t, t.n) for t in P.tracks.values() if t.n > 0]
    P.emit("sp", lambda e: e.nop(), waits=final)
    with nc.Block() as block:
        P.realize(block, sems)
    semstack.close()
    stack.close()
    return nc



def _prep(x, w_attn, w_proj, adapter_wte, gating, W_org, lora_down, lora_up,
          lora_keys, Wq):
    x = np.asarray(x, dtype=np.float32).reshape(T, C)
    xT = np.ascontiguousarray(x.T)
    w_attn = np.asarray(w_attn, dtype=np.float32)
    theta = 1.0 / (10000.0 ** (np.arange(0, HD, 2, dtype=np.float32) / HD))
    idx = np.outer(np.arange(T, dtype=np.float32), theta).astype(np.float32)
    idx2 = np.concatenate([idx, idx], axis=-1)
    cosT = np.ascontiguousarray(np.cos(idx2).astype(np.float32).T)
    sinST = np.ascontiguousarray(np.sin(idx2).astype(np.float32).T).copy()
    sinST[0:64, :] *= -1.0
    mk = np.zeros((4, 128, S), dtype=np.float32)
    qq = np.arange(S)[None, :]
    kk = np.arange(128)[:, None]
    for i, o in enumerate((0, 128, 256, 384)):
        mk[i] = np.where(qq >= kk + o, 0.0, NEG).astype(np.float32)
    shared = {
        "wprojT": np.ascontiguousarray(
            np.asarray(w_proj, np.float32).T.reshape(32, 128, 32, 128)
            .transpose(1, 2, 0, 3).reshape(128, 32 * 4096)),
        "worgT": np.ascontiguousarray(
            np.asarray(W_org, np.float32).T.reshape(32, 128, 16, 256)
            .transpose(1, 2, 0, 3).reshape(128, 16 * 8192)),
        "downT": np.ascontiguousarray(
            np.asarray(lora_down, np.float32).reshape(8, 32, 128, 64)
            .transpose(2, 0, 1, 3).reshape(128, 8 * 2048)),
        "upw": np.ascontiguousarray(
            np.asarray(lora_up, np.float32).reshape(8, 64, 16, 256)
            .transpose(1, 2, 0, 3).reshape(64, 16 * 2048)),
        "wqT": np.ascontiguousarray(
            np.asarray(Wq, np.float32).T.reshape(32, 128, 128)
            .transpose(1, 0, 2).reshape(128, 4096)),
        "keysT": np.ascontiguousarray(np.asarray(lora_keys, np.float32).T),
        "adT": np.ascontiguousarray(np.asarray(adapter_wte, np.float32).T),
        "gat": np.asarray(gating, np.float32).reshape(1, 1),
        "cosT": cosT,
        "sinS": sinST,
        "masks": mk,
    }
    in_maps = []
    for r in range(NCORES):
        im = dict(shared)
        im["xT"] = np.ascontiguousarray(xT[:, TOWN * r:TOWN * (r + 1)])
        rows = np.concatenate([w_attn[HC * r:HC * (r + 1)],
                               w_attn[C + HC * r:C + HC * (r + 1)],
                               w_attn[2 * C + HC * r:2 * C + HC * (r + 1)]], axis=0)
        im["wqkvT"] = np.ascontiguousarray(
            rows.T.reshape(32, 128, 12, 128).transpose(1, 2, 0, 3).reshape(128, 12 * 4096))
        in_maps.append(im)
    return in_maps


def kernel(L0_attn_x, L0_attn_w_attn, L0_attn_w_proj, L0_attn_adapter_wte,
           L0_attn_gating, L0_ffn_W_org, L0_ffn_lora_down, L0_ffn_lora_up,
           L0_ffn_lora_keys, L0_ffn_Wq):
    in_maps = _prep(L0_attn_x, L0_attn_w_attn, L0_attn_w_proj, L0_attn_adapter_wte,
                    L0_attn_gating, L0_ffn_W_org, L0_ffn_lora_down, L0_ffn_lora_up,
                    L0_ffn_lora_keys, L0_ffn_Wq)
    nc = _build(DEBUG_STAGE)
    res = run_bass_kernel_spmd(nc, in_maps, core_ids=list(range(NCORES)))
    if DEBUG_STAGE:
        kernel.last_results = res
        if "out" not in res.results[0]:
            return None
    outs = [res.results[r]["out"] for r in range(NCORES)]
    return np.concatenate(outs, axis=0).reshape(1, T, C)


# version 09
# speedup vs baseline: 1.0836x; 1.0472x over previous
import contextlib
import math
import numpy as np
import concourse.bass as bass
import concourse.mybir as mybir
from concourse.bass_utils import run_bass_kernel_spmd

F32 = mybir.dt.float32
BF16 = mybir.dt.bfloat16
AF = mybir.ActivationFunctionType
ALU = mybir.AluOpType
AX = mybir.AxisListType.X

NCORES = 8
T = 8192
C = 4096
HD = 128
HPC = 4
HC = HPC * HD
TOWN = T // NCORES
S = 512
NTS = T // S
NCC = C // 128
SCALE_MAIN = 1.0 / math.sqrt(HD)
SCALE_AD = 1.0 / math.sqrt(C)
EPS = 1e-5
NEG = -1.0e30

DEBUG_STAGE = 0


class _Trk:
    __slots__ = ("name", "step", "n")

    def __init__(self, name, step):
        self.name, self.step, self.n = name, step, 0

    def bump(self):
        self.n += self.step
        return self.n


class _Fut:
    __slots__ = ("trk", "val")

    def __init__(self):
        self.trk, self.val = None, None


class _Prog:
    ENGS = ("pe", "act", "dve", "pool", "sp")

    def __init__(self):
        self.ops = {e: [] for e in self.ENGS}
        self.tracks = {}

    def trk(self, name, step=1):
        if name not in self.tracks:
            self.tracks[name] = _Trk(name, step)
        return self.tracks[name]

    SERIAL = ("dve", "act", "pool")

    def emit(self, eng, fn, inc=None, waits=()):
        ws = [w for w in waits
              if w is not None and (isinstance(w, _Fut) or w[1])]
        swait = None
        if eng in self.SERIAL:
            et = self.trk(eng)
            if inc is None:
                inc = et
            if et.n > 0:
                swait = (et, et.n)
        if swait is not None:
            ws.append(swait)
        self.ops[eng].append((fn, inc, inc.bump() if inc is not None else None, ws))
        return (inc, inc.n) if inc is not None else None

    def realize(self, block, sems):
        def runner(ename):
            def go(e):
                lw = {}
                for fn, trk, _val, waits in self.ops[ename]:
                    need = {}
                    for w in waits:
                        if isinstance(w, _Fut):
                            if w.trk is None or not w.val:
                                continue
                            t, v = w.trk, w.val
                        else:
                            t, v = w
                        if lw.get(t.name, 0) < v:
                            need[t.name] = max(need.get(t.name, 0), v)
                    for tn, v in need.items():
                        e.wait_ge(sems[tn], v)
                        lw[tn] = v
                    ins = fn(e)
                    if trk is not None:
                        ins.then_inc(sems[trk.name], trk.step)
            return go
        block.tensor(runner("pe"))
        block.scalar(runner("act"))
        block.vector(runner("dve"))
        block.gpsimd(runner("pool"))
        block.sync(runner("sp"))


def _build(debug_stage=0, shared_cc=True):
    nc = bass.Bass("TRN2", target_bir_lowering=False, num_devices=NCORES)
    RG = [list(range(NCORES))]

    xT = nc.dram_tensor("xT", [C, TOWN], F32, kind="ExternalInput").ap()
    wqkvT = nc.dram_tensor("wqkvT", [128, 12 * 4096], F32, kind="ExternalInput").ap()
    wprojT = nc.dram_tensor("wprojT", [128, 32 * 4096], F32, kind="ExternalInput").ap()
    worgT = nc.dram_tensor("worgT", [128, 16 * 8192], F32, kind="ExternalInput").ap()
    downT = nc.dram_tensor("downT", [128, 8 * 2048], F32, kind="ExternalInput").ap()
    upw = nc.dram_tensor("upw", [64, 16 * 2048], F32, kind="ExternalInput").ap()
    wqT = nc.dram_tensor("wqT", [128, 4096], F32, kind="ExternalInput").ap()
    keysT = nc.dram_tensor("keysT", [128, 8], F32, kind="ExternalInput").ap()
    adT = nc.dram_tensor("adT", [C, 10], F32, kind="ExternalInput").ap()
    gat = nc.dram_tensor("gat", [1, 1], F32, kind="ExternalInput").ap()
    cosT = nc.dram_tensor("cosT", [128, T], F32, kind="ExternalInput").ap()
    sinS = nc.dram_tensor("sinS", [128, T], F32, kind="ExternalInput").ap()
    masks = nc.dram_tensor("masks", [4, 128, S], F32, kind="ExternalInput").ap()
    out = nc.dram_tensor("out", [TOWN, C], F32, kind="ExternalOutput").ap()

    h1own_d = nc.dram_tensor("h1own_d", [256, 16384], BF16, kind="Internal").ap()
    h1all_d = nc.dram_tensor("h1all_d", [2048, 16384], BF16, kind="Internal", addr_space="Shared" if shared_cc else "Local").ap()
    wqkv_bf_d = nc.dram_tensor("wqkv_bf_d", [128, 12 * 4096], BF16, kind="Internal").ap()
    q_d = nc.dram_tensor("q_d", [HC, T], BF16, kind="Internal").ap()
    k_d = nc.dram_tensor("k_d", [HC, T], BF16, kind="Internal").ap()
    v_d = nc.dram_tensor("v_d", [128, 32768], BF16, kind="Internal").ap()
    yown_d = nc.dram_tensor("yown_d", [8 * HC, 1024], BF16, kind="Internal").ap()
    yall_d = nc.dram_tensor("yall_d", [8 * NCORES * HC, 1024], BF16, kind="Internal", addr_space="Shared" if shared_cc else "Local").ap()
    x2fm_d = nc.dram_tensor("x2fm_d", [128, 32768], F32, kind="Internal").ap()
    x2tm_d = nc.dram_tensor("x2tm_d", [TOWN, C], F32, kind="Internal").ap()
    bc_d = nc.dram_tensor("bc_d", [4, 512], BF16, kind="Internal").ap()
    hsin_d = nc.dram_tensor("hsin_d", [128, NCC], F32, kind="Internal").ap()
    hsout_d = nc.dram_tensor("hsout_d", [128, NCC], F32, kind="Internal").ap()

    dbg = {}
    if debug_stage >= 1:
        dbg["h1all"] = nc.dram_tensor("dbg_h1all", [2048, 16384], BF16, kind="ExternalOutput").ap()
    if debug_stage >= 2:
        dbg["q"] = nc.dram_tensor("dbg_q", [HC, T], BF16, kind="ExternalOutput").ap()
        dbg["k"] = nc.dram_tensor("dbg_k", [HC, T], BF16, kind="ExternalOutput").ap()
        dbg["v"] = nc.dram_tensor("dbg_v", [128, 32768], BF16, kind="ExternalOutput").ap()
        dbg["ak"] = nc.dram_tensor("dbg_ak", [128, 40], BF16, kind="ExternalOutput").ap()
        dbg["av"] = nc.dram_tensor("dbg_av", [16, 512], BF16, kind="ExternalOutput").ap()
        dbg["wbf"] = nc.dram_tensor("dbg_wbf", [C, 3 * HC], BF16, kind="ExternalOutput").ap()
    if debug_stage >= 3:
        dbg["yall"] = nc.dram_tensor("dbg_yall", [8 * NCORES * HC, 1024], BF16, kind="ExternalOutput").ap()
        dbg["P0"] = nc.dram_tensor("dbg_P0", [128, 1024], BF16, kind="ExternalOutput").ap()
        dbg["rows"] = nc.dram_tensor("dbg_rows", [8, 512], F32, kind="ExternalOutput").ap()
        dbg["expA"] = nc.dram_tensor("dbg_expA", [16, 512], BF16, kind="ExternalOutput").ap()
        dbg["expA2"] = nc.dram_tensor("dbg_expA2", [16, 512], BF16, kind="ExternalOutput").ap()
        dbg["oev"] = nc.dram_tensor("dbg_oev", [128, 512], F32, kind="ExternalOutput").ap()
    if debug_stage >= 4:
        dbg["x2"] = nc.dram_tensor("dbg_x2", [TOWN, C], F32, kind="ExternalOutput").ap()
        dbg["h2"] = nc.dram_tensor("dbg_h2", [C, TOWN], BF16, kind="ExternalOutput").ap()
        dbg["coeffs"] = nc.dram_tensor("dbg_coeffs", [1, 8], F32, kind="ExternalOutput").ap()

    ident_d = nc.inline_tensor(np.eye(128, dtype=np.float32), name="ident_c").ap()

    P = _Prog()
    t_pe = P.trk("pe")
    t_act = P.trk("act")
    t_dve = P.trk("dve")
    t_pool = P.trk("pool")
    L = {i: P.trk(f"L{i}", 16) for i in range(10)}
    Strk = {i: P.trk(f"S{i}", 16) for i in range(6)}
    t_li = P.trk("Li", 16)
    t_fr = [P.trk("FR0", 16), P.trk("FR1", 16)]
    t_rr = [P.trk("RR0", 16), P.trk("RR1", 16)]
    t_fb = [P.trk("FB0", 16), P.trk("FB1", 16)]
    t_rb = [P.trk("RB0", 16), P.trk("RB1", 16)]
    t_ys = [P.trk("YS0", 16), P.trk("YS1", 16)]
    t_ld = P.trk("Ld", 16)
    t_ld2 = P.trk("Ld2", 16)
    t_cc = P.trk("cc", 1)

    stack = contextlib.ExitStack()
    ar1 = stack.enter_context(nc.sbuf_tensor("ar1", [128, 32768], BF16))
    ar2 = stack.enter_context(nc.sbuf_tensor("ar2", [128, 32768], BF16))
    ar3 = stack.enter_context(nc.sbuf_tensor("ar3", [128, 16384], BF16))
    ones_bf = stack.enter_context(nc.sbuf_tensor("ones_bf", [128, 1], BF16))
    onesr_bf = stack.enter_context(nc.sbuf_tensor("onesr_bf", [1, 128], BF16))
    onesr_f = stack.enter_context(nc.sbuf_tensor("onesr_f", [1, 128], F32))
    ident_sb = stack.enter_context(nc.sbuf_tensor("ident_sb", [128, 128], F32))
    akT_sb = stack.enter_context(nc.sbuf_tensor("akT_sb", [128, 40], BF16))
    av_sb = stack.enter_context(nc.sbuf_tensor("av_sb", [16, 512], BF16))
    adT_sb = stack.enter_context(nc.sbuf_tensor("adT_sb", [128, 320], BF16))
    g_sb = stack.enter_context(nc.sbuf_tensor("g_sb", [1, 1], F32))
    ce_sb = stack.enter_context(nc.sbuf_tensor("ce_sb", [128, 8], F32))
    keys_sb = stack.enter_context(nc.sbuf_tensor("keys_sb", [128, 8], F32))
    qT_sb = stack.enter_context(nc.sbuf_tensor("qT_sb", [128, 1], F32))
    hsum_sb = stack.enter_context(nc.sbuf_tensor("hsum_sb", [128, NCC], F32))
    mrow_sb = stack.enter_context(nc.sbuf_tensor("mrow_sb", [1, 4096], F32))
    srow_sb = stack.enter_context(nc.sbuf_tensor("srow_sb", [1, 1024], F32))
    mask_sb = stack.enter_context(nc.sbuf_tensor("mask_sb", [128, 4 * S], F32))

    a1 = ar1.ap()
    a2 = ar2.ap()
    a3 = ar3.ap()
    f1 = ar1.ap().bitcast(F32)
    f2 = ar2.ap().bitcast(F32)
    f3 = ar3.ap().bitcast(F32)

    SA = stack.enter_context(nc.psum_tensor("SA", [128, 1024], F32))
    SB = stack.enter_context(nc.psum_tensor("SB", [128, 1024], F32))
    OA = stack.enter_context(nc.psum_tensor("OA", [128, 512], F32))
    OB = stack.enter_context(nc.psum_tensor("OB", [128, 512], F32))
    SUMS = stack.enter_context(nc.psum_tensor("SUMS", [128, 512], F32))
    AUX = stack.enter_context(nc.psum_tensor("AUX", [128, 512], F32))

    E = P.emit

    E("pool", lambda e: e.memset(ones_bf[:, :], 1.0))
    E("pool", lambda e: e.memset(onesr_bf[:, :], 1.0))
    E("pool", lambda e: e.memset(onesr_f[:, :], 1.0))
    E("pool", lambda e: e.memset(av_sb[:, :], 0.0), inc=t_pool)
    mi_ones = (t_pool, t_pool.n)
    E("sp", lambda e: e.dma_start(out=ident_sb[:, :], in_=ident_d), inc=t_li)
    E("sp", lambda e: e.dma_start(
        out=mask_sb[:, :].rearrange("p (m s) -> p m s", m=4),
        in_=masks.rearrange("m p s -> p m s")), inc=t_li)
    E("sp", lambda e: e.dma_start(out=g_sb[:, :], in_=gat), inc=t_li)
    E("sp", lambda e: e.dma_start(out=keys_sb[:, :], in_=keysT), inc=t_li)
    E("sp", lambda e: e.dma_start(
        out=f3[:, 0:320].rearrange("p (cc a) -> p cc a", a=10),
        in_=adT.rearrange("(cc p) a -> p cc a", p=128)), inc=t_li)
    mi_init = (t_li, t_li.n)
    mi_ident = mi_masks = mi_gat = mi_keys = mi0 = mi_init
    E("pool", lambda e: e.tensor_copy(out=adT_sb[:, :], in_=f3[:, 0:320]),
      inc=t_pool, waits=[mi0])
    mi_adT = (t_pool, t_pool.n)

    free = {}
    for cc in range(NCC):
        sl = cc % 2
        xf = f3[:, 512 + 1024 * sl: 512 + 1024 * sl + 1024]
        xb = a1[:, 1024 * sl:1024 * sl + 1024]
        sq = a1[:, 2048 + 1024 * sl:2048 + 1024 * sl + 1024]
        E("sp", lambda e, d=xf, s=xT[128 * cc:128 * cc + 128, :]:
          e.dma_start(out=d, in_=s), inc=L[sl],
          waits=[free.get(("a0x", sl)), mi_adT if cc < 2 else None])
        mi_x = (L[sl], L[sl].n)
        E("pool", lambda e, d=xb, s=xf: e.tensor_copy(out=d, in_=s),
          inc=t_pool, waits=[mi_x, free.get(("a0b", sl))])
        mi_xb = (t_pool, t_pool.n)
        free[("a0x", sl)] = mi_xb
        E("act", lambda e, d=sq, s=xb: e.activation(out=d, in_=s, func=AF.Square),
          inc=t_act, waits=[mi_xb, free.get(("a0s", sl))])
        mi_sq = (t_act, t_act.n)
        st, sp_ = (cc == 0), (cc == NCC - 1)
        for h in range(2):
            bank = SUMS if h == 0 else AUX
            E("pe", lambda e, b=bank, r=xb, h_=h, st_=st, sp__=sp_:
              e.matmul(b[0:1, :], ones_bf[:, :], r[:, 512 * h_:512 * h_ + 512],
                       start=st_, stop=sp__),
              waits=[mi_xb, mi_ones] if (h == 0) else [])
            E("pe", lambda e, b=bank, r=sq, h_=h, st_=st, sp__=sp_:
              e.matmul(b[32:33, :], ones_bf[:, :], r[:, 512 * h_:512 * h_ + 512],
                       start=st_, stop=sp__),
              inc=t_pe if h == 1 else None, waits=[mi_sq] if h == 0 else [])
        free[("a0b", sl)] = (t_pe, t_pe.n)
        free[("a0s", sl)] = (t_pe, t_pe.n)
    mi_stats1 = (t_pe, t_pe.n)

    def ln_rows(mi_done, roff):
        out_mis = []
        for h in range(2):
            bank = SUMS if h == 0 else AUX
            m_h = mrow_sb[0:1, roff + 512 * h: roff + 512 * h + 512]
            r_h = mrow_sb[0:1, roff + 1024 + 512 * h: roff + 1024 + 512 * h + 512]
            tmp = srow_sb[0:1, 512 * h:512 * h + 512]
            E("act", lambda e, d=m_h, b=bank: e.mul(out=d, in_=b[0:1, :], mul=1.0 / C),
              inc=t_act, waits=[mi_done])
            mi_m = (t_act, t_act.n)
            E("act", lambda e, d=r_h, b=bank: e.mul(out=d, in_=b[32:33, :], mul=1.0 / C),
              inc=t_act)
            mi_e2 = (t_act, t_act.n)
            E("dve", lambda e, d=tmp, s=m_h: e.tensor_mul(d, s, s),
              inc=t_dve, waits=[mi_m])
            E("dve", lambda e, d=r_h, s=tmp: e.tensor_sub(d, d, s),
              inc=t_dve, waits=[mi_e2])
            E("dve", lambda e, d=r_h: e.tensor_scalar_add(d, d, EPS), inc=t_dve)
            mi_v = (t_dve, t_dve.n)
            E("act", lambda e, d=r_h: e.activation(out=d, in_=d, func=AF.Sqrt),
              inc=t_act, waits=[mi_v])
            mi_s = (t_act, t_act.n)
            E("dve", lambda e, d=r_h: e.reciprocal(d, d), inc=t_dve, waits=[mi_s])
            out_mis.append((t_dve, t_dve.n))
        return out_mis[-1]

    mi_rows1 = ln_rows(mi_stats1, 0)

    mi_ag_h1h = {}
    prev_half_done = None
    for h in range(2):
        m_h = mrow_sb[0:1, 512 * h:512 * h + 512]
        r_h = mrow_sb[0:1, 1024 + 512 * h:1024 + 512 * h + 512]
        E("pe", lambda e, r=m_h: e.matmul(OA[:, :], onesr_f[:, :], r, start=True, stop=True),
          inc=t_pe, waits=[mi_rows1, prev_half_done])
        E("pe", lambda e, r=r_h: e.matmul(OB[:, :], onesr_f[:, :], r, start=True, stop=True),
          inc=t_pe)
        mi_bc = (t_pe, t_pe.n)
        for cc in range(NCC):
            sl = cc % 2
            xf = f3[:, 512 + 512 * sl:512 + 512 * sl + 512]
            tm = f3[:, 1536 + 512 * sl:1536 + 512 * sl + 512]
            hb = a1[:, 4096 + 512 * sl:4096 + 512 * sl + 512]
            E("sp", lambda e, d=xf, s=xT[128 * cc:128 * cc + 128, 512 * h:512 * h + 512]:
              e.dma_start(out=d, in_=s), inc=L[sl],
              waits=[free.get(("nx", sl)), mi_stats1 if cc < 2 else None])
            mi_x = (L[sl], L[sl].n)
            E("dve", lambda e, d=tm, s=xf: e.tensor_sub(d, s, OA[:, :]),
              inc=t_dve, waits=[mi_x, mi_bc, free.get(("ntm", sl))])
            mi_sub = (t_dve, t_dve.n)
            free[("nx", sl)] = mi_sub
            E("dve", lambda e, d=hb, s=tm: e.tensor_mul(d, s, OB[:, :]),
              inc=t_dve, waits=[free.get(("nh", sl))])
            mi_h1 = (t_dve, t_dve.n)
            free[("ntm", sl)] = mi_h1
            E("act", lambda e, s=hb, cc_=cc, h_=h:
              e.dma_start(out=h1own_d[128 * h_:128 * h_ + 128, 512 * cc_:512 * cc_ + 512], in_=s),
              inc=Strk[sl], waits=[mi_h1])
            free[("nh", sl)] = (Strk[sl], Strk[sl].n)
        prev_half_done = (t_dve, t_dve.n)
        h_mi = [(Strk[0], Strk[0].n), (Strk[1], Strk[1].n)]
        E("pool", lambda e, h_=h: e.collective_compute(
            "AllGather", ALU.bypass, replica_groups=RG,
            ins=[h1own_d[128 * h_:128 * h_ + 128, :].opt()],
            outs=[h1all_d[1024 * h_:1024 * h_ + 1024, :].opt()]),
          inc=t_cc, waits=h_mi)
        mi_ag_h1h[h] = (t_cc, t_cc.n)
        mi_a0h1 = h_mi

    mi_wspill = None
    for i in range(12):
        sl = i % 2
        wf = f1[:, 4096 + 4096 * sl:4096 + 4096 * sl + 4096]
        wb = a1[:, 24576 + 4096 * sl:24576 + 4096 * sl + 4096]
        E("sp", lambda e, d=wf, s=wqkvT[:, 4096 * i:4096 * i + 4096]:
          e.dma_start(out=d, in_=s),
          inc=L[2 + sl], waits=[free.get(("wsf", sl))])
        mi_w = (L[2 + sl], L[2 + sl].n)
        E("dve", lambda e, d=wb, s=wf: e.tensor_copy(out=d, in_=s),
          inc=t_dve, waits=[mi_w, free.get(("wsb", sl))])
        mi_wb = (t_dve, t_dve.n)
        free[("wsf", sl)] = mi_wb
        E("act", lambda e, s=wb, i_=i:
          e.dma_start(out=wqkv_bf_d[:, 4096 * i_:4096 * i_ + 4096], in_=s),
          inc=Strk[2 + sl], waits=[mi_wb])
        free[("wsb", sl)] = (Strk[2 + sl], Strk[2 + sl].n)
    mi_wspill = [(Strk[2], Strk[2].n), (Strk[3], Strk[3].n)]

    if debug_stage >= 1:
        E("pool", lambda e: e.dma_start(out=dbg["h1all"], in_=h1all_d),
          inc=t_ld, waits=[mi_ag_h1h[0], mi_ag_h1h[1]])
    if debug_stage == 1:
        return _finish(nc, P, stack)

    qkv_banks = [OA[:, :], OB[:, :], SA[:, 0:512], SA[:, 512:1024]]
    h1_slots = [a1[:, 0:16384], a1[:, 16384:32768], a2[:, 13312:29696]]
    mi_adchain = mi_adT
    mi_h1slot = {}
    for it in range(8):
        hfo, q = it // 4, it % 4
        mi_h1s = {}
        for hf in range(2):
            ts = 2 * (2 * q + hf) + hfo
            slot = (2 * it + hf) % 3
            h1s = h1_slots[slot]
            E("sp", lambda e, d=h1s, r=1024 * hfo + 128 * (2 * q + hf):
              e.dma_start(out=d, in_=h1all_d[r:r + 128, :]),
              inc=L[slot], waits=[mi_ag_h1h[hfo], mi_h1slot.get(slot)]
              + (mi_wspill + mi_a0h1 if 2 * it + hf < 3 else []))
            mi_h1s[hf] = ((L[slot], L[slot].n), h1s, slot)
        csl = {}
        cs_trk = {(0, 0): L[4], (0, 1): L[5], (1, 0): L[8], (1, 1): L[9]}
        for hf in range(2):
            ts = 2 * (2 * q + hf) + hfo
            base = 3072 + 2048 * (q % 2) + 1024 * hf
            ctk = cs_trk[(q % 2, hf)]
            E("sp", lambda e, d=f3[:, base:base + 512], s_=cosT[:, 512 * ts:512 * ts + 512]:
              e.dma_start(out=d, in_=s_), inc=ctk, waits=[free.get(("cs", q % 2, hf))])
            E("sp", lambda e, d=f3[:, base + 512:base + 1024], s_=sinS[:, 512 * ts:512 * ts + 512]:
              e.dma_start(out=d, in_=s_), inc=ctk)
            csl[hf] = ((ctk, ctk.n), f3[:, base:base + 512], f3[:, base + 512:base + 1024])
        for ot in range(12):
            wsl = ot % 2
            wb = a2[:, 4096 * wsl:4096 * wsl + 4096]
            E("sp", lambda e, d=wb, o=ot:
              e.dma_start(out=d, in_=wqkv_bf_d[:, 4096 * o:4096 * o + 4096]),
              inc=L[6 + wsl], waits=(mi_wspill if it == 0 else []) + [free.get(("w", wsl))])
            mi_w = (L[6 + wsl], L[6 + wsl].n)
            bk0, bk1 = (0, 1) if ot % 2 == 0 else (2, 3)
            for cc in range(NCC):
                for hf in range(2):
                    bank = qkv_banks[bk0 + hf]
                    E("pe", lambda e, b=bank, w=wb, h_=mi_h1s[hf][1], cc_=cc,
                      st=(cc == 0), sp_=(cc == NCC - 1):
                      e.matmul(b, w[:, 128 * cc_:128 * cc_ + 128],
                               h_[:, 512 * cc_:512 * cc_ + 512], start=st, stop=sp_),
                      inc=t_pe if (cc == NCC - 1 and hf == 1) else None,
                      waits=[mi_w, mi_h1s[0][0], mi_h1s[1][0],
                             free.get(("qb", bk0)), free.get(("qb", bk0 + 1))]
                      if (cc == 0 and hf == 0) else [])
            mi_mm = (t_pe, t_pe.n)
            free[("w", wsl)] = mi_mm
            mi_h1slot[mi_h1s[0][2]] = mi_mm
            mi_h1slot[mi_h1s[1][2]] = mi_mm
            if it == 0 and ot >= 4:
                for cc in range(NCC):
                    E("pe", lambda e, w=wb, cc_=cc, st=(cc == 0), sp_=(cc == NCC - 1):
                      e.matmul(SUMS[:, 0:10], w[:, 128 * cc_:128 * cc_ + 128],
                               adT_sb[:, 10 * cc_:10 * cc_ + 10], start=st, stop=sp_),
                      inc=t_pe if cc == NCC - 1 else None,
                      waits=[mi_adchain] if cc == 0 else [])
                mi_amm = (t_pe, t_pe.n)
                free[("w", wsl)] = mi_amm
                if ot < 8:
                    E("dve", lambda e, o=ot: e.tensor_copy(
                        out=akT_sb[:, 10 * (o - 4):10 * (o - 4) + 10], in_=SUMS[:, 0:10]),
                      inc=t_dve, waits=[mi_amm])
                    mi_adchain = (t_dve, t_dve.n)
                else:
                    avt = f3[:, 7168:7184]
                    E("dve", lambda e, d=avt: e.tensor_copy(out=d[:, 0:10], in_=SUMS[:, 0:10]),
                      inc=t_dve, waits=[mi_amm])
                    mi_ae = (t_dve, t_dve.n)
                    E("pe", lambda e, s_=avt: e.transpose(AUX[0:10, 0:128], s_[:, 0:10], ident_sb[:, :]),
                      inc=t_pe, waits=[mi_ae, mi_ident, free.get("aux_a1")])
                    mi_tr = (t_pe, t_pe.n)
                    E("act", lambda e, o=ot: e.copy(
                        out=av_sb[0:10, 128 * (o - 8):128 * (o - 8) + 128], in_=AUX[0:10, 0:128]),
                      inc=t_act, waits=[mi_tr])
                    mi_adchain = (t_act, t_act.n)
                    free["aux_a1"] = mi_adchain
            for hf in range(2):
                ts = 2 * (2 * q + hf) + hfo
                bank = qkv_banks[bk0 + hf]
                par = bk0 + hf
                if ot < 8:
                    ev = f3[:, 512 * hf:512 * hf + 512]
                    tt1 = f3[:, 1024 + 1024 * hf:1024 + 1024 * hf + 512]
                    ro = f3[:, 1024 + 1024 * hf + 512:1024 + 1024 * hf + 1024]
                    ob = a2[:, 8192 + 512 * hf:8192 + 512 * hf + 512]
                    E("dve", lambda e, d=ev, b=bank: e.tensor_copy(out=d, in_=b),
                      inc=t_dve, waits=[mi_mm, free.get(("ev", hf))])
                    mi_ev = (t_dve, t_dve.n)
                    free[("qb", par)] = mi_ev
                    cs_mi, cs_c, cs_s = csl[hf]
                    E("pool", lambda e, d=tt1, s_=ev, c_=cs_c: e.tensor_mul(d, s_, c_),
                      inc=t_pool, waits=[mi_ev, cs_mi, free.get(("t1", hf))])
                    E("pool", lambda e, d=ro, s_=ev: e.tensor_copy(out=d[0:64, :], in_=s_[64:128, :]))
                    E("pool", lambda e, d=ro, s_=ev: e.tensor_copy(out=d[64:128, :], in_=s_[0:64, :]),
                      inc=t_pool)
                    mi_rot = (t_pool, t_pool.n)
                    free[("ev", hf)] = mi_rot
                    E("pool", lambda e, d=ro, s_=cs_s: e.tensor_mul(d, d, s_), inc=t_pool)
                    E("pool", lambda e, d=ob, a_=tt1, b_=ro: e.tensor_add(d, a_, b_),
                      inc=t_pool, waits=[free.get(("ob", hf))])
                    mi_ob = (t_pool, t_pool.n)
                    free[("t1", hf)] = mi_ob
                    free[("cs", q % 2, hf)] = mi_ob
                    dst = q_d if ot < 4 else k_d
                    E("act", lambda e, s_=ob, d=dst, r=128 * (ot % 4), t0=512 * ts:
                      e.dma_start(out=d[r:r + 128, t0:t0 + 512], in_=s_),
                      inc=Strk[hf], waits=[mi_ob])
                    free[("ob", hf)] = (Strk[hf], Strk[hf].n)
                else:
                    ev = f3[:, 512 * hf:512 * hf + 512]
                    vsb = a2[:, 9216 + 2048 * hf:9216 + 2048 * hf + 2048]
                    E("dve", lambda e, d=ev, b=bank: e.tensor_copy(out=d, in_=b),
                      inc=t_dve, waits=[mi_mm, free.get(("ev", hf))])
                    mi_ev = (t_dve, t_dve.n)
                    free[("qb", par)] = mi_ev
                    for j in range(4):
                        E("pe", lambda e, s_=ev, j_=j: e.transpose(
                            AUX[:, 128 * j_:128 * j_ + 128], s_[:, 128 * j_:128 * j_ + 128],
                            ident_sb[:, :]),
                          inc=t_pe if j == 3 else None,
                          waits=[mi_ev, mi_ident, free.get("aux_a1")] if j == 0 else [])
                    mi_tr = (t_pe, t_pe.n)
                    free[("ev", hf)] = mi_tr
                    for j in range(4):
                        E("act", lambda e, d=vsb, j_=j, c0=128 * (ot - 8):
                          e.copy(out=d[:, 512 * j_ + c0:512 * j_ + c0 + 128],
                                 in_=AUX[:, 128 * j_:128 * j_ + 128]),
                          inc=t_act if j == 3 else None,
                          waits=[mi_tr, free.get(("vsb", hf)) if j == 0 else None]
                          if j == 0 else [])
                    mi_vs = (t_act, t_act.n)
                    free["aux_a1"] = mi_vs
                    if ot == 11:
                        E("act", lambda e, s_=vsb, t_=ts:
                          e.dma_start(out=v_d[:, 2048 * t_:2048 * t_ + 2048], in_=s_),
                          inc=Strk[2 + hf], waits=[mi_vs])
                        free[("vsb", hf)] = (Strk[2 + hf], Strk[2 + hf].n)
    mi_a1_st = [(Strk[i], Strk[i].n) for i in range(4)]
    mi_a1 = mi_a1_st + [(t_pe, t_pe.n), (t_pool, t_pool.n), (t_dve, t_dve.n)]

    if debug_stage >= 2:
        for nm, src in (("q", q_d), ("k", k_d), ("v", v_d)):
            E("pool", lambda e, n=nm, s_=src: e.dma_start(out=dbg[n], in_=s_), inc=t_ld, waits=mi_a1)
        E("pool", lambda e: e.dma_start(out=dbg["ak"], in_=akT_sb[:, :]), inc=t_ld, waits=mi_a1)
        E("pool", lambda e: e.dma_start(out=dbg["av"], in_=av_sb[:, :]), inc=t_ld, waits=mi_a1)
        E("pool", lambda e: e.dma_start(out=dbg["wbf"], in_=wqkv_bf_d), inc=t_ld, waits=mi_a1)
    if debug_stage == 2:
        return _finish(nc, P, stack)

    E("sp", lambda e: e.dma_start(out=a1[:, :].rearrange("p (h t) -> p h t", h=4),
                                  in_=k_d.rearrange("(h p) t -> p h t", p=128)),
      inc=L[2], waits=mi_a1)
    mi_K = (L[2], L[2].n)
    E("sp", lambda e: e.dma_start(out=a2[:, :], in_=v_d),
      inc=L[3], waits=mi_a1)
    mi_V = (L[3], L[3].n)

    mi_sgrp = {}
    mi_pslot = {}
    mi_oband = {}
    mi_mfin = {}
    mi_expA = {}
    mi_expA2 = {}
    mi_fbc = {}
    mi_rbc = {}
    mi_ybf = {}
    mi_oev = {}
    deferred = []

    def flush_defer():
        for eng, fn, inc, waits, fut in deferred:
            mi = E(eng, fn, inc=inc, waits=waits)
            if fut is not None:
                fut.trk, fut.val = mi
        deferred.clear()

    def defer(eng, fn, inc, waits, fut=None):
        deferred.append((eng, fn, inc, waits, fut))

    ystore_futs = []
    for c in range(NTS):
        sl = c % 2
        qs = a3[:, 2048 * sl:2048 * sl + 2048]
        E("sp", lambda e, d=qs, c_=c:
          e.dma_start(out=d.rearrange("p (h t) -> p h t", h=4),
                      in_=q_d[:, 512 * c_:512 * c_ + 512].rearrange("(h p) t -> p h t", p=128)),
          inc=L[4 + sl], waits=mi_a1_st[:2] + [free.get(("qs", sl))])
        mi_q = (L[4 + sl], L[4 + sl].n)
        nkt = 4 * (c + 1)
        npair = nkt // 2
        for h in range(HPC):
            ph = h % 2
            qv = qs[:, 512 * h:512 * h + 512]
            oband = OA if ph == 0 else OB
            sbank = SUMS if ph == 0 else AUX

            def emit_S(p_, first=False):
                sgrp = SA if p_ % 2 == 0 else SB
                for jj in range(2):
                    j = 2 * p_ + jj
                    E("pe", lambda e, g=sgrp, h_=h, j_=j, o=512 * jj, q_=qv:
                      e.matmul(g[:, o:o + 512],
                               a1[:, 8192 * h_ + 128 * j_:8192 * h_ + 128 * j_ + 128],
                               q_, start=True, stop=True),
                      inc=t_pe if jj == 1 else None,
                      waits=[mi_K, mi_q, mi_sgrp.get(p_ % 2)] if jj == 0 else [])
                return (t_pe, t_pe.n)

            mi_s = {}
            mi_s[0] = emit_S(0)
            if npair > 1:
                mi_s[1] = emit_S(1)
            dp = min(2, npair - 1)
            for p_ in range(npair):
                sgrp = SA if p_ % 2 == 0 else SB
                mi_mask = None
                for jj in range(2):
                    o = 2 * p_ + jj - 4 * c
                    if 0 <= o < 4:
                        E("dve", lambda e, g=sgrp, off=512 * jj, o_=o:
                          e.tensor_add(g[:, off:off + 512], g[:, off:off + 512],
                                       mask_sb[:, 512 * o_:512 * o_ + 512]),
                          inc=t_dve, waits=[mi_s[p_], mi_masks])
                        mi_mask = (t_dve, t_dve.n)
                psl = p_ % 3
                pb = a3[:, 4096 + 1024 * psl:4096 + 1024 * psl + 1024]
                E("act", lambda e, d=pb, g=sgrp:
                  e.activation(out=d, in_=g[:, :], func=AF.Exp, scale=SCALE_MAIN),
                  inc=t_act, waits=[mi_s[p_], mi_mask, mi_pslot.get(psl)])
                mi_e = (t_act, t_act.n)
                mi_sgrp[p_ % 2] = mi_e
                if p_ + 2 < npair:
                    mi_s[p_ + 2] = emit_S(p_ + 2)
                if p_ == dp:
                    flush_defer()
                for jj in range(2):
                    j = 2 * p_ + jj
                    E("pe", lambda e, ob=oband, j_=j, h_=h, p__=pb:
                      e.matmul(ob[:, :], a2[:, 512 * j_ + 128 * h_:512 * j_ + 128 * h_ + 128],
                               p__[:, 512 * (j_ % 2):512 * (j_ % 2) + 512],
                               start=(j_ == 0), stop=False),
                      waits=([mi_e, mi_V] + ([mi_oband.get(ph)] if j == 0 else []))
                      if jj == 0 else [])
                    E("pe", lambda e, j_=j, p__=pb, b=sbank, nk=nkt:
                      e.matmul(b[0:1, :], ones_bf[:, :],
                               p__[:, 512 * (j_ % 2):512 * (j_ % 2) + 512],
                               start=(j_ == 0), stop=(j_ == nk - 1)),
                      inc=t_pe if jj == 1 else None,
                      waits=[mi_mfin.get(ph)] if j == 0 else [])
                mi_pslot[psl] = (t_pe, t_pe.n)
            if deferred:
                flush_defer()
            mi_kloop = (t_pe, t_pe.n)
            E("pe", lambda e, h_=h, q_=qv, b=sbank:
              e.matmul(b[32:42, :], akT_sb[:, 10 * h_:10 * h_ + 10], q_, start=True, stop=True),
              inc=t_pe, waits=[mi_expA.get(ph)])
            mi_as = (t_pe, t_pe.n)
            expA = a3[:, 7168 + 512 * ph:7168 + 512 * ph + 512]
            E("act", lambda e, d=expA, b=sbank:
              e.activation(out=d[0:10, :], in_=b[32:42, :], func=AF.Exp, scale=SCALE_AD),
              inc=t_act, waits=[mi_as, mi_expA2.get(ph)])
            mi_ea = (t_act, t_act.n)
            mi_expA[ph] = mi_ea
            E("pe", lambda e, s_=expA, b=sbank:
              e.matmul(b[64:65, :], ones_bf[0:10, :], s_[0:10, :], start=True, stop=True),
              inc=t_pe, waits=[mi_ea])
            mi_asum = (t_pe, t_pe.n)
            lnr = srow_sb[0:1, 512 * ph:512 * ph + 512]
            E("act", lambda e, d=lnr, b=sbank:
              e.activation(out=d, in_=b[64:65, :], func=AF.Ln),
              inc=t_act, waits=[mi_asum])
            E("act", lambda e, d=lnr:
              e.activation(out=d, in_=d, func=AF.Exp, scale=-1.0),
              inc=t_act)
            mi_ras = (t_act, t_act.n)
            frow = a3[0:1, 14336 + 512 * ph:14336 + 512 * ph + 512]
            E("dve", lambda e, d=frow, r=lnr, b=sbank:
              e.scalar_tensor_tensor(d, b[0:1, :], g_sb[0:1, 0:1], r,
                                     op0=ALU.mult, op1=ALU.mult),
              inc=t_dve, waits=[mi_ras, mi_gat, mi_fbc.get(ph)])
            mi_f = (t_dve, t_dve.n)
            E("sp", lambda e, s_=frow, p=ph: e.dma_start(out=bc_d[p:p + 1, :], in_=s_),
              inc=t_fr[ph], waits=[mi_f, mi_fbc.get(ph)])
            mi_fd = (t_fr[ph], t_fr[ph].n)
            fbc = a3[0:10, 9216 + 512 * ph:9216 + 512 * ph + 512]
            E("sp", lambda e, d=fbc, p=ph:
              e.dma_start(out=d, in_=bc_d[p:p + 1, :].broadcast_to([10, 512])),
              inc=t_fb[ph], waits=[mi_fd, mi_expA2.get(ph)])
            mi_fb = (t_fb[ph], t_fb[ph].n)
            mi_fbc[ph] = mi_fb
            expA2 = a3[0:10, 8192 + 512 * ph:8192 + 512 * ph + 512]
            E("pool", lambda e, d=expA2, a=expA, f_=fbc:
              e.tensor_mul(d, a[0:10, :], f_),
              inc=t_pool, waits=[mi_fb, mi_ea])
            mi_e2 = (t_pool, t_pool.n)
            mi_expA2[ph] = mi_e2
            rrow = a3[0:1, 15360 + 512 * ph:15360 + 512 * ph + 512]
            lnm = srow_sb[0:1, 512 * ph:512 * ph + 512]
            E("act", lambda e, d=lnm, b=sbank:
              e.activation(out=d, in_=b[0:1, :], func=AF.Ln),
              inc=t_act, waits=[mi_f])
            E("act", lambda e, d=rrow, s_=lnm:
              e.activation(out=d, in_=s_, func=AF.Exp, scale=-1.0),
              inc=t_act)
            mi_r = (t_act, t_act.n)
            mi_mfin[ph] = mi_r
            E("sp", lambda e, s_=rrow, p=ph: e.dma_start(out=bc_d[2 + p:3 + p, :], in_=s_),
              inc=t_rr[ph], waits=[mi_r, mi_rbc.get(ph)])
            mi_rd = (t_rr[ph], t_rr[ph].n)
            rbc = a3[:, 10240 + 512 * ph:10240 + 512 * ph + 512]
            E("sp", lambda e, d=rbc, p=ph:
              e.dma_start(out=d, in_=bc_d[2 + p:3 + p, :].broadcast_to([128, 512])),
              inc=t_rb[ph], waits=[mi_rd, mi_rbc.get(ph)])
            mi_rb = (t_rb[ph], t_rb[ph].n)
            fut_ao = _Fut()
            defer("pe",
                  (lambda h_, s_, ob: lambda e:
                   e.matmul(ob[:, :], av_sb[0:10, 128 * h_:128 * h_ + 128], s_[0:10, :],
                            start=False, stop=True))(h, expA2, oband),
                  t_pe, [mi_e2], fut_ao)
            oev = f3[:, 6144 + 512 * ph:6144 + 512 * ph + 512]
            fut_oe = _Fut()
            defer("dve",
                  (lambda d, ob: lambda e: e.tensor_copy(out=d, in_=ob[:, :]))(oev, oband),
                  t_dve, [fut_ao, mi_oev.get(ph)], fut_oe)
            ybf = a3[:, 11264 + 512 * ph:11264 + 512 * ph + 512]
            fut_ytt = _Fut()
            defer("pool",
                  (lambda d, s_, r_: lambda e: e.tensor_mul(d, s_, r_))(ybf, oev, rbc),
                  t_pool, [mi_rb, fut_oe, mi_ybf.get(ph)], fut_ytt)
            fut_yst = _Fut()
            defer("sp",
                  (lambda s_, h_, c_: lambda e:
                   e.dma_start(out=yown_d[512 * (c_ // 2) + 128 * h_:
                                          512 * (c_ // 2) + 128 * h_ + 128,
                                          512 * (c_ % 2):512 * (c_ % 2) + 512],
                               in_=s_))(ybf, h, c),
                  t_ys[ph], [fut_ytt], fut_yst)
            ystore_futs.append(fut_yst)
            mi_oband[ph] = fut_ytt
            mi_ybf[ph] = fut_yst
            mi_oev[ph] = fut_ytt
            mi_rbc[ph] = fut_ytt
        free[("qs", sl)] = (t_pe, t_pe.n)
        if c % 2 == 1:
            k = c // 2
            defer("pool",
                  (lambda k_: lambda e: e.collective_compute(
                      "AllGather", ALU.bypass, replica_groups=RG,
                      ins=[yown_d[512 * k_:512 * k_ + 512, :].opt()],
                      outs=[yall_d[4096 * k_:4096 * k_ + 4096, :].opt()]))(k),
                  t_cc, list(ystore_futs))
            ystore_futs = []
    flush_defer()
    mi_ag_y = (t_cc, t_cc.n)

    if debug_stage >= 3:
        E("pool", lambda e: e.dma_start(out=dbg["yall"], in_=yall_d),
          inc=t_ld, waits=[mi_ag_y])
    if debug_stage == 3:
        return _finish(nc, P, stack)

    zbanks = [[OA[:, :], OB[:, :]], [SA[:, 0:512], SA[:, 512:1024]]]
    for th in range(2):
        def _ldy(e, th_=th):
            pid = e.partition_id()
            return e.dma_start(out=a1[:, 16384 * th_:16384 * th_ + 16384]
                               .rearrange("p (cc t) -> p cc t", t=512),
                               in_=yall_d.rearrange("(R p) t -> p R t", p=128)
                               [:, bass.ts(pid, 32), 512 * th_:512 * th_ + 512])
        E("sp", _ldy, inc=L[th], waits=[mi_ag_y])
        free[("yld", th)] = (L[th], L[th].n)
    mi_y = [free[("yld", 0)], free[("yld", 1)]]
    for ot in range(NCC):
        wsl = ot % 2
        wf = f2[:, 8192 + 4096 * wsl:8192 + 4096 * wsl + 4096]
        wb = a3[:, 4096 * wsl:4096 * wsl + 4096]
        E("sp", lambda e, d=wf, o=ot:
          e.dma_start(out=d, in_=wprojT[:, 4096 * o:4096 * o + 4096]),
          inc=L[6 + wsl], waits=[free.get(("cwf", wsl)),
                                 mi_ag_y if ot < 2 else None,
                                 mi_V if ot < 2 else None])
        mi_wf = (L[6 + wsl], L[6 + wsl].n)
        E("dve", lambda e, d=wb, s_=wf: e.tensor_copy(out=d, in_=s_),
          inc=t_dve, waits=[mi_wf, free.get(("cwb", wsl))])
        mi_wb = (t_dve, t_dve.n)
        free[("cwf", wsl)] = mi_wb
        bks = zbanks[ot % 2]
        for cc in range(NCC):
            for th in range(2):
                E("pe", lambda e, b=bks[th], w=wb, cc_=cc, t_=th,
                  st=(cc == 0), sp_=(cc == NCC - 1):
                  e.matmul(b, w[:, 128 * cc_:128 * cc_ + 128],
                           a1[:, 16384 * t_ + 512 * cc_:16384 * t_ + 512 * cc_ + 512],
                           start=st, stop=sp_),
                  inc=t_pe if (cc == NCC - 1 and th == 1) else None,
                  waits=[mi_wb, mi_y[0], mi_y[1],
                         free.get(("zb", ot % 2))] if (cc == 0 and th == 0) else [])
        mi_mm = (t_pe, t_pe.n)
        free[("cwb", wsl)] = mi_mm
        free["yh"] = mi_mm
        for th in range(2):
            par = 2 * (ot % 2) + th
            xsl = f3[:, 4096 + 512 * par:4096 + 512 * par + 512]
            x2t = f3[:, 6144 + 512 * par:6144 + 512 * par + 512]
            x2b = a2[:, 512 * par:512 * par + 512]
            E("sp", lambda e, d=xsl, o=ot, t_=th:
              e.dma_start(out=d, in_=xT[128 * o:128 * o + 128, 512 * t_:512 * t_ + 512]),
              inc=L[2 + par], waits=[free.get(("cx", par))])
            mi_x = (L[2 + par], L[2 + par].n)
            E("dve", lambda e, d=x2t, b=bks[th], x=xsl: e.tensor_add(d, b, x),
              inc=t_dve, waits=[mi_mm, mi_x, free.get(("cx2", par)), free.get(("cx2a", par))])
            mi_x2 = (t_dve, t_dve.n)
            free[("cx", par)] = mi_x2
            E("act", lambda e, s_=x2t, o=ot, t_=th:
              e.dma_start(out=x2fm_d[:, 1024 * o + 512 * t_:1024 * o + 512 * t_ + 512], in_=s_),
              inc=Strk[1 + par], waits=[mi_x2])
            free[("cx2a", par)] = (Strk[1 + par], Strk[1 + par].n)
            E("pool", lambda e, d=x2b, s_=x2t: e.tensor_copy(out=d, in_=s_),
              inc=t_pool, waits=[mi_x2, free.get(("cxb", par))])
            mi_xb = (t_pool, t_pool.n)
            free[("cx2", par)] = mi_xb
            sqb = a2[:, 2048 + 512 * th:2048 + 512 * th + 512]
            E("act", lambda e, d=sqb, s_=x2b: e.activation(out=d, in_=s_, func=AF.Square),
              inc=t_act, waits=[mi_xb, free.get(("csq", th))])
            mi_sq = (t_act, t_act.n)
            st, sp_ = (ot == 0), (ot == NCC - 1)
            sbank = SUMS if th == 0 else AUX
            E("pe", lambda e, b=sbank, r=x2b, st_=st, sp__=sp_:
              e.matmul(b[0:1, :], ones_bf[:, :], r, start=st_, stop=sp__),
              waits=[mi_xb])
            E("pe", lambda e, b=sbank, r=sqb, st_=st, sp__=sp_:
              e.matmul(b[32:33, :], ones_bf[:, :], r, start=st_, stop=sp__),
              inc=t_pe, waits=[mi_sq])
            free[("cxb", par)] = (t_pe, t_pe.n)
            free[("csq", th)] = (t_pe, t_pe.n)
        free[("zb", ot % 2)] = (t_dve, t_dve.n)
    mi_cstats = (t_pe, t_pe.n)
    mi_x2fm = [(Strk[i], Strk[i].n) for i in range(1, 5)]

    mi_rows2 = ln_rows(mi_cstats, 2048)

    bcm = f3[:, 0:1024]
    bcr = f3[:, 1024:2048]
    mi_bc = None
    for h in range(2):
        m_h = mrow_sb[0:1, 2048 + 512 * h:2048 + 512 * h + 512]
        r_h = mrow_sb[0:1, 3072 + 512 * h:3072 + 512 * h + 512]
        E("pe", lambda e, r=m_h: e.matmul(OA[:, :], onesr_f[:, :], r, start=True, stop=True),
          inc=t_pe, waits=[mi_rows2, free.get(("zb", 0)), free.get(("zb", 1))])
        mi1 = (t_pe, t_pe.n)
        E("pe", lambda e, r=r_h: e.matmul(OB[:, :], onesr_f[:, :], r, start=True, stop=True),
          inc=t_pe)
        mi2 = (t_pe, t_pe.n)
        E("dve", lambda e, d=bcm, h_=h: e.tensor_copy(out=d[:, 512 * h_:512 * h_ + 512],
                                                      in_=OA[:, :]),
          inc=t_dve, waits=[mi1])
        E("dve", lambda e, d=bcr, h_=h: e.tensor_copy(out=d[:, 512 * h_:512 * h_ + 512],
                                                      in_=OB[:, :]),
          inc=t_dve, waits=[mi2])
        mi_bc = (t_dve, t_dve.n)
        free[("zb", 0)] = mi_bc
        free[("zb", 1)] = mi_bc

    for cc in range(NCC):
        sl = cc % 2
        x2s = f3[:, 4096 + 1024 * sl:4096 + 1024 * sl + 1024]
        tmp = f3[:, 2048 + 1024 * sl:2048 + 1024 * sl + 1024]
        E("sp", lambda e, d=x2s, c_=cc:
          e.dma_start(out=d, in_=x2fm_d[:, 1024 * c_:1024 * c_ + 1024]),
          inc=L[2 + sl], waits=(mi_x2fm + [mi_cstats] if cc < 2 else []) + [free.get(("h2x", sl))])
        mi_x = (L[2 + sl], L[2 + sl].n)
        E("pool", lambda e, d=tmp, s=x2s, m=bcm: e.tensor_sub(d, s, m),
          inc=t_pool, waits=[mi_x, mi_bc, free.get(("h2t", sl))])
        mi_t = (t_pool, t_pool.n)
        free[("h2x", sl)] = mi_t
        h2v = a2[:, 1024 * cc:1024 * cc + 1024]
        E("pool", lambda e, d=h2v, s=tmp, r=bcr: e.tensor_mul(d, s, r),
          inc=t_pool, waits=[mi_V if cc == 0 else None])
        mi_h = (t_pool, t_pool.n)
        free[("h2t", sl)] = mi_h
        E("dve", lambda e, d=hsum_sb[:, cc:cc + 1], s=h2v: e.reduce_sum(d, s, axis=AX),
          inc=t_dve, waits=[mi_h])
    mi_h2 = (t_pool, t_pool.n)
    mi_hsum = (t_dve, t_dve.n)

    for tt in range(8):
        sl = tt % 2
        xsl = f1[:, 4096 * sl:4096 * sl + 4096]
        stg = f1[:, 8192 + 4096 * sl:8192 + 4096 * sl + 4096]
        E("sp", lambda e, d=xsl, t_=tt:
          e.dma_start(out=d.rearrange("p (cc t) -> p cc t", t=128),
                      in_=x2fm_d.rearrange("p (cc t) -> p cc t", t=1024)
                      [:, 0:32, 128 * t_:128 * t_ + 128]),
          inc=L[8 + sl], waits=mi_x2fm + [free.get(("xt", sl)),
                            free.get("yh") if tt < 2 else None])
        mi_x = (L[8 + sl], L[8 + sl].n)
        for g in range(8):
            for j in range(4):
                cc = 4 * g + j
                E("pe", lambda e, s=xsl, c_=cc, j_=j:
                  e.transpose(AUX[:, 128 * j_:128 * j_ + 128],
                              s[:, 128 * c_:128 * c_ + 128], ident_sb[:, :]),
                  inc=t_pe if j == 3 else None,
                  waits=[mi_x, mi_rows2, free.get("xaux")] if j == 0 else [])
            mi_tr = (t_pe, t_pe.n)
            E("act", lambda e, d=stg, g_=g: e.copy(out=d[:, 512 * g_:512 * g_ + 512],
                                                   in_=AUX[:, :]),
              inc=t_act, waits=[mi_tr, free.get(("xstg", sl)) if g == 0 else None])
            free["xaux"] = (t_act, t_act.n)
        mi_stg = (t_act, t_act.n)
        free[("xt", sl)] = mi_stg
        E("act", lambda e, s=stg, t_=tt:
          e.dma_start(out=x2tm_d[128 * t_:128 * t_ + 128, :], in_=s),
          inc=Strk[3 + sl], waits=[mi_stg])
        free[("xstg", sl)] = (Strk[3 + sl], Strk[3 + sl].n)
    mi_x2tm = [(Strk[3], Strk[3].n), (Strk[4], Strk[4].n)]

    E("act", lambda e: e.dma_start(out=hsin_d, in_=hsum_sb[:, :]),
      inc=Strk[5], waits=[mi_hsum])
    mi_hs = (Strk[5], Strk[5].n)
    E("pool", lambda e: e.collective_compute(
        "AllReduce", ALU.add, replica_groups=RG, ins=[hsin_d.opt()], outs=[hsout_d.opt()]),
      inc=t_cc, waits=[mi_hs])
    mi_ar = (t_cc, t_cc.n)
    mcol = f3[:, 6144:6176]
    E("sp", lambda e: e.dma_start(out=mcol, in_=hsout_d), inc=L[1], waits=[mi_ar])
    mi_m = (L[1], L[1].n)
    E("dve", lambda e: e.tensor_scalar_mul(mcol, mcol, 1.0 / T), inc=t_dve, waits=[mi_m])
    mi_m2 = (t_dve, t_dve.n)
    wqs = f1[:, 0:4096]
    E("sp", lambda e: e.dma_start(out=wqs, in_=wqT),
      inc=L[3], waits=[free.get(("xt", 0))])
    mi_wq = (L[3], L[3].n)
    for cc in range(NCC):
        E("pe", lambda e, c_=cc, st=(cc == 0), sp_=(cc == NCC - 1):
          e.matmul(AUX[0:1, 0:128], mcol[:, c_:c_ + 1], wqs[:, 128 * c_:128 * c_ + 128],
                   start=st, stop=sp_),
          inc=t_pe if cc == NCC - 1 else None,
          waits=[mi_m2, mi_wq, free.get("xaux"), (t_act, P.tracks["act"].n)] if cc == 0 else [])
    mi_qmm = (t_pe, t_pe.n)
    qrow = f3[:, 6176:6304]
    E("dve", lambda e: e.tensor_copy(out=qrow[0:1, :], in_=AUX[0:1, 0:128]),
      inc=t_dve, waits=[mi_qmm])
    mi_qr = (t_dve, t_dve.n)
    E("pe", lambda e: e.transpose(AUX[:, 256:257], qrow[0:1, :], ident_sb[0:1, 0:1]),
      inc=t_pe, waits=[mi_qr])
    mi_qt = (t_pe, t_pe.n)
    E("dve", lambda e: e.tensor_copy(out=qT_sb[:, :], in_=AUX[:, 256:257]),
      inc=t_dve, waits=[mi_qt])
    mi_qT = (t_dve, t_dve.n)
    E("pe", lambda e: e.matmul(AUX[0:1, 0:8], qT_sb[:, :], keys_sb[:, :],
                               start=True, stop=True),
      inc=t_pe, waits=[mi_qT, mi_keys])
    mi_sc = (t_pe, t_pe.n)
    e8 = f3[:, 6304:6312]
    rt = f3[:, 6312:6320]
    E("dve", lambda e: e.reduce_max(rt[0:1, 0:1], AUX[0:1, 0:8], axis=AX, negate=True),
      inc=t_dve, waits=[mi_sc])
    E("dve", lambda e: e.tensor_scalar_mul(rt[0:1, 0:1], rt[0:1, 0:1], SCALE_MAIN),
      inc=t_dve)
    mi_ng = (t_dve, t_dve.n)
    E("act", lambda e: e.activation(out=e8[0:1, :], in_=AUX[0:1, 0:8], func=AF.Exp,
                                    scale=SCALE_MAIN, bias=rt[0:1, 0:1]),
      inc=t_act, waits=[mi_ng])
    mi_e8 = (t_act, t_act.n)
    E("dve", lambda e: e.reduce_sum(rt[0:1, 1:2], e8[0:1, :], axis=AX),
      inc=t_dve, waits=[mi_e8])
    E("dve", lambda e: e.reciprocal(rt[0:1, 2:3], rt[0:1, 1:2]), inc=t_dve)
    E("dve", lambda e: e.tensor_scalar_mul(e8[0:1, :], e8[0:1, :], rt[0:1, 2:3]),
      inc=t_dve)
    mi_cf = (t_dve, t_dve.n)
    E("pe", lambda e: e.matmul(AUX[:, 0:8], onesr_f[:, :], e8[0:1, :], start=True, stop=True),
      inc=t_pe, waits=[mi_cf])
    mi_cemm = (t_pe, t_pe.n)
    E("dve", lambda e: e.tensor_copy(out=ce_sb[:, :], in_=AUX[:, 0:8]),
      inc=t_dve, waits=[mi_cemm])
    mi_ce = (t_dve, t_dve.n)

    if debug_stage >= 4:
        E("pool", lambda e: e.dma_start(out=dbg["x2"], in_=x2tm_d), inc=t_ld,
          waits=mi_x2tm)
        E("act", lambda e: e.dma_start(
            out=dbg["h2"].rearrange("(cc p) t -> p cc t", p=128),
            in_=a2[:, :].rearrange("p (cc t) -> p cc t", t=1024)), inc=t_ld2,
          waits=[mi_h2])
        E("act", lambda e: e.dma_start(out=dbg["coeffs"], in_=e8[0:1, :]),
          inc=t_ld2, waits=[mi_cf])
    mi_dbg4 = (t_ld2, t_ld2.n) if debug_stage >= 4 else None
    if debug_stage == 4:
        return _finish(nc, P, stack)

    gpack = a3[0:64, 0:8192]
    gbanks = [OA, OB, SUMS, AUX]
    mi_gev = {}
    for e_ in range(8):
        sl = e_ % 2
        dwf = f1[:, 2048 * sl:2048 * sl + 2048]
        dwb = a3[:, 8192 + 2048 * sl:8192 + 2048 * sl + 2048]
        E("sp", lambda e, d=dwf, i=e_:
          e.dma_start(out=d, in_=downT[:, 2048 * i:2048 * i + 2048]),
          inc=L[4 + sl], waits=[free.get(("df", sl)), mi_qmm if e_ < 2 else None])
        mi_w = (L[4 + sl], L[4 + sl].n)
        E("pool", lambda e, d=dwb, s=dwf: e.tensor_copy(out=d, in_=s),
          inc=t_pool, waits=[mi_w, free.get(("db", sl))])
        mi_wb = (t_pool, t_pool.n)
        free[("df", sl)] = mi_wb
        for th in range(2):
            gi = (e_ * 2 + th) % 4
            bank = gbanks[gi]
            E_waits = [mi_wb, mi_h2, mi_gev.get(gi),
                       mi_ce if e_ == 0 else None]
            for cc in range(NCC):
                E("pe", lambda e, b=bank, w=dwb, cc_=cc, t_=th,
                  st=(cc == 0), sp_=(cc == NCC - 1):
                  e.matmul(b[0:64, :], w[:, 64 * cc_:64 * cc_ + 64],
                           a2[:, 1024 * cc_ + 512 * t_:1024 * cc_ + 512 * t_ + 512],
                           start=st, stop=sp_),
                  inc=t_pe if cc == NCC - 1 else None,
                  waits=E_waits if cc == 0 else [])
            mi_mm = (t_pe, t_pe.n)
            E("dve", lambda e, b=bank, i=e_, t_=th:
              e.tensor_scalar_mul(
                  gpack[0:64, 1024 * i + 512 * t_:1024 * i + 512 * t_ + 512],
                  b[0:64, :], ce_sb[0:64, i:i + 1]),
              inc=t_dve, waits=[mi_mm, mi_ce])
            mi_gev[gi] = (t_dve, t_dve.n)
        free[("db", sl)] = (t_pe, t_pe.n)
    mi_g = (t_dve, t_dve.n)
    if debug_stage == 41:
        return _finish(nc, P, stack)

    grp = [(SA, 0), (SA, 512), (SB, 0), (SB, 512),
           (OA, 0), (OB, 0), (SUMS, 0), (AUX, 0)]
    mi_outev = {}
    n_ot = 2 if debug_stage in (42, 43, 44) else 16
    skip_up = (debug_stage == 43)
    wbfH = [a3[:, 8192:12288], a3[:, 12288:16384]]
    wffH = [f1[:, 8192:12288], f1[:, 12288:16384]]
    upb = a1[0:64, 10240:12288]
    upb2 = a1[0:64, 12288:14336]
    mi_wc = {}
    mi_wread = {}
    for ot in range(n_ot):
        for hfd in range(2):
            E("sp", lambda e, d=wffH[hfd], o=ot, f=hfd:
              e.dma_start(out=d, in_=worgT[:, 8192 * o + 4096 * f:
                                           8192 * o + 4096 * f + 4096]),
              inc=L[6 + hfd], waits=[mi_wc.get(hfd), mi_g if ot == 0 else None,
                                     mi_dbg4 if ot == 0 else None]
              + (mi_x2tm if ot == 0 else []))
            mi_wf = (L[6 + hfd], L[6 + hfd].n)
            E("dve", lambda e, d=wbfH[hfd], s=wffH[hfd]: e.tensor_copy(out=d, in_=s),
              inc=t_dve, waits=[mi_wf, mi_wread.get(hfd)])
            mi_wc[hfd] = (t_dve, t_dve.n)
        usl = ot % 2
        po = 256 * usl
        upx = upb if usl == 0 else upb2
        upff = f1[0:64, 2048 * usl:2048 * usl + 2048]
        E("sp", lambda e, d=upff, o=ot:
          e.dma_start(out=d, in_=upw[:, 2048 * o:2048 * o + 2048]),
          inc=L[8 + usl], waits=[free.get(("uf", usl)), mi_g if ot < 2 else None]
          + (mi_x2tm if ot < 2 else []))
        mi_uf = (L[8 + usl], L[8 + usl].n)
        E("pool", lambda e, d=upx, s=upff: e.tensor_copy(out=d, in_=s),
          inc=t_pool, waits=[mi_uf, free.get(("ub", usl))])
        mi_up = (t_pool, t_pool.n)
        free[("uf", usl)] = mi_up
        mi_tt = {}
        for cc in range(NCC):
            hfd = cc // 16
            for tt in range(8):
                bnk, boff = grp[tt]
                w = []
                if cc % 16 == 0 and tt == 0:
                    w.append(mi_wc[hfd])
                if cc == 0:
                    w += [mi_g, mi_outev.get((usl, tt))]
                last = (skip_up and cc == NCC - 1)
                E("pe", lambda e, b=bnk, o_=boff + po, cc_=cc, t_=tt, h_=hfd,
                  st=(cc == 0), sp_=last:
                  e.matmul(b[:, o_:o_ + 256],
                           a2[:, 1024 * cc_ + 128 * t_:1024 * cc_ + 128 * t_ + 128],
                           wbfH[h_][:, 256 * (cc_ % 16):256 * (cc_ % 16) + 256],
                           start=st, stop=sp_),
                  inc=t_pe if ((tt == 7 and cc % 16 == 15) or last) else None,
                  waits=w)
                if last:
                    mi_tt[tt] = (t_pe, t_pe.n)
            if cc % 16 == 15:
                mi_wread[cc // 16] = (t_pe, t_pe.n)
        if not skip_up:
            for e_ in range(8):
                for tt in range(8):
                    bnk, boff = grp[tt]
                    E("pe", lambda e, b=bnk, o_=boff + po, i=e_, t_=tt,
                      sp_=(e_ == 7), u=upx:
                      e.matmul(b[:, o_:o_ + 256],
                               gpack[0:64, 1024 * i + 128 * t_:1024 * i + 128 * t_ + 128],
                               u[0:64, 256 * i:256 * i + 256],
                               start=False, stop=sp_),
                      inc=t_pe if e_ == 7 else None,
                      waits=[mi_up] if (e_ == 0 and tt == 0) else [])
                    if e_ == 7:
                        mi_tt[tt] = (t_pe, t_pe.n)
        for tt in range(8):
            bnk, boff = grp[tt]
            off = boff + po
            mm_gate = mi_tt[tt]
            xsl = f1[:, 7168 + 256 * (tt % 2):7168 + 256 * (tt % 2) + 256]
            och = f1[:, 7680 + 256 * (tt % 2):7680 + 256 * (tt % 2) + 256]
            E("sp", lambda e, d=xsl, o=ot, t_=tt:
              e.dma_start(out=d, in_=x2tm_d[128 * t_:128 * t_ + 128, 256 * o:256 * o + 256]),
              inc=L[tt % 2], waits=(mi_x2tm if ot == 0 else []) + [free.get(("ox", tt % 2))])
            mi_x = (L[tt % 2], L[tt % 2].n)
            E("dve", lambda e, d=och, b=bnk, o_=off, x=xsl:
              e.tensor_add(d, b[:, o_:o_ + 256], x),
              inc=t_dve, waits=[mm_gate, mi_x, free.get(("oo", tt % 2))])
            mi_o = (t_dve, t_dve.n)
            free[("ox", tt % 2)] = mi_o
            mi_outev[(ot % 2, tt)] = mi_o
            E("act", lambda e, s=och, o=ot, t_=tt:
              e.dma_start(out=out[128 * t_:128 * t_ + 128, 256 * o:256 * o + 256], in_=s),
              inc=Strk[tt % 2], waits=[mi_o])
            free[("oo", tt % 2)] = (Strk[tt % 2], Strk[tt % 2].n)
        free[("ub", usl)] = (t_pe, t_pe.n)

    return _finish(nc, P, stack)


def _finish(nc, P, stack):
    semstack = contextlib.ExitStack()
    sems = {name: semstack.enter_context(nc.semaphore(f"sem_{name}"))
            for name in P.tracks}
    final = [(t, t.n) for t in P.tracks.values() if t.n > 0]
    P.emit("sp", lambda e: e.nop(), waits=final)
    with nc.Block() as block:
        P.realize(block, sems)
    semstack.close()
    stack.close()
    return nc



def _prep(x, w_attn, w_proj, adapter_wte, gating, W_org, lora_down, lora_up,
          lora_keys, Wq):
    x = np.asarray(x, dtype=np.float32).reshape(T, C)
    xT = np.ascontiguousarray(x.T)
    w_attn = np.asarray(w_attn, dtype=np.float32)
    theta = 1.0 / (10000.0 ** (np.arange(0, HD, 2, dtype=np.float32) / HD))
    idx = np.outer(np.arange(T, dtype=np.float32), theta).astype(np.float32)
    idx2 = np.concatenate([idx, idx], axis=-1)
    cosT = np.ascontiguousarray(np.cos(idx2).astype(np.float32).T)
    sinST = np.ascontiguousarray(np.sin(idx2).astype(np.float32).T).copy()
    sinST[0:64, :] *= -1.0
    mk = np.zeros((4, 128, S), dtype=np.float32)
    qq = np.arange(S)[None, :]
    kk = np.arange(128)[:, None]
    for i, o in enumerate((0, 128, 256, 384)):
        mk[i] = np.where(qq >= kk + o, 0.0, NEG).astype(np.float32)
    shared = {
        "wprojT": np.ascontiguousarray(
            np.asarray(w_proj, np.float32).T.reshape(32, 128, 32, 128)
            .transpose(1, 2, 0, 3).reshape(128, 32 * 4096)),
        "worgT": np.ascontiguousarray(
            np.asarray(W_org, np.float32).T.reshape(32, 128, 16, 256)
            .transpose(1, 2, 0, 3).reshape(128, 16 * 8192)),
        "downT": np.ascontiguousarray(
            np.asarray(lora_down, np.float32).reshape(8, 32, 128, 64)
            .transpose(2, 0, 1, 3).reshape(128, 8 * 2048)),
        "upw": np.ascontiguousarray(
            np.asarray(lora_up, np.float32).reshape(8, 64, 16, 256)
            .transpose(1, 2, 0, 3).reshape(64, 16 * 2048)),
        "wqT": np.ascontiguousarray(
            np.asarray(Wq, np.float32).T.reshape(32, 128, 128)
            .transpose(1, 0, 2).reshape(128, 4096)),
        "keysT": np.ascontiguousarray(np.asarray(lora_keys, np.float32).T),
        "adT": np.ascontiguousarray(np.asarray(adapter_wte, np.float32).T),
        "gat": np.asarray(gating, np.float32).reshape(1, 1),
        "cosT": cosT,
        "sinS": sinST,
        "masks": mk,
    }
    in_maps = []
    for r in range(NCORES):
        im = dict(shared)
        im["xT"] = np.ascontiguousarray(xT[:, TOWN * r:TOWN * (r + 1)])
        rows = np.concatenate([w_attn[HC * r:HC * (r + 1)],
                               w_attn[C + HC * r:C + HC * (r + 1)],
                               w_attn[2 * C + HC * r:2 * C + HC * (r + 1)]], axis=0)
        im["wqkvT"] = np.ascontiguousarray(
            rows.T.reshape(32, 128, 12, 128).transpose(1, 2, 0, 3).reshape(128, 12 * 4096))
        in_maps.append(im)
    return in_maps


def kernel(L0_attn_x, L0_attn_w_attn, L0_attn_w_proj, L0_attn_adapter_wte,
           L0_attn_gating, L0_ffn_W_org, L0_ffn_lora_down, L0_ffn_lora_up,
           L0_ffn_lora_keys, L0_ffn_Wq):
    in_maps = _prep(L0_attn_x, L0_attn_w_attn, L0_attn_w_proj, L0_attn_adapter_wte,
                    L0_attn_gating, L0_ffn_W_org, L0_ffn_lora_down, L0_ffn_lora_up,
                    L0_ffn_lora_keys, L0_ffn_Wq)
    nc = _build(DEBUG_STAGE)
    res = run_bass_kernel_spmd(nc, in_maps, core_ids=list(range(NCORES)))
    if DEBUG_STAGE:
        kernel.last_results = res
        if "out" not in res.results[0]:
            return None
    outs = [res.results[r]["out"] for r in range(NCORES)]
    return np.concatenate(outs, axis=0).reshape(1, T, C)


# version 10
# speedup vs baseline: 1.0971x; 1.0125x over previous
import contextlib
import math
import numpy as np
import concourse.bass as bass
import concourse.mybir as mybir
from concourse.bass_utils import run_bass_kernel_spmd

F32 = mybir.dt.float32
BF16 = mybir.dt.bfloat16
AF = mybir.ActivationFunctionType
ALU = mybir.AluOpType
AX = mybir.AxisListType.X

NCORES = 8
T = 8192
C = 4096
HD = 128
HPC = 4
HC = HPC * HD
TOWN = T // NCORES
S = 512
NTS = T // S
NCC = C // 128
SCALE_MAIN = 1.0 / math.sqrt(HD)
SCALE_AD = 1.0 / math.sqrt(C)
EPS = 1e-5
NEG = -1.0e30

DEBUG_STAGE = 0


class _Trk:
    __slots__ = ("name", "step", "n")

    def __init__(self, name, step):
        self.name, self.step, self.n = name, step, 0

    def bump(self):
        self.n += self.step
        return self.n


class _Fut:
    __slots__ = ("trk", "val")

    def __init__(self):
        self.trk, self.val = None, None


class _Prog:
    ENGS = ("pe", "act", "dve", "pool", "sp")

    def __init__(self):
        self.ops = {e: [] for e in self.ENGS}
        self.tracks = {}

    def trk(self, name, step=1):
        if name not in self.tracks:
            self.tracks[name] = _Trk(name, step)
        return self.tracks[name]

    SERIAL = ("dve", "act", "pool")

    def emit(self, eng, fn, inc=None, waits=()):
        ws = [w for w in waits
              if w is not None and (isinstance(w, _Fut) or w[1])]
        swait = None
        if eng in self.SERIAL:
            et = self.trk(eng)
            if inc is None:
                inc = et
            if et.n > 0:
                swait = (et, et.n)
        if swait is not None:
            ws.append(swait)
        self.ops[eng].append((fn, inc, inc.bump() if inc is not None else None, ws))
        return (inc, inc.n) if inc is not None else None

    def realize(self, block, sems):
        def runner(ename):
            def go(e):
                lw = {}
                for fn, trk, _val, waits in self.ops[ename]:
                    need = {}
                    for w in waits:
                        if isinstance(w, _Fut):
                            if w.trk is None or not w.val:
                                continue
                            t, v = w.trk, w.val
                        else:
                            t, v = w
                        if lw.get(t.name, 0) < v:
                            need[t.name] = max(need.get(t.name, 0), v)
                    for tn, v in need.items():
                        e.wait_ge(sems[tn], v)
                        lw[tn] = v
                    ins = fn(e)
                    if trk is not None:
                        ins.then_inc(sems[trk.name], trk.step)
            return go
        block.tensor(runner("pe"))
        block.scalar(runner("act"))
        block.vector(runner("dve"))
        block.gpsimd(runner("pool"))
        block.sync(runner("sp"))


def _build(debug_stage=0, shared_cc=True):
    nc = bass.Bass("TRN2", target_bir_lowering=False, num_devices=NCORES)
    RG = [list(range(NCORES))]

    xT = nc.dram_tensor("xT", [C, TOWN], F32, kind="ExternalInput").ap()
    wqkvT = nc.dram_tensor("wqkvT", [128, 12 * 4096], F32, kind="ExternalInput").ap()
    wprojT = nc.dram_tensor("wprojT", [128, 32 * 4096], F32, kind="ExternalInput").ap()
    worgT = nc.dram_tensor("worgT", [128, 16 * 8192], F32, kind="ExternalInput").ap()
    downT = nc.dram_tensor("downT", [128, 8 * 2048], F32, kind="ExternalInput").ap()
    upw = nc.dram_tensor("upw", [64, 16 * 2048], F32, kind="ExternalInput").ap()
    wqT = nc.dram_tensor("wqT", [128, 4096], F32, kind="ExternalInput").ap()
    keysT = nc.dram_tensor("keysT", [128, 8], F32, kind="ExternalInput").ap()
    adT = nc.dram_tensor("adT", [C, 10], F32, kind="ExternalInput").ap()
    gat = nc.dram_tensor("gat", [1, 1], F32, kind="ExternalInput").ap()
    cosT = nc.dram_tensor("cosT", [128, T], F32, kind="ExternalInput").ap()
    sinS = nc.dram_tensor("sinS", [128, T], F32, kind="ExternalInput").ap()
    masks = nc.dram_tensor("masks", [4, 128, S], F32, kind="ExternalInput").ap()
    out = nc.dram_tensor("out", [TOWN, C], F32, kind="ExternalOutput").ap()

    h1own_d = nc.dram_tensor("h1own_d", [256, 16384], BF16, kind="Internal").ap()
    h1all_d = nc.dram_tensor("h1all_d", [2048, 16384], BF16, kind="Internal", addr_space="Shared" if shared_cc else "Local").ap()
    wqkv_bf_d = nc.dram_tensor("wqkv_bf_d", [128, 12 * 4096], BF16, kind="Internal").ap()
    q_d = nc.dram_tensor("q_d", [HC, T], BF16, kind="Internal").ap()
    k_d = nc.dram_tensor("k_d", [HC, T], BF16, kind="Internal").ap()
    v_d = nc.dram_tensor("v_d", [128, 32768], BF16, kind="Internal").ap()
    yown_d = nc.dram_tensor("yown_d", [8 * HC, 1024], BF16, kind="Internal").ap()
    yall_d = nc.dram_tensor("yall_d", [8 * NCORES * HC, 1024], BF16, kind="Internal", addr_space="Shared" if shared_cc else "Local").ap()
    x2fm_d = nc.dram_tensor("x2fm_d", [128, 32768], F32, kind="Internal").ap()
    x2tm_d = nc.dram_tensor("x2tm_d", [TOWN, C], F32, kind="Internal").ap()
    bc_d = nc.dram_tensor("bc_d", [4, 512], BF16, kind="Internal").ap()
    hsin_d = nc.dram_tensor("hsin_d", [128, NCC], F32, kind="Internal").ap()
    hsout_d = nc.dram_tensor("hsout_d", [128, NCC], F32, kind="Internal").ap()

    dbg = {}
    if debug_stage >= 1:
        dbg["h1all"] = nc.dram_tensor("dbg_h1all", [2048, 16384], BF16, kind="ExternalOutput").ap()
    if debug_stage >= 2:
        dbg["q"] = nc.dram_tensor("dbg_q", [HC, T], BF16, kind="ExternalOutput").ap()
        dbg["k"] = nc.dram_tensor("dbg_k", [HC, T], BF16, kind="ExternalOutput").ap()
        dbg["v"] = nc.dram_tensor("dbg_v", [128, 32768], BF16, kind="ExternalOutput").ap()
        dbg["ak"] = nc.dram_tensor("dbg_ak", [128, 40], BF16, kind="ExternalOutput").ap()
        dbg["av"] = nc.dram_tensor("dbg_av", [16, 512], BF16, kind="ExternalOutput").ap()
        dbg["wbf"] = nc.dram_tensor("dbg_wbf", [C, 3 * HC], BF16, kind="ExternalOutput").ap()
    if debug_stage >= 3:
        dbg["yall"] = nc.dram_tensor("dbg_yall", [8 * NCORES * HC, 1024], BF16, kind="ExternalOutput").ap()
        dbg["P0"] = nc.dram_tensor("dbg_P0", [128, 1024], BF16, kind="ExternalOutput").ap()
        dbg["rows"] = nc.dram_tensor("dbg_rows", [8, 512], F32, kind="ExternalOutput").ap()
        dbg["expA"] = nc.dram_tensor("dbg_expA", [16, 512], BF16, kind="ExternalOutput").ap()
        dbg["expA2"] = nc.dram_tensor("dbg_expA2", [16, 512], BF16, kind="ExternalOutput").ap()
        dbg["oev"] = nc.dram_tensor("dbg_oev", [128, 512], F32, kind="ExternalOutput").ap()
    if debug_stage >= 4:
        dbg["x2"] = nc.dram_tensor("dbg_x2", [TOWN, C], F32, kind="ExternalOutput").ap()
        dbg["h2"] = nc.dram_tensor("dbg_h2", [C, TOWN], BF16, kind="ExternalOutput").ap()
        dbg["coeffs"] = nc.dram_tensor("dbg_coeffs", [1, 8], F32, kind="ExternalOutput").ap()

    ident_d = nc.inline_tensor(np.eye(128, dtype=np.float32), name="ident_c").ap()

    P = _Prog()
    t_pe = P.trk("pe")
    t_act = P.trk("act")
    t_dve = P.trk("dve")
    t_pool = P.trk("pool")
    L = {i: P.trk(f"L{i}", 16) for i in range(10)}
    Strk = {i: P.trk(f"S{i}", 16) for i in range(6)}
    t_li = P.trk("Li", 16)
    t_fr = [P.trk("FR0", 16), P.trk("FR1", 16)]
    t_rr = [P.trk("RR0", 16), P.trk("RR1", 16)]
    t_fb = [P.trk("FB0", 16), P.trk("FB1", 16)]
    t_rb = [P.trk("RB0", 16), P.trk("RB1", 16)]
    t_ys = [P.trk("YS0", 16), P.trk("YS1", 16)]
    t_ld = P.trk("Ld", 16)
    t_ld2 = P.trk("Ld2", 16)
    t_cc = P.trk("cc", 1)

    stack = contextlib.ExitStack()
    ar1 = stack.enter_context(nc.sbuf_tensor("ar1", [128, 32768], BF16))
    ar2 = stack.enter_context(nc.sbuf_tensor("ar2", [128, 32768], BF16))
    ar3 = stack.enter_context(nc.sbuf_tensor("ar3", [128, 16384], BF16))
    ones_bf = stack.enter_context(nc.sbuf_tensor("ones_bf", [128, 1], BF16))
    onesr_bf = stack.enter_context(nc.sbuf_tensor("onesr_bf", [1, 128], BF16))
    onesr_f = stack.enter_context(nc.sbuf_tensor("onesr_f", [1, 128], F32))
    ident_sb = stack.enter_context(nc.sbuf_tensor("ident_sb", [128, 128], F32))
    akT_sb = stack.enter_context(nc.sbuf_tensor("akT_sb", [128, 40], BF16))
    av_sb = stack.enter_context(nc.sbuf_tensor("av_sb", [16, 512], BF16))
    adT_sb = stack.enter_context(nc.sbuf_tensor("adT_sb", [128, 320], BF16))
    g_sb = stack.enter_context(nc.sbuf_tensor("g_sb", [1, 1], F32))
    ce_sb = stack.enter_context(nc.sbuf_tensor("ce_sb", [128, 8], F32))
    keys_sb = stack.enter_context(nc.sbuf_tensor("keys_sb", [128, 8], F32))
    qT_sb = stack.enter_context(nc.sbuf_tensor("qT_sb", [128, 1], F32))
    hsum_sb = stack.enter_context(nc.sbuf_tensor("hsum_sb", [128, NCC], F32))
    mrow_sb = stack.enter_context(nc.sbuf_tensor("mrow_sb", [1, 4096], F32))
    srow_sb = stack.enter_context(nc.sbuf_tensor("srow_sb", [1, 1024], F32))
    mask_sb = stack.enter_context(nc.sbuf_tensor("mask_sb", [128, 4 * S], F32))

    a1 = ar1.ap()
    a2 = ar2.ap()
    a3 = ar3.ap()
    f1 = ar1.ap().bitcast(F32)
    f2 = ar2.ap().bitcast(F32)
    f3 = ar3.ap().bitcast(F32)

    SA = stack.enter_context(nc.psum_tensor("SA", [128, 1024], F32))
    SB = stack.enter_context(nc.psum_tensor("SB", [128, 1024], F32))
    OA = stack.enter_context(nc.psum_tensor("OA", [128, 512], F32))
    OB = stack.enter_context(nc.psum_tensor("OB", [128, 512], F32))
    SUMS = stack.enter_context(nc.psum_tensor("SUMS", [128, 512], F32))
    AUX = stack.enter_context(nc.psum_tensor("AUX", [128, 512], F32))

    E = P.emit

    E("pool", lambda e: e.memset(ones_bf[:, :], 1.0))
    E("pool", lambda e: e.memset(onesr_bf[:, :], 1.0))
    E("pool", lambda e: e.memset(onesr_f[:, :], 1.0))
    E("pool", lambda e: e.memset(av_sb[:, :], 0.0), inc=t_pool)
    mi_ones = (t_pool, t_pool.n)
    E("sp", lambda e: e.dma_start(out=ident_sb[:, :], in_=ident_d), inc=t_li)
    E("sp", lambda e: e.dma_start(
        out=mask_sb[:, :].rearrange("p (m s) -> p m s", m=4),
        in_=masks.rearrange("m p s -> p m s")), inc=t_li)
    E("sp", lambda e: e.dma_start(out=g_sb[:, :], in_=gat), inc=t_li)
    E("sp", lambda e: e.dma_start(out=keys_sb[:, :], in_=keysT), inc=t_li)
    E("sp", lambda e: e.dma_start(
        out=f3[:, 0:320].rearrange("p (cc a) -> p cc a", a=10),
        in_=adT.rearrange("(cc p) a -> p cc a", p=128)), inc=t_li)
    mi_init = (t_li, t_li.n)
    mi_ident = mi_masks = mi_gat = mi_keys = mi0 = mi_init
    E("pool", lambda e: e.tensor_copy(out=adT_sb[:, :], in_=f3[:, 0:320]),
      inc=t_pool, waits=[mi0])
    mi_adT = (t_pool, t_pool.n)

    free = {}
    for cc in range(NCC):
        sl = cc % 2
        xf = f3[:, 512 + 1024 * sl: 512 + 1024 * sl + 1024]
        xb = a1[:, 1024 * sl:1024 * sl + 1024]
        sq = a1[:, 2048 + 1024 * sl:2048 + 1024 * sl + 1024]
        E("sp", lambda e, d=xf, s=xT[128 * cc:128 * cc + 128, :]:
          e.dma_start(out=d, in_=s), inc=L[sl],
          waits=[free.get(("a0x", sl)), mi_adT if cc < 2 else None])
        mi_x = (L[sl], L[sl].n)
        E("pool", lambda e, d=xb, s=xf: e.tensor_copy(out=d, in_=s),
          inc=t_pool, waits=[mi_x, free.get(("a0b", sl))])
        mi_xb = (t_pool, t_pool.n)
        free[("a0x", sl)] = mi_xb
        E("act", lambda e, d=sq, s=xb: e.activation(out=d, in_=s, func=AF.Square),
          inc=t_act, waits=[mi_xb, free.get(("a0s", sl))])
        mi_sq = (t_act, t_act.n)
        st, sp_ = (cc == 0), (cc == NCC - 1)
        for h in range(2):
            bank = SUMS if h == 0 else AUX
            E("pe", lambda e, b=bank, r=xb, h_=h, st_=st, sp__=sp_:
              e.matmul(b[0:1, :], ones_bf[:, :], r[:, 512 * h_:512 * h_ + 512],
                       start=st_, stop=sp__),
              waits=[mi_xb, mi_ones] if (h == 0) else [])
            E("pe", lambda e, b=bank, r=sq, h_=h, st_=st, sp__=sp_:
              e.matmul(b[32:33, :], ones_bf[:, :], r[:, 512 * h_:512 * h_ + 512],
                       start=st_, stop=sp__),
              inc=t_pe if h == 1 else None, waits=[mi_sq] if h == 0 else [])
        free[("a0b", sl)] = (t_pe, t_pe.n)
        free[("a0s", sl)] = (t_pe, t_pe.n)
    mi_stats1 = (t_pe, t_pe.n)

    def ln_rows(mi_done, roff):
        out_mis = []
        for h in range(2):
            bank = SUMS if h == 0 else AUX
            m_h = mrow_sb[0:1, roff + 512 * h: roff + 512 * h + 512]
            r_h = mrow_sb[0:1, roff + 1024 + 512 * h: roff + 1024 + 512 * h + 512]
            tmp = srow_sb[0:1, 512 * h:512 * h + 512]
            E("act", lambda e, d=m_h, b=bank: e.mul(out=d, in_=b[0:1, :], mul=1.0 / C),
              inc=t_act, waits=[mi_done])
            mi_m = (t_act, t_act.n)
            E("act", lambda e, d=r_h, b=bank: e.mul(out=d, in_=b[32:33, :], mul=1.0 / C),
              inc=t_act)
            mi_e2 = (t_act, t_act.n)
            E("dve", lambda e, d=tmp, s=m_h: e.tensor_mul(d, s, s),
              inc=t_dve, waits=[mi_m])
            E("dve", lambda e, d=r_h, s=tmp: e.tensor_sub(d, d, s),
              inc=t_dve, waits=[mi_e2])
            E("dve", lambda e, d=r_h: e.tensor_scalar_add(d, d, EPS), inc=t_dve)
            mi_v = (t_dve, t_dve.n)
            E("act", lambda e, d=r_h: e.activation(out=d, in_=d, func=AF.Sqrt),
              inc=t_act, waits=[mi_v])
            mi_s = (t_act, t_act.n)
            E("dve", lambda e, d=r_h: e.reciprocal(d, d), inc=t_dve, waits=[mi_s])
            out_mis.append((t_dve, t_dve.n))
        return out_mis[-1]

    mi_rows1 = ln_rows(mi_stats1, 0)

    mi_ag_h1h = {}
    prev_half_done = None
    for h in range(2):
        m_h = mrow_sb[0:1, 512 * h:512 * h + 512]
        r_h = mrow_sb[0:1, 1024 + 512 * h:1024 + 512 * h + 512]
        E("pe", lambda e, r=m_h: e.matmul(OA[:, :], onesr_f[:, :], r, start=True, stop=True),
          inc=t_pe, waits=[mi_rows1, prev_half_done])
        E("pe", lambda e, r=r_h: e.matmul(OB[:, :], onesr_f[:, :], r, start=True, stop=True),
          inc=t_pe)
        mi_bc = (t_pe, t_pe.n)
        for cc in range(NCC):
            sl = cc % 2
            xf = f3[:, 512 + 512 * sl:512 + 512 * sl + 512]
            tm = f3[:, 1536 + 512 * sl:1536 + 512 * sl + 512]
            hb = a1[:, 4096 + 512 * sl:4096 + 512 * sl + 512]
            E("sp", lambda e, d=xf, s=xT[128 * cc:128 * cc + 128, 512 * h:512 * h + 512]:
              e.dma_start(out=d, in_=s), inc=L[sl],
              waits=[free.get(("nx", sl)), mi_stats1 if cc < 2 else None])
            mi_x = (L[sl], L[sl].n)
            E("dve", lambda e, d=tm, s=xf: e.tensor_sub(d, s, OA[:, :]),
              inc=t_dve, waits=[mi_x, mi_bc, free.get(("ntm", sl))])
            mi_sub = (t_dve, t_dve.n)
            free[("nx", sl)] = mi_sub
            E("dve", lambda e, d=hb, s=tm: e.tensor_mul(d, s, OB[:, :]),
              inc=t_dve, waits=[free.get(("nh", sl))])
            mi_h1 = (t_dve, t_dve.n)
            free[("ntm", sl)] = mi_h1
            E("act", lambda e, s=hb, cc_=cc, h_=h:
              e.dma_start(out=h1own_d[128 * h_:128 * h_ + 128, 512 * cc_:512 * cc_ + 512], in_=s),
              inc=Strk[sl], waits=[mi_h1])
            free[("nh", sl)] = (Strk[sl], Strk[sl].n)
        prev_half_done = (t_dve, t_dve.n)
        h_mi = [(Strk[0], Strk[0].n), (Strk[1], Strk[1].n)]
        E("pool", lambda e, h_=h: e.collective_compute(
            "AllGather", ALU.bypass, replica_groups=RG,
            ins=[h1own_d[128 * h_:128 * h_ + 128, :].opt()],
            outs=[h1all_d[1024 * h_:1024 * h_ + 1024, :].opt()]),
          inc=t_cc, waits=h_mi)
        mi_ag_h1h[h] = (t_cc, t_cc.n)
        mi_a0h1 = h_mi

    mi_wspill = None
    for i in range(12):
        sl = i % 2
        wf = f1[:, 4096 + 4096 * sl:4096 + 4096 * sl + 4096]
        wb = a1[:, 24576 + 4096 * sl:24576 + 4096 * sl + 4096]
        E("sp", lambda e, d=wf, s=wqkvT[:, 4096 * i:4096 * i + 4096]:
          e.dma_start(out=d, in_=s),
          inc=L[2 + sl], waits=[free.get(("wsf", sl))])
        mi_w = (L[2 + sl], L[2 + sl].n)
        E("dve", lambda e, d=wb, s=wf: e.tensor_copy(out=d, in_=s),
          inc=t_dve, waits=[mi_w, free.get(("wsb", sl))])
        mi_wb = (t_dve, t_dve.n)
        free[("wsf", sl)] = mi_wb
        E("act", lambda e, s=wb, i_=i:
          e.dma_start(out=wqkv_bf_d[:, 4096 * i_:4096 * i_ + 4096], in_=s),
          inc=Strk[2 + sl], waits=[mi_wb])
        free[("wsb", sl)] = (Strk[2 + sl], Strk[2 + sl].n)
    mi_wspill = [(Strk[2], Strk[2].n), (Strk[3], Strk[3].n)]

    if debug_stage >= 1:
        E("pool", lambda e: e.dma_start(out=dbg["h1all"], in_=h1all_d),
          inc=t_ld, waits=[mi_ag_h1h[0], mi_ag_h1h[1]])
    if debug_stage == 1:
        return _finish(nc, P, stack)

    qkv_banks = [OA[:, :], OB[:, :], SA[:, 0:512], SA[:, 512:1024]]
    h1_slots = [a1[:, 0:16384], a1[:, 16384:32768], a2[:, 13312:29696]]
    mi_adchain = mi_adT
    mi_h1slot = {}
    for it in range(8):
        hfo, q = it // 4, it % 4
        mi_h1s = {}
        for hf in range(2):
            ts = 2 * (2 * q + hf) + hfo
            slot = (2 * it + hf) % 3
            h1s = h1_slots[slot]
            E("sp", lambda e, d=h1s, r=1024 * hfo + 128 * (2 * q + hf):
              e.dma_start(out=d, in_=h1all_d[r:r + 128, :]),
              inc=L[slot], waits=[mi_ag_h1h[hfo], mi_h1slot.get(slot)]
              + (mi_wspill + mi_a0h1 if 2 * it + hf < 3 else []))
            mi_h1s[hf] = ((L[slot], L[slot].n), h1s, slot)
        csl = {}
        cs_trk = {(0, 0): L[4], (0, 1): L[5], (1, 0): L[8], (1, 1): L[9]}
        for hf in range(2):
            ts = 2 * (2 * q + hf) + hfo
            base = 3072 + 2048 * (q % 2) + 1024 * hf
            ctk = cs_trk[(q % 2, hf)]
            E("sp", lambda e, d=f3[:, base:base + 512], s_=cosT[:, 512 * ts:512 * ts + 512]:
              e.dma_start(out=d, in_=s_), inc=ctk, waits=[free.get(("cs", q % 2, hf))])
            E("sp", lambda e, d=f3[:, base + 512:base + 1024], s_=sinS[:, 512 * ts:512 * ts + 512]:
              e.dma_start(out=d, in_=s_), inc=ctk)
            csl[hf] = ((ctk, ctk.n), f3[:, base:base + 512], f3[:, base + 512:base + 1024])
        for ot in range(12):
            wsl = ot % 2
            wb = a2[:, 4096 * wsl:4096 * wsl + 4096]
            E("sp", lambda e, d=wb, o=ot:
              e.dma_start(out=d, in_=wqkv_bf_d[:, 4096 * o:4096 * o + 4096]),
              inc=L[6 + wsl], waits=(mi_wspill if it == 0 else []) + [free.get(("w", wsl))])
            mi_w = (L[6 + wsl], L[6 + wsl].n)
            bk0, bk1 = (0, 1) if ot % 2 == 0 else (2, 3)
            for cc in range(NCC):
                for hf in range(2):
                    bank = qkv_banks[bk0 + hf]
                    E("pe", lambda e, b=bank, w=wb, h_=mi_h1s[hf][1], cc_=cc,
                      st=(cc == 0), sp_=(cc == NCC - 1):
                      e.matmul(b, w[:, 128 * cc_:128 * cc_ + 128],
                               h_[:, 512 * cc_:512 * cc_ + 512], start=st, stop=sp_),
                      inc=t_pe if (cc == NCC - 1 and hf == 1) else None,
                      waits=[mi_w, mi_h1s[0][0], mi_h1s[1][0],
                             free.get(("qb", bk0)), free.get(("qb", bk0 + 1))]
                      if (cc == 0 and hf == 0) else [])
            mi_mm = (t_pe, t_pe.n)
            free[("w", wsl)] = mi_mm
            mi_h1slot[mi_h1s[0][2]] = mi_mm
            mi_h1slot[mi_h1s[1][2]] = mi_mm
            if it == 0 and ot >= 4:
                for cc in range(NCC):
                    E("pe", lambda e, w=wb, cc_=cc, st=(cc == 0), sp_=(cc == NCC - 1):
                      e.matmul(SUMS[:, 0:10], w[:, 128 * cc_:128 * cc_ + 128],
                               adT_sb[:, 10 * cc_:10 * cc_ + 10], start=st, stop=sp_),
                      inc=t_pe if cc == NCC - 1 else None,
                      waits=[mi_adchain] if cc == 0 else [])
                mi_amm = (t_pe, t_pe.n)
                free[("w", wsl)] = mi_amm
                if ot < 8:
                    E("dve", lambda e, o=ot: e.tensor_copy(
                        out=akT_sb[:, 10 * (o - 4):10 * (o - 4) + 10], in_=SUMS[:, 0:10]),
                      inc=t_dve, waits=[mi_amm])
                    mi_adchain = (t_dve, t_dve.n)
                else:
                    avt = f3[:, 7168:7184]
                    E("dve", lambda e, d=avt: e.tensor_copy(out=d[:, 0:10], in_=SUMS[:, 0:10]),
                      inc=t_dve, waits=[mi_amm])
                    mi_ae = (t_dve, t_dve.n)
                    E("pe", lambda e, s_=avt: e.transpose(AUX[0:10, 0:128], s_[:, 0:10], ident_sb[:, :]),
                      inc=t_pe, waits=[mi_ae, mi_ident, free.get(("aux_a1", 0))])
                    mi_tr = (t_pe, t_pe.n)
                    E("act", lambda e, o=ot: e.copy(
                        out=av_sb[0:10, 128 * (o - 8):128 * (o - 8) + 128], in_=AUX[0:10, 0:128]),
                      inc=t_act, waits=[mi_tr])
                    mi_adchain = (t_act, t_act.n)
                    free[("aux_a1", 0)] = mi_adchain
            for hf in range(2):
                ts = 2 * (2 * q + hf) + hfo
                bank = qkv_banks[bk0 + hf]
                par = bk0 + hf
                if ot < 8:
                    ev = f3[:, 512 * hf:512 * hf + 512]
                    tt1 = f3[:, 1024 + 1024 * hf:1024 + 1024 * hf + 512]
                    ro = f3[:, 1024 + 1024 * hf + 512:1024 + 1024 * hf + 1024]
                    ob = a2[:, 8192 + 512 * hf:8192 + 512 * hf + 512]
                    E("dve", lambda e, d=ev, b=bank: e.tensor_copy(out=d, in_=b),
                      inc=t_dve, waits=[mi_mm, free.get(("ev", hf))])
                    mi_ev = (t_dve, t_dve.n)
                    free[("qb", par)] = mi_ev
                    cs_mi, cs_c, cs_s = csl[hf]
                    E("pool", lambda e, d=tt1, s_=ev, c_=cs_c: e.tensor_mul(d, s_, c_),
                      inc=t_pool, waits=[mi_ev, cs_mi, free.get(("t1", hf))])
                    E("pool", lambda e, d=ro, s_=ev: e.tensor_copy(out=d[0:64, :], in_=s_[64:128, :]))
                    E("pool", lambda e, d=ro, s_=ev: e.tensor_copy(out=d[64:128, :], in_=s_[0:64, :]),
                      inc=t_pool)
                    mi_rot = (t_pool, t_pool.n)
                    free[("ev", hf)] = mi_rot
                    E("pool", lambda e, d=ro, s_=cs_s: e.tensor_mul(d, d, s_), inc=t_pool)
                    E("pool", lambda e, d=ob, a_=tt1, b_=ro: e.tensor_add(d, a_, b_),
                      inc=t_pool, waits=[free.get(("ob", hf))])
                    mi_ob = (t_pool, t_pool.n)
                    free[("t1", hf)] = mi_ob
                    free[("cs", q % 2, hf)] = mi_ob
                    dst = q_d if ot < 4 else k_d
                    E("act", lambda e, s_=ob, d=dst, r=128 * (ot % 4), t0=512 * ts:
                      e.dma_start(out=d[r:r + 128, t0:t0 + 512], in_=s_),
                      inc=Strk[hf], waits=[mi_ob])
                    free[("ob", hf)] = (Strk[hf], Strk[hf].n)
                else:
                    ev = f3[:, 512 * hf:512 * hf + 512]
                    vsb = a2[:, 9216 + 2048 * hf:9216 + 2048 * hf + 2048]
                    tb = AUX if hf == 0 else SB[:, 0:512]
                    E("dve", lambda e, d=ev, b=bank: e.tensor_copy(out=d, in_=b),
                      inc=t_dve, waits=[mi_mm, free.get(("ev", hf))])
                    mi_ev = (t_dve, t_dve.n)
                    free[("qb", par)] = mi_ev
                    for j in range(4):
                        E("pe", lambda e, s_=ev, j_=j, b=tb: e.transpose(
                            b[:, 128 * j_:128 * j_ + 128], s_[:, 128 * j_:128 * j_ + 128],
                            ident_sb[:, :]),
                          inc=t_pe if j == 3 else None,
                          waits=[mi_ev, mi_ident, free.get(("aux_a1", hf))] if j == 0 else [])
                    mi_tr = (t_pe, t_pe.n)
                    free[("ev", hf)] = mi_tr
                    for j in range(4):
                        E("act", lambda e, d=vsb, j_=j, c0=128 * (ot - 8), b=tb:
                          e.copy(out=d[:, 512 * j_ + c0:512 * j_ + c0 + 128],
                                 in_=b[:, 128 * j_:128 * j_ + 128]),
                          inc=t_act if j == 3 else None,
                          waits=[mi_tr, free.get(("vsb", hf)) if j == 0 else None]
                          if j == 0 else [])
                    mi_vs = (t_act, t_act.n)
                    free[("aux_a1", hf)] = mi_vs
                    if ot == 11:
                        E("act", lambda e, s_=vsb, t_=ts:
                          e.dma_start(out=v_d[:, 2048 * t_:2048 * t_ + 2048], in_=s_),
                          inc=Strk[2 + hf], waits=[mi_vs])
                        free[("vsb", hf)] = (Strk[2 + hf], Strk[2 + hf].n)
    mi_a1_st = [(Strk[i], Strk[i].n) for i in range(4)]
    mi_a1 = mi_a1_st + [(t_pe, t_pe.n), (t_pool, t_pool.n), (t_dve, t_dve.n)]

    if debug_stage >= 2:
        for nm, src in (("q", q_d), ("k", k_d), ("v", v_d)):
            E("pool", lambda e, n=nm, s_=src: e.dma_start(out=dbg[n], in_=s_), inc=t_ld, waits=mi_a1)
        E("pool", lambda e: e.dma_start(out=dbg["ak"], in_=akT_sb[:, :]), inc=t_ld, waits=mi_a1)
        E("pool", lambda e: e.dma_start(out=dbg["av"], in_=av_sb[:, :]), inc=t_ld, waits=mi_a1)
        E("pool", lambda e: e.dma_start(out=dbg["wbf"], in_=wqkv_bf_d), inc=t_ld, waits=mi_a1)
    if debug_stage == 2:
        return _finish(nc, P, stack)

    E("sp", lambda e: e.dma_start(out=a1[:, :].rearrange("p (h t) -> p h t", h=4),
                                  in_=k_d.rearrange("(h p) t -> p h t", p=128)),
      inc=L[2], waits=mi_a1)
    mi_K = (L[2], L[2].n)
    E("sp", lambda e: e.dma_start(out=a2[:, :], in_=v_d),
      inc=L[3], waits=mi_a1)
    mi_V = (L[3], L[3].n)

    mi_sgrp = {}
    mi_pslot = {}
    mi_oband = {}
    mi_mfin = {}
    mi_expA = {}
    mi_expA2 = {}
    mi_fbc = {}
    mi_rbc = {}
    mi_ybf = {}
    mi_oev = {}
    deferred = []

    def flush_defer():
        for eng, fn, inc, waits, fut in deferred:
            mi = E(eng, fn, inc=inc, waits=waits)
            if fut is not None:
                fut.trk, fut.val = mi
        deferred.clear()

    def defer(eng, fn, inc, waits, fut=None):
        deferred.append((eng, fn, inc, waits, fut))

    ystore_futs = []
    for c in range(NTS):
        sl = c % 2
        qs = a3[:, 2048 * sl:2048 * sl + 2048]
        E("sp", lambda e, d=qs, c_=c:
          e.dma_start(out=d.rearrange("p (h t) -> p h t", h=4),
                      in_=q_d[:, 512 * c_:512 * c_ + 512].rearrange("(h p) t -> p h t", p=128)),
          inc=L[4 + sl], waits=mi_a1_st[:2] + [free.get(("qs", sl))])
        mi_q = (L[4 + sl], L[4 + sl].n)
        nkt = 4 * (c + 1)
        npair = nkt // 2
        for h in range(HPC):
            ph = h % 2
            qv = qs[:, 512 * h:512 * h + 512]
            oband = OA if ph == 0 else OB
            sbank = SUMS if ph == 0 else AUX

            def emit_S(p_, first=False):
                sgrp = SA if p_ % 2 == 0 else SB
                for jj in range(2):
                    j = 2 * p_ + jj
                    E("pe", lambda e, g=sgrp, h_=h, j_=j, o=512 * jj, q_=qv:
                      e.matmul(g[:, o:o + 512],
                               a1[:, 8192 * h_ + 128 * j_:8192 * h_ + 128 * j_ + 128],
                               q_, start=True, stop=True),
                      inc=t_pe if jj == 1 else None,
                      waits=[mi_K, mi_q, mi_sgrp.get(p_ % 2)] if jj == 0 else [])
                return (t_pe, t_pe.n)

            mi_s = {}
            mi_s[0] = emit_S(0)
            if npair > 1:
                mi_s[1] = emit_S(1)
            dp = min(2, npair - 1)
            for p_ in range(npair):
                sgrp = SA if p_ % 2 == 0 else SB
                mi_mask = None
                for jj in range(2):
                    o = 2 * p_ + jj - 4 * c
                    if 0 <= o < 4:
                        E("dve", lambda e, g=sgrp, off=512 * jj, o_=o:
                          e.tensor_add(g[:, off:off + 512], g[:, off:off + 512],
                                       mask_sb[:, 512 * o_:512 * o_ + 512]),
                          inc=t_dve, waits=[mi_s[p_], mi_masks])
                        mi_mask = (t_dve, t_dve.n)
                psl = p_ % 3
                pb = a3[:, 4096 + 1024 * psl:4096 + 1024 * psl + 1024]
                E("act", lambda e, d=pb, g=sgrp:
                  e.activation(out=d, in_=g[:, :], func=AF.Exp, scale=SCALE_MAIN),
                  inc=t_act, waits=[mi_s[p_], mi_mask, mi_pslot.get(psl)])
                mi_e = (t_act, t_act.n)
                mi_sgrp[p_ % 2] = mi_e
                if p_ + 2 < npair:
                    mi_s[p_ + 2] = emit_S(p_ + 2)
                if p_ == dp:
                    flush_defer()
                for jj in range(2):
                    j = 2 * p_ + jj
                    E("pe", lambda e, ob=oband, j_=j, h_=h, p__=pb:
                      e.matmul(ob[:, :], a2[:, 512 * j_ + 128 * h_:512 * j_ + 128 * h_ + 128],
                               p__[:, 512 * (j_ % 2):512 * (j_ % 2) + 512],
                               start=(j_ == 0), stop=False),
                      waits=([mi_e, mi_V] + ([mi_oband.get(ph)] if j == 0 else []))
                      if jj == 0 else [])
                    E("pe", lambda e, j_=j, p__=pb, b=sbank, nk=nkt:
                      e.matmul(b[0:1, :], ones_bf[:, :],
                               p__[:, 512 * (j_ % 2):512 * (j_ % 2) + 512],
                               start=(j_ == 0), stop=(j_ == nk - 1)),
                      inc=t_pe if jj == 1 else None,
                      waits=[mi_mfin.get(ph)] if j == 0 else [])
                mi_pslot[psl] = (t_pe, t_pe.n)
            if deferred:
                flush_defer()
            mi_kloop = (t_pe, t_pe.n)
            E("pe", lambda e, h_=h, q_=qv, b=sbank:
              e.matmul(b[32:42, :], akT_sb[:, 10 * h_:10 * h_ + 10], q_, start=True, stop=True),
              inc=t_pe, waits=[mi_expA.get(ph)])
            mi_as = (t_pe, t_pe.n)
            expA = a3[:, 7168 + 512 * ph:7168 + 512 * ph + 512]
            E("act", lambda e, d=expA, b=sbank:
              e.activation(out=d[0:10, :], in_=b[32:42, :], func=AF.Exp, scale=SCALE_AD),
              inc=t_act, waits=[mi_as, mi_expA2.get(ph)])
            mi_ea = (t_act, t_act.n)
            mi_expA[ph] = mi_ea
            E("pe", lambda e, s_=expA, b=sbank:
              e.matmul(b[64:65, :], ones_bf[0:10, :], s_[0:10, :], start=True, stop=True),
              inc=t_pe, waits=[mi_ea])
            mi_asum = (t_pe, t_pe.n)
            lnr = srow_sb[0:1, 512 * ph:512 * ph + 512]
            E("act", lambda e, d=lnr, b=sbank:
              e.activation(out=d, in_=b[64:65, :], func=AF.Ln),
              inc=t_act, waits=[mi_asum])
            E("act", lambda e, d=lnr:
              e.activation(out=d, in_=d, func=AF.Exp, scale=-1.0),
              inc=t_act)
            mi_ras = (t_act, t_act.n)
            frow = a3[0:1, 14336 + 512 * ph:14336 + 512 * ph + 512]
            E("dve", lambda e, d=frow, r=lnr, b=sbank:
              e.scalar_tensor_tensor(d, b[0:1, :], g_sb[0:1, 0:1], r,
                                     op0=ALU.mult, op1=ALU.mult),
              inc=t_dve, waits=[mi_ras, mi_gat, mi_fbc.get(ph)])
            mi_f = (t_dve, t_dve.n)
            E("sp", lambda e, s_=frow, p=ph: e.dma_start(out=bc_d[p:p + 1, :], in_=s_),
              inc=t_fr[ph], waits=[mi_f, mi_fbc.get(ph)])
            mi_fd = (t_fr[ph], t_fr[ph].n)
            fbc = a3[0:10, 9216 + 512 * ph:9216 + 512 * ph + 512]
            E("sp", lambda e, d=fbc, p=ph:
              e.dma_start(out=d, in_=bc_d[p:p + 1, :].broadcast_to([10, 512])),
              inc=t_fb[ph], waits=[mi_fd, mi_expA2.get(ph)])
            mi_fb = (t_fb[ph], t_fb[ph].n)
            mi_fbc[ph] = mi_fb
            expA2 = a3[0:10, 8192 + 512 * ph:8192 + 512 * ph + 512]
            E("pool", lambda e, d=expA2, a=expA, f_=fbc:
              e.tensor_mul(d, a[0:10, :], f_),
              inc=t_pool, waits=[mi_fb, mi_ea])
            mi_e2 = (t_pool, t_pool.n)
            mi_expA2[ph] = mi_e2
            rrow = a3[0:1, 15360 + 512 * ph:15360 + 512 * ph + 512]
            lnm = srow_sb[0:1, 512 * ph:512 * ph + 512]
            E("act", lambda e, d=lnm, b=sbank:
              e.activation(out=d, in_=b[0:1, :], func=AF.Ln),
              inc=t_act, waits=[mi_f])
            E("act", lambda e, d=rrow, s_=lnm:
              e.activation(out=d, in_=s_, func=AF.Exp, scale=-1.0),
              inc=t_act)
            mi_r = (t_act, t_act.n)
            mi_mfin[ph] = mi_r
            E("sp", lambda e, s_=rrow, p=ph: e.dma_start(out=bc_d[2 + p:3 + p, :], in_=s_),
              inc=t_rr[ph], waits=[mi_r, mi_rbc.get(ph)])
            mi_rd = (t_rr[ph], t_rr[ph].n)
            rbc = a3[:, 10240 + 512 * ph:10240 + 512 * ph + 512]
            E("sp", lambda e, d=rbc, p=ph:
              e.dma_start(out=d, in_=bc_d[2 + p:3 + p, :].broadcast_to([128, 512])),
              inc=t_rb[ph], waits=[mi_rd, mi_rbc.get(ph)])
            mi_rb = (t_rb[ph], t_rb[ph].n)
            fut_ao = _Fut()
            defer("pe",
                  (lambda h_, s_, ob: lambda e:
                   e.matmul(ob[:, :], av_sb[0:10, 128 * h_:128 * h_ + 128], s_[0:10, :],
                            start=False, stop=True))(h, expA2, oband),
                  t_pe, [mi_e2], fut_ao)
            oev = f3[:, 6144 + 512 * ph:6144 + 512 * ph + 512]
            fut_oe = _Fut()
            defer("dve",
                  (lambda d, ob: lambda e: e.tensor_copy(out=d, in_=ob[:, :]))(oev, oband),
                  t_dve, [fut_ao, mi_oev.get(ph)], fut_oe)
            ybf = a3[:, 11264 + 512 * ph:11264 + 512 * ph + 512]
            fut_ytt = _Fut()
            defer("pool",
                  (lambda d, s_, r_: lambda e: e.tensor_mul(d, s_, r_))(ybf, oev, rbc),
                  t_pool, [mi_rb, fut_oe, mi_ybf.get(ph)], fut_ytt)
            fut_yst = _Fut()
            defer("sp",
                  (lambda s_, h_, c_: lambda e:
                   e.dma_start(out=yown_d[512 * (c_ // 2) + 128 * h_:
                                          512 * (c_ // 2) + 128 * h_ + 128,
                                          512 * (c_ % 2):512 * (c_ % 2) + 512],
                               in_=s_))(ybf, h, c),
                  t_ys[ph], [fut_ytt], fut_yst)
            ystore_futs.append(fut_yst)
            mi_oband[ph] = fut_ytt
            mi_ybf[ph] = fut_yst
            mi_oev[ph] = fut_ytt
            mi_rbc[ph] = fut_ytt
        free[("qs", sl)] = (t_pe, t_pe.n)
        if c % 2 == 1:
            k = c // 2
            defer("pool",
                  (lambda k_: lambda e: e.collective_compute(
                      "AllGather", ALU.bypass, replica_groups=RG,
                      ins=[yown_d[512 * k_:512 * k_ + 512, :].opt()],
                      outs=[yall_d[4096 * k_:4096 * k_ + 4096, :].opt()]))(k),
                  t_cc, list(ystore_futs))
            ystore_futs = []
    flush_defer()
    mi_ag_y = (t_cc, t_cc.n)

    if debug_stage >= 3:
        E("pool", lambda e: e.dma_start(out=dbg["yall"], in_=yall_d),
          inc=t_ld, waits=[mi_ag_y])
    if debug_stage == 3:
        return _finish(nc, P, stack)

    zbanks = [[OA[:, :], OB[:, :]], [SA[:, 0:512], SA[:, 512:1024]]]
    pend_stats = []
    for th in range(2):
        def _ldy(e, th_=th):
            pid = e.partition_id()
            return e.dma_start(out=a1[:, 16384 * th_:16384 * th_ + 16384]
                               .rearrange("p (cc t) -> p cc t", t=512),
                               in_=yall_d.rearrange("(R p) t -> p R t", p=128)
                               [:, bass.ts(pid, 32), 512 * th_:512 * th_ + 512])
        E("sp", _ldy, inc=L[th], waits=[mi_ag_y])
        free[("yld", th)] = (L[th], L[th].n)
    mi_y = [free[("yld", 0)], free[("yld", 1)]]
    for ot in range(NCC):
        wsl = ot % 2
        wf = f2[:, 8192 + 4096 * wsl:8192 + 4096 * wsl + 4096]
        wb = a3[:, 4096 * wsl:4096 * wsl + 4096]
        E("sp", lambda e, d=wf, o=ot:
          e.dma_start(out=d, in_=wprojT[:, 4096 * o:4096 * o + 4096]),
          inc=L[6 + wsl], waits=[free.get(("cwf", wsl)),
                                 mi_ag_y if ot < 2 else None,
                                 mi_V if ot < 2 else None])
        mi_wf = (L[6 + wsl], L[6 + wsl].n)
        E("dve", lambda e, d=wb, s_=wf: e.tensor_copy(out=d, in_=s_),
          inc=t_dve, waits=[mi_wf, free.get(("cwb", wsl))])
        mi_wb = (t_dve, t_dve.n)
        free[("cwf", wsl)] = mi_wb
        bks = zbanks[ot % 2]
        for cc in range(NCC):
            for th in range(2):
                E("pe", lambda e, b=bks[th], w=wb, cc_=cc, t_=th,
                  st=(cc == 0), sp_=(cc == NCC - 1):
                  e.matmul(b, w[:, 128 * cc_:128 * cc_ + 128],
                           a1[:, 16384 * t_ + 512 * cc_:16384 * t_ + 512 * cc_ + 512],
                           start=st, stop=sp_),
                  inc=t_pe if (cc == NCC - 1 and th == 1) else None,
                  waits=[mi_wb, mi_y[0], mi_y[1],
                         free.get(("zb", ot % 2))] if (cc == 0 and th == 0) else [])
        mi_mm = (t_pe, t_pe.n)
        free[("cwb", wsl)] = mi_mm
        free["yh"] = mi_mm
        for g_ in pend_stats:
            g_()
        pend_stats = []
        for th in range(2):
            par = 2 * (ot % 2) + th
            xsl = f3[:, 4096 + 512 * par:4096 + 512 * par + 512]
            x2t = f3[:, 6144 + 512 * par:6144 + 512 * par + 512]
            x2b = a2[:, 512 * par:512 * par + 512]
            E("sp", lambda e, d=xsl, o=ot, t_=th:
              e.dma_start(out=d, in_=xT[128 * o:128 * o + 128, 512 * t_:512 * t_ + 512]),
              inc=L[2 + par], waits=[free.get(("cx", par))])
            mi_x = (L[2 + par], L[2 + par].n)
            E("dve", lambda e, d=x2t, b=bks[th], x=xsl: e.tensor_add(d, b, x),
              inc=t_dve, waits=[mi_mm, mi_x, free.get(("cx2", par)), free.get(("cx2a", par))])
            mi_x2 = (t_dve, t_dve.n)
            free[("cx", par)] = mi_x2
            E("act", lambda e, s_=x2t, o=ot, t_=th:
              e.dma_start(out=x2fm_d[:, 1024 * o + 512 * t_:1024 * o + 512 * t_ + 512], in_=s_),
              inc=Strk[1 + par], waits=[mi_x2])
            free[("cx2a", par)] = (Strk[1 + par], Strk[1 + par].n)
            E("pool", lambda e, d=x2b, s_=x2t: e.tensor_copy(out=d, in_=s_),
              inc=t_pool, waits=[mi_x2, free.get(("cxb", par))])
            mi_xb = (t_pool, t_pool.n)
            free[("cx2", par)] = mi_xb
            sqb = a2[:, 2048 + 512 * th:2048 + 512 * th + 512]
            E("act", lambda e, d=sqb, s_=x2b: e.activation(out=d, in_=s_, func=AF.Square),
              inc=t_act, waits=[mi_xb, free.get(("csq", th))])
            mi_sq = (t_act, t_act.n)
            st, sp_ = (ot == 0), (ot == NCC - 1)
            sbank = SUMS if th == 0 else AUX
            fut_st = _Fut()

            def mk_stats(b=sbank, r1=x2b, r2=sqb, st_=st, sp__=sp_,
                         w1=mi_xb, w2=mi_sq, f=fut_st):
                def go():
                    E("pe", lambda e: e.matmul(b[0:1, :], ones_bf[:, :], r1,
                                               start=st_, stop=sp__),
                      waits=[w1])
                    E("pe", lambda e: e.matmul(b[32:33, :], ones_bf[:, :], r2,
                                               start=st_, stop=sp__),
                      inc=t_pe, waits=[w2])
                    f.trk, f.val = (t_pe, t_pe.n)
                return go
            pend_stats.append(mk_stats())
            free[("cxb", par)] = fut_st
            free[("csq", th)] = fut_st
        free[("zb", ot % 2)] = (t_dve, t_dve.n)
    for g_ in pend_stats:
        g_()
    pend_stats = []
    mi_cstats = (t_pe, t_pe.n)
    mi_x2fm = [(Strk[i], Strk[i].n) for i in range(1, 5)]

    mi_rows2 = ln_rows(mi_cstats, 2048)

    bcm = f3[:, 0:1024]
    bcr = f3[:, 1024:2048]
    mi_bc = None
    for h in range(2):
        m_h = mrow_sb[0:1, 2048 + 512 * h:2048 + 512 * h + 512]
        r_h = mrow_sb[0:1, 3072 + 512 * h:3072 + 512 * h + 512]
        E("pe", lambda e, r=m_h: e.matmul(OA[:, :], onesr_f[:, :], r, start=True, stop=True),
          inc=t_pe, waits=[mi_rows2, free.get(("zb", 0)), free.get(("zb", 1))])
        mi1 = (t_pe, t_pe.n)
        E("pe", lambda e, r=r_h: e.matmul(OB[:, :], onesr_f[:, :], r, start=True, stop=True),
          inc=t_pe)
        mi2 = (t_pe, t_pe.n)
        E("dve", lambda e, d=bcm, h_=h: e.tensor_copy(out=d[:, 512 * h_:512 * h_ + 512],
                                                      in_=OA[:, :]),
          inc=t_dve, waits=[mi1])
        E("dve", lambda e, d=bcr, h_=h: e.tensor_copy(out=d[:, 512 * h_:512 * h_ + 512],
                                                      in_=OB[:, :]),
          inc=t_dve, waits=[mi2])
        mi_bc = (t_dve, t_dve.n)
        free[("zb", 0)] = mi_bc
        free[("zb", 1)] = mi_bc

    for cc in range(NCC):
        sl = cc % 2
        x2s = f3[:, 4096 + 1024 * sl:4096 + 1024 * sl + 1024]
        tmp = f3[:, 2048 + 1024 * sl:2048 + 1024 * sl + 1024]
        E("sp", lambda e, d=x2s, c_=cc:
          e.dma_start(out=d, in_=x2fm_d[:, 1024 * c_:1024 * c_ + 1024]),
          inc=L[2 + sl], waits=(mi_x2fm + [mi_cstats] if cc < 2 else []) + [free.get(("h2x", sl))])
        mi_x = (L[2 + sl], L[2 + sl].n)
        E("pool", lambda e, d=tmp, s=x2s, m=bcm: e.tensor_sub(d, s, m),
          inc=t_pool, waits=[mi_x, mi_bc, free.get(("h2t", sl))])
        mi_t = (t_pool, t_pool.n)
        free[("h2x", sl)] = mi_t
        h2v = a2[:, 1024 * cc:1024 * cc + 1024]
        E("pool", lambda e, d=h2v, s=tmp, r=bcr: e.tensor_mul(d, s, r),
          inc=t_pool, waits=[mi_V if cc == 0 else None])
        mi_h = (t_pool, t_pool.n)
        free[("h2t", sl)] = mi_h
        E("dve", lambda e, d=hsum_sb[:, cc:cc + 1], s=h2v: e.reduce_sum(d, s, axis=AX),
          inc=t_dve, waits=[mi_h])
    mi_h2 = (t_pool, t_pool.n)
    mi_hsum = (t_dve, t_dve.n)

    for tt in range(8):
        sl = tt % 2
        xsl = f1[:, 4096 * sl:4096 * sl + 4096]
        stg = f1[:, 8192 + 4096 * sl:8192 + 4096 * sl + 4096]
        E("sp", lambda e, d=xsl, t_=tt:
          e.dma_start(out=d.rearrange("p (cc t) -> p cc t", t=128),
                      in_=x2fm_d.rearrange("p (cc t) -> p cc t", t=1024)
                      [:, 0:32, 128 * t_:128 * t_ + 128]),
          inc=L[8 + sl], waits=mi_x2fm + [free.get(("xt", sl)),
                            free.get("yh") if tt < 2 else None])
        mi_x = (L[8 + sl], L[8 + sl].n)
        for g in range(8):
            for j in range(4):
                cc = 4 * g + j
                E("pe", lambda e, s=xsl, c_=cc, j_=j:
                  e.transpose(AUX[:, 128 * j_:128 * j_ + 128],
                              s[:, 128 * c_:128 * c_ + 128], ident_sb[:, :]),
                  inc=t_pe if j == 3 else None,
                  waits=[mi_x, mi_rows2, free.get("xaux")] if j == 0 else [])
            mi_tr = (t_pe, t_pe.n)
            E("act", lambda e, d=stg, g_=g: e.copy(out=d[:, 512 * g_:512 * g_ + 512],
                                                   in_=AUX[:, :]),
              inc=t_act, waits=[mi_tr, free.get(("xstg", sl)) if g == 0 else None])
            free["xaux"] = (t_act, t_act.n)
        mi_stg = (t_act, t_act.n)
        free[("xt", sl)] = mi_stg
        E("act", lambda e, s=stg, t_=tt:
          e.dma_start(out=x2tm_d[128 * t_:128 * t_ + 128, :], in_=s),
          inc=Strk[3 + sl], waits=[mi_stg])
        free[("xstg", sl)] = (Strk[3 + sl], Strk[3 + sl].n)
    mi_x2tm = [(Strk[3], Strk[3].n), (Strk[4], Strk[4].n)]

    E("act", lambda e: e.dma_start(out=hsin_d, in_=hsum_sb[:, :]),
      inc=Strk[5], waits=[mi_hsum])
    mi_hs = (Strk[5], Strk[5].n)
    E("pool", lambda e: e.collective_compute(
        "AllReduce", ALU.add, replica_groups=RG, ins=[hsin_d.opt()], outs=[hsout_d.opt()]),
      inc=t_cc, waits=[mi_hs])
    mi_ar = (t_cc, t_cc.n)
    mcol = f3[:, 6144:6176]
    E("sp", lambda e: e.dma_start(out=mcol, in_=hsout_d), inc=L[1], waits=[mi_ar])
    mi_m = (L[1], L[1].n)
    E("dve", lambda e: e.tensor_scalar_mul(mcol, mcol, 1.0 / T), inc=t_dve, waits=[mi_m])
    mi_m2 = (t_dve, t_dve.n)
    wqs = f1[:, 0:4096]
    E("sp", lambda e: e.dma_start(out=wqs, in_=wqT),
      inc=L[3], waits=[free.get(("xt", 0))])
    mi_wq = (L[3], L[3].n)
    for cc in range(NCC):
        E("pe", lambda e, c_=cc, st=(cc == 0), sp_=(cc == NCC - 1):
          e.matmul(AUX[0:1, 0:128], mcol[:, c_:c_ + 1], wqs[:, 128 * c_:128 * c_ + 128],
                   start=st, stop=sp_),
          inc=t_pe if cc == NCC - 1 else None,
          waits=[mi_m2, mi_wq, free.get("xaux"), (t_act, P.tracks["act"].n)] if cc == 0 else [])
    mi_qmm = (t_pe, t_pe.n)
    qrow = f3[:, 6176:6304]
    E("dve", lambda e: e.tensor_copy(out=qrow[0:1, :], in_=AUX[0:1, 0:128]),
      inc=t_dve, waits=[mi_qmm])
    mi_qr = (t_dve, t_dve.n)
    E("pe", lambda e: e.transpose(AUX[:, 256:257], qrow[0:1, :], ident_sb[0:1, 0:1]),
      inc=t_pe, waits=[mi_qr])
    mi_qt = (t_pe, t_pe.n)
    E("dve", lambda e: e.tensor_copy(out=qT_sb[:, :], in_=AUX[:, 256:257]),
      inc=t_dve, waits=[mi_qt])
    mi_qT = (t_dve, t_dve.n)
    E("pe", lambda e: e.matmul(AUX[0:1, 0:8], qT_sb[:, :], keys_sb[:, :],
                               start=True, stop=True),
      inc=t_pe, waits=[mi_qT, mi_keys])
    mi_sc = (t_pe, t_pe.n)
    e8 = f3[:, 6304:6312]
    rt = f3[:, 6312:6320]
    E("dve", lambda e: e.reduce_max(rt[0:1, 0:1], AUX[0:1, 0:8], axis=AX, negate=True),
      inc=t_dve, waits=[mi_sc])
    E("dve", lambda e: e.tensor_scalar_mul(rt[0:1, 0:1], rt[0:1, 0:1], SCALE_MAIN),
      inc=t_dve)
    mi_ng = (t_dve, t_dve.n)
    E("act", lambda e: e.activation(out=e8[0:1, :], in_=AUX[0:1, 0:8], func=AF.Exp,
                                    scale=SCALE_MAIN, bias=rt[0:1, 0:1]),
      inc=t_act, waits=[mi_ng])
    mi_e8 = (t_act, t_act.n)
    E("dve", lambda e: e.reduce_sum(rt[0:1, 1:2], e8[0:1, :], axis=AX),
      inc=t_dve, waits=[mi_e8])
    E("dve", lambda e: e.reciprocal(rt[0:1, 2:3], rt[0:1, 1:2]), inc=t_dve)
    E("dve", lambda e: e.tensor_scalar_mul(e8[0:1, :], e8[0:1, :], rt[0:1, 2:3]),
      inc=t_dve)
    mi_cf = (t_dve, t_dve.n)
    E("pe", lambda e: e.matmul(AUX[:, 0:8], onesr_f[:, :], e8[0:1, :], start=True, stop=True),
      inc=t_pe, waits=[mi_cf])
    mi_cemm = (t_pe, t_pe.n)
    E("dve", lambda e: e.tensor_copy(out=ce_sb[:, :], in_=AUX[:, 0:8]),
      inc=t_dve, waits=[mi_cemm])
    mi_ce = (t_dve, t_dve.n)

    if debug_stage >= 4:
        E("pool", lambda e: e.dma_start(out=dbg["x2"], in_=x2tm_d), inc=t_ld,
          waits=mi_x2tm)
        E("act", lambda e: e.dma_start(
            out=dbg["h2"].rearrange("(cc p) t -> p cc t", p=128),
            in_=a2[:, :].rearrange("p (cc t) -> p cc t", t=1024)), inc=t_ld2,
          waits=[mi_h2])
        E("act", lambda e: e.dma_start(out=dbg["coeffs"], in_=e8[0:1, :]),
          inc=t_ld2, waits=[mi_cf])
    mi_dbg4 = (t_ld2, t_ld2.n) if debug_stage >= 4 else None
    if debug_stage == 4:
        return _finish(nc, P, stack)

    gpack = a3[0:64, 0:8192]
    gbanks = [OA, OB, SUMS, AUX]
    mi_gev = {}
    for e_ in range(8):
        sl = e_ % 2
        dwf = f1[:, 2048 * sl:2048 * sl + 2048]
        dwb = a3[:, 8192 + 2048 * sl:8192 + 2048 * sl + 2048]
        E("sp", lambda e, d=dwf, i=e_:
          e.dma_start(out=d, in_=downT[:, 2048 * i:2048 * i + 2048]),
          inc=L[4 + sl], waits=[free.get(("df", sl)), mi_qmm if e_ < 2 else None])
        mi_w = (L[4 + sl], L[4 + sl].n)
        E("pool", lambda e, d=dwb, s=dwf: e.tensor_copy(out=d, in_=s),
          inc=t_pool, waits=[mi_w, free.get(("db", sl))])
        mi_wb = (t_pool, t_pool.n)
        free[("df", sl)] = mi_wb
        for th in range(2):
            gi = (e_ * 2 + th) % 4
            bank = gbanks[gi]
            E_waits = [mi_wb, mi_h2, mi_gev.get(gi),
                       mi_ce if e_ == 0 else None]
            for cc in range(NCC):
                E("pe", lambda e, b=bank, w=dwb, cc_=cc, t_=th,
                  st=(cc == 0), sp_=(cc == NCC - 1):
                  e.matmul(b[0:64, :], w[:, 64 * cc_:64 * cc_ + 64],
                           a2[:, 1024 * cc_ + 512 * t_:1024 * cc_ + 512 * t_ + 512],
                           start=st, stop=sp_),
                  inc=t_pe if cc == NCC - 1 else None,
                  waits=E_waits if cc == 0 else [])
            mi_mm = (t_pe, t_pe.n)
            E("dve", lambda e, b=bank, i=e_, t_=th:
              e.tensor_scalar_mul(
                  gpack[0:64, 1024 * i + 512 * t_:1024 * i + 512 * t_ + 512],
                  b[0:64, :], ce_sb[0:64, i:i + 1]),
              inc=t_dve, waits=[mi_mm, mi_ce])
            mi_gev[gi] = (t_dve, t_dve.n)
        free[("db", sl)] = (t_pe, t_pe.n)
    mi_g = (t_dve, t_dve.n)
    if debug_stage == 41:
        return _finish(nc, P, stack)

    grp = [(SA, 0), (SA, 512), (SB, 0), (SB, 512),
           (OA, 0), (OB, 0), (SUMS, 0), (AUX, 0)]
    mi_outev = {}
    n_ot = 2 if debug_stage in (42, 43, 44) else 16
    skip_up = (debug_stage == 43)
    wbfH = [a3[:, 8192:12288], a3[:, 12288:16384]]
    wffH = [f1[:, 8192:12288], f1[:, 12288:16384]]
    upb = a1[0:64, 10240:12288]
    upb2 = a1[0:64, 12288:14336]
    mi_wc = {}
    mi_wread = {}
    for ot in range(n_ot):
        for hfd in range(2):
            E("sp", lambda e, d=wffH[hfd], o=ot, f=hfd:
              e.dma_start(out=d, in_=worgT[:, 8192 * o + 4096 * f:
                                           8192 * o + 4096 * f + 4096]),
              inc=L[6 + hfd], waits=[mi_wc.get(hfd), mi_g if ot == 0 else None,
                                     mi_dbg4 if ot == 0 else None]
              + (mi_x2tm if ot == 0 else []))
            mi_wf = (L[6 + hfd], L[6 + hfd].n)
            E("pool", lambda e, d=wbfH[hfd], s=wffH[hfd]: e.tensor_copy(out=d, in_=s),
              inc=t_pool, waits=[mi_wf, mi_wread.get(hfd)])
            mi_wc[hfd] = (t_pool, t_pool.n)
        usl = ot % 2
        po = 256 * usl
        upx = upb if usl == 0 else upb2
        upff = f1[0:64, 2048 * usl:2048 * usl + 2048]
        E("sp", lambda e, d=upff, o=ot:
          e.dma_start(out=d, in_=upw[:, 2048 * o:2048 * o + 2048]),
          inc=L[8 + usl], waits=[free.get(("uf", usl)), mi_g if ot < 2 else None]
          + (mi_x2tm if ot < 2 else []))
        mi_uf = (L[8 + usl], L[8 + usl].n)
        E("pool", lambda e, d=upx, s=upff: e.tensor_copy(out=d, in_=s),
          inc=t_pool, waits=[mi_uf, free.get(("ub", usl))])
        mi_up = (t_pool, t_pool.n)
        free[("uf", usl)] = mi_up
        mi_xg = {}
        for tt in range(4):
            E("sp", lambda e, d=f1[:, 4096 + 256 * tt:4096 + 256 * tt + 256], o=ot, t_=tt:
              e.dma_start(out=d, in_=x2tm_d[128 * t_:128 * t_ + 128, 256 * o:256 * o + 256]),
              inc=L[tt], waits=(mi_x2tm if ot == 0 else []) + [free.get(("ox", tt))])
            mi_xg[tt] = (L[tt], L[tt].n)
        mi_tt = {}
        for cc in range(NCC):
            hfd = cc // 16
            for tt in range(8):
                bnk, boff = grp[tt]
                w = []
                if cc % 16 == 0 and tt == 0:
                    w.append(mi_wc[hfd])
                if cc == 0:
                    w += [mi_g, mi_outev.get((usl, tt))]
                last = (skip_up and cc == NCC - 1)
                E("pe", lambda e, b=bnk, o_=boff + po, cc_=cc, t_=tt, h_=hfd,
                  st=(cc == 0), sp_=last:
                  e.matmul(b[:, o_:o_ + 256],
                           a2[:, 1024 * cc_ + 128 * t_:1024 * cc_ + 128 * t_ + 128],
                           wbfH[h_][:, 256 * (cc_ % 16):256 * (cc_ % 16) + 256],
                           start=st, stop=sp_),
                  inc=t_pe if ((tt == 7 and cc % 16 == 15) or last) else None,
                  waits=w)
                if last:
                    mi_tt[tt] = (t_pe, t_pe.n)
            if cc % 16 == 15:
                mi_wread[cc // 16] = (t_pe, t_pe.n)
        if not skip_up:
            for e_ in range(8):
                for tt in range(8):
                    bnk, boff = grp[tt]
                    E("pe", lambda e, b=bnk, o_=boff + po, i=e_, t_=tt,
                      sp_=(e_ == 7), u=upx:
                      e.matmul(b[:, o_:o_ + 256],
                               gpack[0:64, 1024 * i + 128 * t_:1024 * i + 128 * t_ + 128],
                               u[0:64, 256 * i:256 * i + 256],
                               start=False, stop=sp_),
                      inc=t_pe if e_ == 7 else None,
                      waits=[mi_up] if (e_ == 0 and tt == 0) else [])
                    if e_ == 7:
                        mi_tt[tt] = (t_pe, t_pe.n)
        for tt in range(8):
            bnk, boff = grp[tt]
            off = boff + po
            mm_gate = mi_tt[tt]
            xsl = f1[:, 4096 + 256 * (tt % 4):4096 + 256 * (tt % 4) + 256]
            och = f1[:, 7168 + 256 * (tt % 4):7168 + 256 * (tt % 4) + 256]
            mi_x = mi_xg[tt % 4]
            E("dve", lambda e, d=och, b=bnk, o_=off, x=xsl:
              e.tensor_add(d, b[:, o_:o_ + 256], x),
              inc=t_dve, waits=[mm_gate, mi_x, free.get(("oo", tt % 4))])
            mi_o = (t_dve, t_dve.n)
            mi_outev[(ot % 2, tt)] = mi_o
            if tt < 4:
                E("sp", lambda e, d=xsl, o=ot, t_=tt + 4:
                  e.dma_start(out=d, in_=x2tm_d[128 * t_:128 * t_ + 128,
                                                256 * o:256 * o + 256]),
                  inc=L[tt], waits=[mi_o])
                mi_xg[tt] = (L[tt], L[tt].n)
            else:
                free[("ox", tt % 4)] = mi_o
            E("act", lambda e, s=och, o=ot, t_=tt:
              e.dma_start(out=out[128 * t_:128 * t_ + 128, 256 * o:256 * o + 256], in_=s),
              inc=Strk[tt % 4], waits=[mi_o])
            free[("oo", tt % 4)] = (Strk[tt % 4], Strk[tt % 4].n)
        free[("ub", usl)] = (t_pe, t_pe.n)

    return _finish(nc, P, stack)


def _finish(nc, P, stack):
    semstack = contextlib.ExitStack()
    sems = {name: semstack.enter_context(nc.semaphore(f"sem_{name}"))
            for name in P.tracks}
    final = [(t, t.n) for t in P.tracks.values() if t.n > 0]
    P.emit("sp", lambda e: e.nop(), waits=final)
    with nc.Block() as block:
        P.realize(block, sems)
    semstack.close()
    stack.close()
    return nc



def _prep(x, w_attn, w_proj, adapter_wte, gating, W_org, lora_down, lora_up,
          lora_keys, Wq):
    x = np.asarray(x, dtype=np.float32).reshape(T, C)
    xT = np.ascontiguousarray(x.T)
    w_attn = np.asarray(w_attn, dtype=np.float32)
    theta = 1.0 / (10000.0 ** (np.arange(0, HD, 2, dtype=np.float32) / HD))
    idx = np.outer(np.arange(T, dtype=np.float32), theta).astype(np.float32)
    idx2 = np.concatenate([idx, idx], axis=-1)
    cosT = np.ascontiguousarray(np.cos(idx2).astype(np.float32).T)
    sinST = np.ascontiguousarray(np.sin(idx2).astype(np.float32).T).copy()
    sinST[0:64, :] *= -1.0
    mk = np.zeros((4, 128, S), dtype=np.float32)
    qq = np.arange(S)[None, :]
    kk = np.arange(128)[:, None]
    for i, o in enumerate((0, 128, 256, 384)):
        mk[i] = np.where(qq >= kk + o, 0.0, NEG).astype(np.float32)
    shared = {
        "wprojT": np.ascontiguousarray(
            np.asarray(w_proj, np.float32).T.reshape(32, 128, 32, 128)
            .transpose(1, 2, 0, 3).reshape(128, 32 * 4096)),
        "worgT": np.ascontiguousarray(
            np.asarray(W_org, np.float32).T.reshape(32, 128, 16, 256)
            .transpose(1, 2, 0, 3).reshape(128, 16 * 8192)),
        "downT": np.ascontiguousarray(
            np.asarray(lora_down, np.float32).reshape(8, 32, 128, 64)
            .transpose(2, 0, 1, 3).reshape(128, 8 * 2048)),
        "upw": np.ascontiguousarray(
            np.asarray(lora_up, np.float32).reshape(8, 64, 16, 256)
            .transpose(1, 2, 0, 3).reshape(64, 16 * 2048)),
        "wqT": np.ascontiguousarray(
            np.asarray(Wq, np.float32).T.reshape(32, 128, 128)
            .transpose(1, 0, 2).reshape(128, 4096)),
        "keysT": np.ascontiguousarray(np.asarray(lora_keys, np.float32).T),
        "adT": np.ascontiguousarray(np.asarray(adapter_wte, np.float32).T),
        "gat": np.asarray(gating, np.float32).reshape(1, 1),
        "cosT": cosT,
        "sinS": sinST,
        "masks": mk,
    }
    in_maps = []
    for r in range(NCORES):
        im = dict(shared)
        im["xT"] = np.ascontiguousarray(xT[:, TOWN * r:TOWN * (r + 1)])
        rows = np.concatenate([w_attn[HC * r:HC * (r + 1)],
                               w_attn[C + HC * r:C + HC * (r + 1)],
                               w_attn[2 * C + HC * r:2 * C + HC * (r + 1)]], axis=0)
        im["wqkvT"] = np.ascontiguousarray(
            rows.T.reshape(32, 128, 12, 128).transpose(1, 2, 0, 3).reshape(128, 12 * 4096))
        in_maps.append(im)
    return in_maps


def kernel(L0_attn_x, L0_attn_w_attn, L0_attn_w_proj, L0_attn_adapter_wte,
           L0_attn_gating, L0_ffn_W_org, L0_ffn_lora_down, L0_ffn_lora_up,
           L0_ffn_lora_keys, L0_ffn_Wq):
    in_maps = _prep(L0_attn_x, L0_attn_w_attn, L0_attn_w_proj, L0_attn_adapter_wte,
                    L0_attn_gating, L0_ffn_W_org, L0_ffn_lora_down, L0_ffn_lora_up,
                    L0_ffn_lora_keys, L0_ffn_Wq)
    nc = _build(DEBUG_STAGE)
    res = run_bass_kernel_spmd(nc, in_maps, core_ids=list(range(NCORES)))
    if DEBUG_STAGE:
        kernel.last_results = res
        if "out" not in res.results[0]:
            return None
    outs = [res.results[r]["out"] for r in range(NCORES)]
    return np.concatenate(outs, axis=0).reshape(1, T, C)


# version 11
# speedup vs baseline: 1.1160x; 1.0172x over previous
import contextlib
import math
import numpy as np
import concourse.bass as bass
import concourse.mybir as mybir
from concourse.bass_utils import run_bass_kernel_spmd

F32 = mybir.dt.float32
BF16 = mybir.dt.bfloat16
AF = mybir.ActivationFunctionType
ALU = mybir.AluOpType
AX = mybir.AxisListType.X

NCORES = 8
T = 8192
C = 4096
HD = 128
HPC = 4
HC = HPC * HD
TOWN = T // NCORES
S = 512
NTS = T // S
NCC = C // 128
SCALE_MAIN = 1.0 / math.sqrt(HD)
SCALE_AD = 1.0 / math.sqrt(C)
EPS = 1e-5
NEG = -1.0e30

DEBUG_STAGE = 0


class _Trk:
    __slots__ = ("name", "step", "n")

    def __init__(self, name, step):
        self.name, self.step, self.n = name, step, 0

    def bump(self):
        self.n += self.step
        return self.n


class _Fut:
    __slots__ = ("trk", "val")

    def __init__(self):
        self.trk, self.val = None, None


class _Prog:
    ENGS = ("pe", "act", "dve", "pool", "sp")

    def __init__(self):
        self.ops = {e: [] for e in self.ENGS}
        self.tracks = {}

    def trk(self, name, step=1):
        if name not in self.tracks:
            self.tracks[name] = _Trk(name, step)
        return self.tracks[name]

    SERIAL = ("dve", "act", "pool")

    def emit(self, eng, fn, inc=None, waits=()):
        ws = [w for w in waits
              if w is not None and (isinstance(w, _Fut) or w[1])]
        swait = None
        if eng in self.SERIAL:
            et = self.trk(eng)
            if inc is None:
                inc = et
            if et.n > 0:
                swait = (et, et.n)
        if swait is not None:
            ws.append(swait)
        self.ops[eng].append((fn, inc, inc.bump() if inc is not None else None, ws))
        return (inc, inc.n) if inc is not None else None

    def realize(self, block, sems):
        def runner(ename):
            def go(e):
                lw = {}
                for fn, trk, _val, waits in self.ops[ename]:
                    need = {}
                    for w in waits:
                        if isinstance(w, _Fut):
                            if w.trk is None or not w.val:
                                continue
                            t, v = w.trk, w.val
                        else:
                            t, v = w
                        if lw.get(t.name, 0) < v:
                            need[t.name] = max(need.get(t.name, 0), v)
                    for tn, v in need.items():
                        e.wait_ge(sems[tn], v)
                        lw[tn] = v
                    ins = fn(e)
                    if trk is not None:
                        ins.then_inc(sems[trk.name], trk.step)
            return go
        block.tensor(runner("pe"))
        block.scalar(runner("act"))
        block.vector(runner("dve"))
        block.gpsimd(runner("pool"))
        block.sync(runner("sp"))


def _build(debug_stage=0, shared_cc=True):
    nc = bass.Bass("TRN2", target_bir_lowering=False, num_devices=NCORES)
    RG = [list(range(NCORES))]

    xT = nc.dram_tensor("xT", [C, TOWN], F32, kind="ExternalInput").ap()
    wqkvT = nc.dram_tensor("wqkvT", [128, 12 * 4096], F32, kind="ExternalInput").ap()
    wprojT = nc.dram_tensor("wprojT", [128, 32 * 4096], F32, kind="ExternalInput").ap()
    worgT = nc.dram_tensor("worgT", [128, 16 * 8192], F32, kind="ExternalInput").ap()
    downT = nc.dram_tensor("downT", [128, 8 * 2048], F32, kind="ExternalInput").ap()
    upw = nc.dram_tensor("upw", [64, 16 * 2048], F32, kind="ExternalInput").ap()
    wqT = nc.dram_tensor("wqT", [128, 4096], F32, kind="ExternalInput").ap()
    keysT = nc.dram_tensor("keysT", [128, 8], F32, kind="ExternalInput").ap()
    adT = nc.dram_tensor("adT", [C, 10], F32, kind="ExternalInput").ap()
    gat = nc.dram_tensor("gat", [1, 1], F32, kind="ExternalInput").ap()
    cosT = nc.dram_tensor("cosT", [128, T], F32, kind="ExternalInput").ap()
    sinS = nc.dram_tensor("sinS", [128, T], F32, kind="ExternalInput").ap()
    masks = nc.dram_tensor("masks", [4, 128, S], F32, kind="ExternalInput").ap()
    out = nc.dram_tensor("out", [TOWN, C], F32, kind="ExternalOutput").ap()

    h1own_d = nc.dram_tensor("h1own_d", [256, 16384], BF16, kind="Internal").ap()
    h1all_d = nc.dram_tensor("h1all_d", [2048, 16384], BF16, kind="Internal", addr_space="Shared" if shared_cc else "Local").ap()
    wqkv_bf_d = nc.dram_tensor("wqkv_bf_d", [128, 12 * 4096], BF16, kind="Internal").ap()
    q_d = nc.dram_tensor("q_d", [HC, T], BF16, kind="Internal").ap()
    k_d = nc.dram_tensor("k_d", [HC, T], BF16, kind="Internal").ap()
    v_d = nc.dram_tensor("v_d", [128, 32768], BF16, kind="Internal").ap()
    yown_d = nc.dram_tensor("yown_d", [8 * HC, 1024], BF16, kind="Internal").ap()
    yall_d = nc.dram_tensor("yall_d", [8 * NCORES * HC, 1024], BF16, kind="Internal", addr_space="Shared" if shared_cc else "Local").ap()
    x2fm_d = nc.dram_tensor("x2fm_d", [128, 32768], F32, kind="Internal").ap()
    x2tm_d = nc.dram_tensor("x2tm_d", [TOWN, C], F32, kind="Internal").ap()
    bc_d = nc.dram_tensor("bc_d", [4, 512], BF16, kind="Internal").ap()
    hsin_d = nc.dram_tensor("hsin_d", [128, NCC], F32, kind="Internal").ap()
    hsout_d = nc.dram_tensor("hsout_d", [128, NCC], F32, kind="Internal").ap()

    dbg = {}
    if debug_stage >= 1:
        dbg["h1all"] = nc.dram_tensor("dbg_h1all", [2048, 16384], BF16, kind="ExternalOutput").ap()
    if debug_stage >= 2:
        dbg["q"] = nc.dram_tensor("dbg_q", [HC, T], BF16, kind="ExternalOutput").ap()
        dbg["k"] = nc.dram_tensor("dbg_k", [HC, T], BF16, kind="ExternalOutput").ap()
        dbg["v"] = nc.dram_tensor("dbg_v", [128, 32768], BF16, kind="ExternalOutput").ap()
        dbg["ak"] = nc.dram_tensor("dbg_ak", [128, 40], BF16, kind="ExternalOutput").ap()
        dbg["av"] = nc.dram_tensor("dbg_av", [16, 512], BF16, kind="ExternalOutput").ap()
        dbg["wbf"] = nc.dram_tensor("dbg_wbf", [C, 3 * HC], BF16, kind="ExternalOutput").ap()
    if debug_stage >= 3:
        dbg["yall"] = nc.dram_tensor("dbg_yall", [8 * NCORES * HC, 1024], BF16, kind="ExternalOutput").ap()
        dbg["P0"] = nc.dram_tensor("dbg_P0", [128, 1024], BF16, kind="ExternalOutput").ap()
        dbg["rows"] = nc.dram_tensor("dbg_rows", [8, 512], F32, kind="ExternalOutput").ap()
        dbg["expA"] = nc.dram_tensor("dbg_expA", [16, 512], BF16, kind="ExternalOutput").ap()
        dbg["expA2"] = nc.dram_tensor("dbg_expA2", [16, 512], BF16, kind="ExternalOutput").ap()
        dbg["oev"] = nc.dram_tensor("dbg_oev", [128, 512], F32, kind="ExternalOutput").ap()
    if debug_stage >= 4:
        dbg["x2"] = nc.dram_tensor("dbg_x2", [TOWN, C], F32, kind="ExternalOutput").ap()
        dbg["h2"] = nc.dram_tensor("dbg_h2", [C, TOWN], BF16, kind="ExternalOutput").ap()
        dbg["coeffs"] = nc.dram_tensor("dbg_coeffs", [1, 8], F32, kind="ExternalOutput").ap()

    ident_d = nc.inline_tensor(np.eye(128, dtype=np.float32), name="ident_c").ap()

    P = _Prog()
    t_pe = P.trk("pe")
    t_act = P.trk("act")
    t_dve = P.trk("dve")
    t_pool = P.trk("pool")
    L = {i: P.trk(f"L{i}", 16) for i in range(10)}
    Strk = {i: P.trk(f"S{i}", 16) for i in range(6)}
    t_li = P.trk("Li", 16)
    t_fr = [P.trk("FR0", 16), P.trk("FR1", 16)]
    t_rr = [P.trk("RR0", 16), P.trk("RR1", 16)]
    t_fb = [P.trk("FB0", 16), P.trk("FB1", 16)]
    t_rb = [P.trk("RB0", 16), P.trk("RB1", 16)]
    t_ys = [P.trk("YS0", 16), P.trk("YS1", 16)]
    t_ld = P.trk("Ld", 16)
    t_ld2 = P.trk("Ld2", 16)
    t_cc = P.trk("cc", 1)

    stack = contextlib.ExitStack()
    ar1 = stack.enter_context(nc.sbuf_tensor("ar1", [128, 32768], BF16))
    ar2 = stack.enter_context(nc.sbuf_tensor("ar2", [128, 32768], BF16))
    ar3 = stack.enter_context(nc.sbuf_tensor("ar3", [128, 16384], BF16))
    ones_bf = stack.enter_context(nc.sbuf_tensor("ones_bf", [128, 1], BF16))
    onesr_bf = stack.enter_context(nc.sbuf_tensor("onesr_bf", [1, 128], BF16))
    onesr_f = stack.enter_context(nc.sbuf_tensor("onesr_f", [1, 128], F32))
    ident_sb = stack.enter_context(nc.sbuf_tensor("ident_sb", [128, 128], F32))
    akT_sb = stack.enter_context(nc.sbuf_tensor("akT_sb", [128, 40], BF16))
    av_sb = stack.enter_context(nc.sbuf_tensor("av_sb", [16, 512], BF16))
    adT_sb = stack.enter_context(nc.sbuf_tensor("adT_sb", [128, 320], BF16))
    g_sb = stack.enter_context(nc.sbuf_tensor("g_sb", [1, 1], F32))
    ce_sb = stack.enter_context(nc.sbuf_tensor("ce_sb", [128, 8], F32))
    keys_sb = stack.enter_context(nc.sbuf_tensor("keys_sb", [128, 8], F32))
    qT_sb = stack.enter_context(nc.sbuf_tensor("qT_sb", [128, 1], F32))
    hsum_sb = stack.enter_context(nc.sbuf_tensor("hsum_sb", [128, NCC], F32))
    mrow_sb = stack.enter_context(nc.sbuf_tensor("mrow_sb", [1, 4096], F32))
    srow_sb = stack.enter_context(nc.sbuf_tensor("srow_sb", [1, 1024], F32))
    mask_sb = stack.enter_context(nc.sbuf_tensor("mask_sb", [128, 4 * S], F32))

    a1 = ar1.ap()
    a2 = ar2.ap()
    a3 = ar3.ap()
    f1 = ar1.ap().bitcast(F32)
    f2 = ar2.ap().bitcast(F32)
    f3 = ar3.ap().bitcast(F32)

    SA = stack.enter_context(nc.psum_tensor("SA", [128, 1024], F32))
    SB = stack.enter_context(nc.psum_tensor("SB", [128, 1024], F32))
    OA = stack.enter_context(nc.psum_tensor("OA", [128, 512], F32))
    OB = stack.enter_context(nc.psum_tensor("OB", [128, 512], F32))
    SUMS = stack.enter_context(nc.psum_tensor("SUMS", [128, 512], F32))
    AUX = stack.enter_context(nc.psum_tensor("AUX", [128, 512], F32))

    E = P.emit

    E("pool", lambda e: e.memset(ones_bf[:, :], 1.0))
    E("pool", lambda e: e.memset(onesr_bf[:, :], 1.0))
    E("pool", lambda e: e.memset(onesr_f[:, :], 1.0))
    E("pool", lambda e: e.memset(av_sb[:, :], 0.0), inc=t_pool)
    mi_ones = (t_pool, t_pool.n)
    E("sp", lambda e: e.dma_start(out=ident_sb[:, :], in_=ident_d), inc=t_li)
    E("sp", lambda e: e.dma_start(
        out=mask_sb[:, :].rearrange("p (m s) -> p m s", m=4),
        in_=masks.rearrange("m p s -> p m s")), inc=t_li)
    E("sp", lambda e: e.dma_start(out=g_sb[:, :], in_=gat), inc=t_li)
    E("sp", lambda e: e.dma_start(out=keys_sb[:, :], in_=keysT), inc=t_li)
    E("sp", lambda e: e.dma_start(
        out=f3[:, 0:320].rearrange("p (cc a) -> p cc a", a=10),
        in_=adT.rearrange("(cc p) a -> p cc a", p=128)), inc=t_li)
    mi_init = (t_li, t_li.n)
    mi_ident = mi_masks = mi_gat = mi_keys = mi0 = mi_init
    E("pool", lambda e: e.tensor_copy(out=adT_sb[:, :], in_=f3[:, 0:320]),
      inc=t_pool, waits=[mi0])
    mi_adT = (t_pool, t_pool.n)

    free = {}
    for cc in range(NCC):
        sl = cc % 2
        xf = f3[:, 512 + 1024 * sl: 512 + 1024 * sl + 1024]
        xb = a1[:, 1024 * sl:1024 * sl + 1024]
        sq = a1[:, 2048 + 1024 * sl:2048 + 1024 * sl + 1024]
        E("sp", lambda e, d=xf, s=xT[128 * cc:128 * cc + 128, :]:
          e.dma_start(out=d, in_=s), inc=L[sl],
          waits=[free.get(("a0x", sl)), mi_adT if cc < 2 else None])
        mi_x = (L[sl], L[sl].n)
        E("pool", lambda e, d=xb, s=xf: e.tensor_copy(out=d, in_=s),
          inc=t_pool, waits=[mi_x, free.get(("a0b", sl))])
        mi_xb = (t_pool, t_pool.n)
        free[("a0x", sl)] = mi_xb
        E("act", lambda e, d=sq, s=xb: e.activation(out=d, in_=s, func=AF.Square),
          inc=t_act, waits=[mi_xb, free.get(("a0s", sl))])
        mi_sq = (t_act, t_act.n)
        st, sp_ = (cc == 0), (cc == NCC - 1)
        for h in range(2):
            bank = SUMS if h == 0 else AUX
            E("pe", lambda e, b=bank, r=xb, h_=h, st_=st, sp__=sp_:
              e.matmul(b[0:1, :], ones_bf[:, :], r[:, 512 * h_:512 * h_ + 512],
                       start=st_, stop=sp__),
              waits=[mi_xb, mi_ones] if (h == 0) else [])
            E("pe", lambda e, b=bank, r=sq, h_=h, st_=st, sp__=sp_:
              e.matmul(b[32:33, :], ones_bf[:, :], r[:, 512 * h_:512 * h_ + 512],
                       start=st_, stop=sp__),
              inc=t_pe if h == 1 else None, waits=[mi_sq] if h == 0 else [])
        free[("a0b", sl)] = (t_pe, t_pe.n)
        free[("a0s", sl)] = (t_pe, t_pe.n)
    mi_stats1 = (t_pe, t_pe.n)

    def ln_rows(mi_done, roff):
        out_mis = []
        for h in range(2):
            bank = SUMS if h == 0 else AUX
            m_h = mrow_sb[0:1, roff + 512 * h: roff + 512 * h + 512]
            r_h = mrow_sb[0:1, roff + 1024 + 512 * h: roff + 1024 + 512 * h + 512]
            tmp = srow_sb[0:1, 512 * h:512 * h + 512]
            E("act", lambda e, d=m_h, b=bank: e.mul(out=d, in_=b[0:1, :], mul=1.0 / C),
              inc=t_act, waits=[mi_done])
            mi_m = (t_act, t_act.n)
            E("act", lambda e, d=r_h, b=bank: e.mul(out=d, in_=b[32:33, :], mul=1.0 / C),
              inc=t_act)
            mi_e2 = (t_act, t_act.n)
            E("dve", lambda e, d=tmp, s=m_h: e.tensor_mul(d, s, s),
              inc=t_dve, waits=[mi_m])
            E("dve", lambda e, d=r_h, s=tmp: e.tensor_sub(d, d, s),
              inc=t_dve, waits=[mi_e2])
            E("dve", lambda e, d=r_h: e.tensor_scalar_add(d, d, EPS), inc=t_dve)
            mi_v = (t_dve, t_dve.n)
            E("act", lambda e, d=r_h: e.activation(out=d, in_=d, func=AF.Sqrt),
              inc=t_act, waits=[mi_v])
            mi_s = (t_act, t_act.n)
            E("dve", lambda e, d=r_h: e.reciprocal(d, d), inc=t_dve, waits=[mi_s])
            out_mis.append((t_dve, t_dve.n))
        return out_mis[-1]

    mi_rows1 = ln_rows(mi_stats1, 0)

    mi_ag_h1h = {}
    prev_half_done = None
    for h in range(2):
        m_h = mrow_sb[0:1, 512 * h:512 * h + 512]
        r_h = mrow_sb[0:1, 1024 + 512 * h:1024 + 512 * h + 512]
        E("pe", lambda e, r=m_h: e.matmul(OA[:, :], onesr_f[:, :], r, start=True, stop=True),
          inc=t_pe, waits=[mi_rows1, prev_half_done])
        E("pe", lambda e, r=r_h: e.matmul(OB[:, :], onesr_f[:, :], r, start=True, stop=True),
          inc=t_pe)
        mi_bc = (t_pe, t_pe.n)
        for cc in range(NCC):
            sl = cc % 2
            xf = f3[:, 512 + 512 * sl:512 + 512 * sl + 512]
            tm = f3[:, 1536 + 512 * sl:1536 + 512 * sl + 512]
            hb = a1[:, 4096 + 512 * sl:4096 + 512 * sl + 512]
            E("sp", lambda e, d=xf, s=xT[128 * cc:128 * cc + 128, 512 * h:512 * h + 512]:
              e.dma_start(out=d, in_=s), inc=L[sl],
              waits=[free.get(("nx", sl)), mi_stats1 if cc < 2 else None])
            mi_x = (L[sl], L[sl].n)
            E("dve", lambda e, d=tm, s=xf: e.tensor_sub(d, s, OA[:, :]),
              inc=t_dve, waits=[mi_x, mi_bc, free.get(("ntm", sl))])
            mi_sub = (t_dve, t_dve.n)
            free[("nx", sl)] = mi_sub
            E("dve", lambda e, d=hb, s=tm: e.tensor_mul(d, s, OB[:, :]),
              inc=t_dve, waits=[free.get(("nh", sl))])
            mi_h1 = (t_dve, t_dve.n)
            free[("ntm", sl)] = mi_h1
            E("act", lambda e, s=hb, cc_=cc, h_=h:
              e.dma_start(out=h1own_d[128 * h_:128 * h_ + 128, 512 * cc_:512 * cc_ + 512], in_=s),
              inc=Strk[sl], waits=[mi_h1])
            free[("nh", sl)] = (Strk[sl], Strk[sl].n)
        prev_half_done = (t_dve, t_dve.n)
        h_mi = [(Strk[0], Strk[0].n), (Strk[1], Strk[1].n)]
        E("pool", lambda e, h_=h: e.collective_compute(
            "AllGather", ALU.bypass, replica_groups=RG,
            ins=[h1own_d[128 * h_:128 * h_ + 128, :].opt()],
            outs=[h1all_d[1024 * h_:1024 * h_ + 1024, :].opt()]),
          inc=t_cc, waits=h_mi)
        mi_ag_h1h[h] = (t_cc, t_cc.n)
        mi_a0h1 = h_mi

    mi_wspill = None
    for i in range(12):
        sl = i % 2
        wf = f1[:, 4096 + 4096 * sl:4096 + 4096 * sl + 4096]
        wb = a1[:, 24576 + 4096 * sl:24576 + 4096 * sl + 4096]
        E("sp", lambda e, d=wf, s=wqkvT[:, 4096 * i:4096 * i + 4096]:
          e.dma_start(out=d, in_=s),
          inc=L[2 + sl], waits=[free.get(("wsf", sl))])
        mi_w = (L[2 + sl], L[2 + sl].n)
        E("dve", lambda e, d=wb, s=wf: e.tensor_copy(out=d, in_=s),
          inc=t_dve, waits=[mi_w, free.get(("wsb", sl))])
        mi_wb = (t_dve, t_dve.n)
        free[("wsf", sl)] = mi_wb
        E("act", lambda e, s=wb, i_=i:
          e.dma_start(out=wqkv_bf_d[:, 4096 * i_:4096 * i_ + 4096], in_=s),
          inc=Strk[2 + sl], waits=[mi_wb])
        free[("wsb", sl)] = (Strk[2 + sl], Strk[2 + sl].n)
    mi_wspill = [(Strk[2], Strk[2].n), (Strk[3], Strk[3].n)]

    if debug_stage >= 1:
        E("pool", lambda e: e.dma_start(out=dbg["h1all"], in_=h1all_d),
          inc=t_ld, waits=[mi_ag_h1h[0], mi_ag_h1h[1]])
    if debug_stage == 1:
        return _finish(nc, P, stack)

    qkv_banks = [OA[:, :], OB[:, :], SA[:, 0:512], SA[:, 512:1024]]
    h1_slots = [a1[:, 0:16384], a1[:, 16384:32768], a2[:, 13312:29696]]
    mi_adchain = mi_adT
    mi_h1slot = {}
    for it in range(8):
        hfo, q = it // 4, it % 4
        mi_h1s = {}
        for hf in range(2):
            ts = 2 * (2 * q + hf) + hfo
            slot = (2 * it + hf) % 3
            h1s = h1_slots[slot]
            E("sp", lambda e, d=h1s, r=1024 * hfo + 128 * (2 * q + hf):
              e.dma_start(out=d, in_=h1all_d[r:r + 128, :]),
              inc=L[slot], waits=[mi_ag_h1h[hfo], mi_h1slot.get(slot)]
              + (mi_wspill + mi_a0h1 if 2 * it + hf < 3 else []))
            mi_h1s[hf] = ((L[slot], L[slot].n), h1s, slot)
        csl = {}
        cs_trk = {(0, 0): L[4], (0, 1): L[5], (1, 0): L[8], (1, 1): L[9]}
        for hf in range(2):
            ts = 2 * (2 * q + hf) + hfo
            base = 3072 + 2048 * (q % 2) + 1024 * hf
            ctk = cs_trk[(q % 2, hf)]
            E("sp", lambda e, d=f3[:, base:base + 512], s_=cosT[:, 512 * ts:512 * ts + 512]:
              e.dma_start(out=d, in_=s_), inc=ctk, waits=[free.get(("cs", q % 2, hf))])
            E("sp", lambda e, d=f3[:, base + 512:base + 1024], s_=sinS[:, 512 * ts:512 * ts + 512]:
              e.dma_start(out=d, in_=s_), inc=ctk)
            csl[hf] = ((ctk, ctk.n), f3[:, base:base + 512], f3[:, base + 512:base + 1024])
        for ot in range(12):
            wsl = ot % 2
            wb = a2[:, 4096 * wsl:4096 * wsl + 4096]
            E("sp", lambda e, d=wb, o=ot:
              e.dma_start(out=d, in_=wqkv_bf_d[:, 4096 * o:4096 * o + 4096]),
              inc=L[6 + wsl], waits=(mi_wspill if it == 0 else []) + [free.get(("w", wsl))])
            mi_w = (L[6 + wsl], L[6 + wsl].n)
            bk0, bk1 = (0, 1) if ot % 2 == 0 else (2, 3)
            for cc in range(NCC):
                for hf in range(2):
                    bank = qkv_banks[bk0 + hf]
                    E("pe", lambda e, b=bank, w=wb, h_=mi_h1s[hf][1], cc_=cc,
                      st=(cc == 0), sp_=(cc == NCC - 1):
                      e.matmul(b, w[:, 128 * cc_:128 * cc_ + 128],
                               h_[:, 512 * cc_:512 * cc_ + 512], start=st, stop=sp_),
                      inc=t_pe if (cc == NCC - 1 and hf == 1) else None,
                      waits=[mi_w, mi_h1s[0][0], mi_h1s[1][0],
                             free.get(("qb", bk0)), free.get(("qb", bk0 + 1))]
                      if (cc == 0 and hf == 0) else [])
            mi_mm = (t_pe, t_pe.n)
            free[("w", wsl)] = mi_mm
            mi_h1slot[mi_h1s[0][2]] = mi_mm
            mi_h1slot[mi_h1s[1][2]] = mi_mm
            if it == 0 and ot >= 4:
                for cc in range(NCC):
                    E("pe", lambda e, w=wb, cc_=cc, st=(cc == 0), sp_=(cc == NCC - 1):
                      e.matmul(SUMS[:, 0:10], w[:, 128 * cc_:128 * cc_ + 128],
                               adT_sb[:, 10 * cc_:10 * cc_ + 10], start=st, stop=sp_),
                      inc=t_pe if cc == NCC - 1 else None,
                      waits=[mi_adchain] if cc == 0 else [])
                mi_amm = (t_pe, t_pe.n)
                free[("w", wsl)] = mi_amm
                if ot < 8:
                    E("dve", lambda e, o=ot: e.tensor_copy(
                        out=akT_sb[:, 10 * (o - 4):10 * (o - 4) + 10], in_=SUMS[:, 0:10]),
                      inc=t_dve, waits=[mi_amm])
                    mi_adchain = (t_dve, t_dve.n)
                else:
                    avt = f3[:, 7168:7184]
                    E("dve", lambda e, d=avt: e.tensor_copy(out=d[:, 0:10], in_=SUMS[:, 0:10]),
                      inc=t_dve, waits=[mi_amm])
                    mi_ae = (t_dve, t_dve.n)
                    E("pe", lambda e, s_=avt: e.transpose(AUX[0:10, 0:128], s_[:, 0:10], ident_sb[:, :]),
                      inc=t_pe, waits=[mi_ae, mi_ident, free.get(("aux_a1", 0))])
                    mi_tr = (t_pe, t_pe.n)
                    E("act", lambda e, o=ot: e.copy(
                        out=av_sb[0:10, 128 * (o - 8):128 * (o - 8) + 128], in_=AUX[0:10, 0:128]),
                      inc=t_act, waits=[mi_tr])
                    mi_adchain = (t_act, t_act.n)
                    free[("aux_a1", 0)] = mi_adchain
            for hf in range(2):
                ts = 2 * (2 * q + hf) + hfo
                bank = qkv_banks[bk0 + hf]
                par = bk0 + hf
                if ot < 8:
                    ev = f3[:, 512 * hf:512 * hf + 512]
                    tt1 = f3[:, 1024 + 1024 * hf:1024 + 1024 * hf + 512]
                    ro = f3[:, 1024 + 1024 * hf + 512:1024 + 1024 * hf + 1024]
                    ob = a2[:, 8192 + 512 * hf:8192 + 512 * hf + 512]
                    E("dve", lambda e, d=ev, b=bank: e.tensor_copy(out=d, in_=b),
                      inc=t_dve, waits=[mi_mm, free.get(("ev", hf))])
                    mi_ev = (t_dve, t_dve.n)
                    free[("qb", par)] = mi_ev
                    cs_mi, cs_c, cs_s = csl[hf]
                    E("pool", lambda e, d=tt1, s_=ev, c_=cs_c: e.tensor_mul(d, s_, c_),
                      inc=t_pool, waits=[mi_ev, cs_mi, free.get(("t1", hf))])
                    E("pool", lambda e, d=ro, s_=ev: e.tensor_copy(out=d[0:64, :], in_=s_[64:128, :]))
                    E("pool", lambda e, d=ro, s_=ev: e.tensor_copy(out=d[64:128, :], in_=s_[0:64, :]),
                      inc=t_pool)
                    mi_rot = (t_pool, t_pool.n)
                    free[("ev", hf)] = mi_rot
                    E("pool", lambda e, d=ro, s_=cs_s: e.tensor_mul(d, d, s_), inc=t_pool)
                    E("pool", lambda e, d=ob, a_=tt1, b_=ro: e.tensor_add(d, a_, b_),
                      inc=t_pool, waits=[free.get(("ob", hf))])
                    mi_ob = (t_pool, t_pool.n)
                    free[("t1", hf)] = mi_ob
                    free[("cs", q % 2, hf)] = mi_ob
                    dst = q_d if ot < 4 else k_d
                    E("act", lambda e, s_=ob, d=dst, r=128 * (ot % 4), t0=512 * ts:
                      e.dma_start(out=d[r:r + 128, t0:t0 + 512], in_=s_),
                      inc=Strk[hf], waits=[mi_ob])
                    free[("ob", hf)] = (Strk[hf], Strk[hf].n)
                else:
                    ev = f3[:, 512 * hf:512 * hf + 512]
                    vsb = a2[:, 9216 + 2048 * hf:9216 + 2048 * hf + 2048]
                    tb = AUX if hf == 0 else SB[:, 0:512]
                    E("dve", lambda e, d=ev, b=bank: e.tensor_copy(out=d, in_=b),
                      inc=t_dve, waits=[mi_mm, free.get(("ev", hf))])
                    mi_ev = (t_dve, t_dve.n)
                    free[("qb", par)] = mi_ev
                    for j in range(4):
                        E("pe", lambda e, s_=ev, j_=j, b=tb: e.transpose(
                            b[:, 128 * j_:128 * j_ + 128], s_[:, 128 * j_:128 * j_ + 128],
                            ident_sb[:, :]),
                          inc=t_pe if j == 3 else None,
                          waits=[mi_ev, mi_ident, free.get(("aux_a1", hf))] if j == 0 else [])
                    mi_tr = (t_pe, t_pe.n)
                    free[("ev", hf)] = mi_tr
                    for j in range(4):
                        E("act", lambda e, d=vsb, j_=j, c0=128 * (ot - 8), b=tb:
                          e.copy(out=d[:, 512 * j_ + c0:512 * j_ + c0 + 128],
                                 in_=b[:, 128 * j_:128 * j_ + 128]),
                          inc=t_act if j == 3 else None,
                          waits=[mi_tr, free.get(("vsb", hf)) if j == 0 else None]
                          if j == 0 else [])
                    mi_vs = (t_act, t_act.n)
                    free[("aux_a1", hf)] = mi_vs
                    if ot == 11:
                        E("act", lambda e, s_=vsb, t_=ts:
                          e.dma_start(out=v_d[:, 2048 * t_:2048 * t_ + 2048], in_=s_),
                          inc=Strk[2 + hf], waits=[mi_vs])
                        free[("vsb", hf)] = (Strk[2 + hf], Strk[2 + hf].n)
    mi_a1_st = [(Strk[i], Strk[i].n) for i in range(4)]
    mi_a1 = mi_a1_st + [(t_pe, t_pe.n), (t_pool, t_pool.n), (t_dve, t_dve.n)]

    if debug_stage >= 2:
        for nm, src in (("q", q_d), ("k", k_d), ("v", v_d)):
            E("pool", lambda e, n=nm, s_=src: e.dma_start(out=dbg[n], in_=s_), inc=t_ld, waits=mi_a1)
        E("pool", lambda e: e.dma_start(out=dbg["ak"], in_=akT_sb[:, :]), inc=t_ld, waits=mi_a1)
        E("pool", lambda e: e.dma_start(out=dbg["av"], in_=av_sb[:, :]), inc=t_ld, waits=mi_a1)
        E("pool", lambda e: e.dma_start(out=dbg["wbf"], in_=wqkv_bf_d), inc=t_ld, waits=mi_a1)
    if debug_stage == 2:
        return _finish(nc, P, stack)

    E("sp", lambda e: e.dma_start(out=a1[:, :].rearrange("p (h t) -> p h t", h=4),
                                  in_=k_d.rearrange("(h p) t -> p h t", p=128)),
      inc=L[2], waits=mi_a1)
    mi_K = (L[2], L[2].n)
    E("sp", lambda e: e.dma_start(out=a2[:, :], in_=v_d),
      inc=L[3], waits=mi_a1)
    mi_V = (L[3], L[3].n)

    mi_sgrp = {}
    mi_pslot = {}
    mi_oband = {}
    mi_mfin = {}
    mi_expA = {}
    mi_expA2 = {}
    mi_fbc = {}
    mi_rbc = {}
    mi_ybf = {}
    mi_oev = {}
    deferred = []

    def flush_defer():
        for eng, fn, inc, waits, fut in deferred:
            mi = E(eng, fn, inc=inc, waits=waits)
            if fut is not None:
                fut.trk, fut.val = mi
        deferred.clear()

    def defer(eng, fn, inc, waits, fut=None):
        deferred.append((eng, fn, inc, waits, fut))

    ystore_futs = []
    for c in range(NTS):
        sl = c % 2
        qs = a3[:, 2048 * sl:2048 * sl + 2048]
        E("sp", lambda e, d=qs, c_=c:
          e.dma_start(out=d.rearrange("p (h t) -> p h t", h=4),
                      in_=q_d[:, 512 * c_:512 * c_ + 512].rearrange("(h p) t -> p h t", p=128)),
          inc=L[4 + sl], waits=mi_a1_st[:2] + [free.get(("qs", sl))])
        mi_q = (L[4 + sl], L[4 + sl].n)
        nkt = 4 * (c + 1)
        npair = nkt // 2
        for h in range(HPC):
            ph = h % 2
            qv = qs[:, 512 * h:512 * h + 512]
            oband = OA if ph == 0 else OB
            sbank = SUMS if ph == 0 else AUX

            def emit_S(p_, first=False):
                sgrp = SA if p_ % 2 == 0 else SB
                for jj in range(2):
                    j = 2 * p_ + jj
                    E("pe", lambda e, g=sgrp, h_=h, j_=j, o=512 * jj, q_=qv:
                      e.matmul(g[:, o:o + 512],
                               a1[:, 8192 * h_ + 128 * j_:8192 * h_ + 128 * j_ + 128],
                               q_, start=True, stop=True),
                      inc=t_pe if jj == 1 else None,
                      waits=[mi_K, mi_q, mi_sgrp.get(p_ % 2)] if jj == 0 else [])
                return (t_pe, t_pe.n)

            mi_s = {}
            mi_s[0] = emit_S(0)
            if npair > 1:
                mi_s[1] = emit_S(1)
            dp = min(2, npair - 1)
            for p_ in range(npair):
                sgrp = SA if p_ % 2 == 0 else SB
                mi_mask = None
                for jj in range(2):
                    o = 2 * p_ + jj - 4 * c
                    if 0 <= o < 4:
                        E("dve", lambda e, g=sgrp, off=512 * jj, o_=o:
                          e.tensor_add(g[:, off:off + 512], g[:, off:off + 512],
                                       mask_sb[:, 512 * o_:512 * o_ + 512]),
                          inc=t_dve, waits=[mi_s[p_], mi_masks])
                        mi_mask = (t_dve, t_dve.n)
                psl = p_ % 3
                pb = a3[:, 4096 + 1024 * psl:4096 + 1024 * psl + 1024]
                E("act", lambda e, d=pb, g=sgrp:
                  e.activation(out=d, in_=g[:, :], func=AF.Exp, scale=SCALE_MAIN),
                  inc=t_act, waits=[mi_s[p_], mi_mask, mi_pslot.get(psl)])
                mi_e = (t_act, t_act.n)
                mi_sgrp[p_ % 2] = mi_e
                if p_ + 2 < npair:
                    mi_s[p_ + 2] = emit_S(p_ + 2)
                if p_ == dp:
                    flush_defer()
                for jj in range(2):
                    j = 2 * p_ + jj
                    E("pe", lambda e, ob=oband, j_=j, h_=h, p__=pb:
                      e.matmul(ob[:, :], a2[:, 512 * j_ + 128 * h_:512 * j_ + 128 * h_ + 128],
                               p__[:, 512 * (j_ % 2):512 * (j_ % 2) + 512],
                               start=(j_ == 0), stop=False),
                      waits=([mi_e, mi_V] + ([mi_oband.get(ph)] if j == 0 else []))
                      if jj == 0 else [])
                    E("pe", lambda e, j_=j, p__=pb, b=sbank, nk=nkt:
                      e.matmul(b[0:1, :], ones_bf[:, :],
                               p__[:, 512 * (j_ % 2):512 * (j_ % 2) + 512],
                               start=(j_ == 0), stop=(j_ == nk - 1)),
                      inc=t_pe if jj == 1 else None,
                      waits=[mi_mfin.get(ph)] if j == 0 else [])
                mi_pslot[psl] = (t_pe, t_pe.n)
            if deferred:
                flush_defer()
            mi_kloop = (t_pe, t_pe.n)
            fut_as = _Fut()
            defer("pe",
                  (lambda h_, q_, b: lambda e:
                   e.matmul(b[32:42, :], akT_sb[:, 10 * h_:10 * h_ + 10], q_,
                            start=True, stop=True))(h, qv, sbank),
                  t_pe, [mi_expA.get(ph)], fut_as)
            last_fut_as = fut_as
            expA = a3[:, 7168 + 512 * ph:7168 + 512 * ph + 512]
            fut_ea = _Fut()
            defer("act",
                  (lambda d, b: lambda e:
                   e.activation(out=d[0:10, :], in_=b[32:42, :], func=AF.Exp,
                                scale=SCALE_AD))(expA, sbank),
                  t_act, [fut_as, mi_expA2.get(ph)], fut_ea)
            mi_expA[ph] = fut_ea
            fut_asum = _Fut()
            defer("pe",
                  (lambda s_, b: lambda e:
                   e.matmul(b[64:65, :], ones_bf[0:10, :], s_[0:10, :],
                            start=True, stop=True))(expA, sbank),
                  t_pe, [fut_ea], fut_asum)
            lnr = srow_sb[0:1, 512 * ph:512 * ph + 512]
            fut_ln1 = _Fut()
            defer("act",
                  (lambda d, b: lambda e:
                   e.activation(out=d, in_=b[64:65, :], func=AF.Ln))(lnr, sbank),
                  t_act, [fut_asum], fut_ln1)
            fut_ras = _Fut()
            defer("act",
                  (lambda d: lambda e:
                   e.activation(out=d, in_=d, func=AF.Exp, scale=-1.0))(lnr),
                  t_act, [fut_ln1], fut_ras)
            frow = a3[0:1, 14336 + 512 * ph:14336 + 512 * ph + 512]
            fut_f = _Fut()
            defer("dve",
                  (lambda d, r, b: lambda e:
                   e.scalar_tensor_tensor(d, b[0:1, :], g_sb[0:1, 0:1], r,
                                          op0=ALU.mult, op1=ALU.mult))(frow, lnr, sbank),
                  t_dve, [fut_ras, mi_gat, mi_fbc.get(ph)], fut_f)
            fut_fd = _Fut()
            defer("sp",
                  (lambda s_, p: lambda e:
                   e.dma_start(out=bc_d[p:p + 1, :], in_=s_))(frow, ph),
                  t_fr[ph], [fut_f, mi_fbc.get(ph)], fut_fd)
            fbc = a3[0:10, 9216 + 512 * ph:9216 + 512 * ph + 512]
            fut_fb = _Fut()
            defer("sp",
                  (lambda d, p: lambda e:
                   e.dma_start(out=d, in_=bc_d[p:p + 1, :].broadcast_to([10, 512])))(fbc, ph),
                  t_fb[ph], [fut_fd, mi_expA2.get(ph)], fut_fb)
            mi_fbc[ph] = fut_fb
            expA2 = a3[0:10, 8192 + 512 * ph:8192 + 512 * ph + 512]
            fut_e2 = _Fut()
            defer("pool",
                  (lambda d, a, f_: lambda e:
                   e.tensor_mul(d, a[0:10, :], f_))(expA2, expA, fbc),
                  t_pool, [fut_fb, fut_ea], fut_e2)
            mi_expA2[ph] = fut_e2
            rrow = a3[0:1, 15360 + 512 * ph:15360 + 512 * ph + 512]
            lnm = srow_sb[0:1, 512 * ph:512 * ph + 512]
            fut_ln2 = _Fut()
            defer("act",
                  (lambda d, b: lambda e:
                   e.activation(out=d, in_=b[0:1, :], func=AF.Ln))(lnm, sbank),
                  t_act, [fut_f], fut_ln2)
            fut_r = _Fut()
            defer("act",
                  (lambda d, s_: lambda e:
                   e.activation(out=d, in_=s_, func=AF.Exp, scale=-1.0))(rrow, lnm),
                  t_act, [fut_ln2], fut_r)
            mi_mfin[ph] = fut_r
            fut_rd = _Fut()
            defer("sp",
                  (lambda s_, p: lambda e:
                   e.dma_start(out=bc_d[2 + p:3 + p, :], in_=s_))(rrow, ph),
                  t_rr[ph], [fut_r, mi_rbc.get(ph)], fut_rd)
            rbc = a3[:, 10240 + 512 * ph:10240 + 512 * ph + 512]
            fut_rb = _Fut()
            defer("sp",
                  (lambda d, p: lambda e:
                   e.dma_start(out=d, in_=bc_d[2 + p:3 + p, :].broadcast_to([128, 512])))(rbc, ph),
                  t_rb[ph], [fut_rd, mi_rbc.get(ph)], fut_rb)
            fut_ao = _Fut()
            defer("pe",
                  (lambda h_, s_, ob: lambda e:
                   e.matmul(ob[:, :], av_sb[0:10, 128 * h_:128 * h_ + 128], s_[0:10, :],
                            start=False, stop=True))(h, expA2, oband),
                  t_pe, [fut_e2], fut_ao)
            oev = f3[:, 6144 + 512 * ph:6144 + 512 * ph + 512]
            fut_oe = _Fut()
            defer("dve",
                  (lambda d, ob: lambda e: e.tensor_copy(out=d, in_=ob[:, :]))(oev, oband),
                  t_dve, [fut_ao, mi_oev.get(ph)], fut_oe)
            ybf = a3[:, 11264 + 512 * ph:11264 + 512 * ph + 512]
            fut_ytt = _Fut()
            defer("pool",
                  (lambda d, s_, r_: lambda e: e.tensor_mul(d, s_, r_))(ybf, oev, rbc),
                  t_pool, [fut_rb, fut_oe, mi_ybf.get(ph)], fut_ytt)
            fut_yst = _Fut()
            defer("sp",
                  (lambda s_, h_, c_: lambda e:
                   e.dma_start(out=yown_d[512 * (c_ // 2) + 128 * h_:
                                          512 * (c_ // 2) + 128 * h_ + 128,
                                          512 * (c_ % 2):512 * (c_ % 2) + 512],
                               in_=s_))(ybf, h, c),
                  t_ys[ph], [fut_ytt], fut_yst)
            ystore_futs.append(fut_yst)
            mi_oband[ph] = fut_ytt
            mi_ybf[ph] = fut_yst
            mi_oev[ph] = fut_ytt
            mi_rbc[ph] = fut_ytt
        free[("qs", sl)] = last_fut_as
        if c % 2 == 1:
            k = c // 2
            defer("pool",
                  (lambda k_: lambda e: e.collective_compute(
                      "AllGather", ALU.bypass, replica_groups=RG,
                      ins=[yown_d[512 * k_:512 * k_ + 512, :].opt()],
                      outs=[yall_d[4096 * k_:4096 * k_ + 4096, :].opt()]))(k),
                  t_cc, list(ystore_futs))
            ystore_futs = []
    flush_defer()
    mi_ag_y = (t_cc, t_cc.n)

    if debug_stage >= 3:
        E("pool", lambda e: e.dma_start(out=dbg["yall"], in_=yall_d),
          inc=t_ld, waits=[mi_ag_y])
    if debug_stage == 3:
        return _finish(nc, P, stack)

    zbanks = [[OA[:, :], OB[:, :]], [SA[:, 0:512], SA[:, 512:1024]]]
    pend_stats = []
    for th in range(2):
        def _ldy(e, th_=th):
            pid = e.partition_id()
            return e.dma_start(out=a1[:, 16384 * th_:16384 * th_ + 16384]
                               .rearrange("p (cc t) -> p cc t", t=512),
                               in_=yall_d.rearrange("(R p) t -> p R t", p=128)
                               [:, bass.ts(pid, 32), 512 * th_:512 * th_ + 512])
        E("sp", _ldy, inc=L[th], waits=[mi_ag_y])
        free[("yld", th)] = (L[th], L[th].n)
    mi_y = [free[("yld", 0)], free[("yld", 1)]]
    for ot in range(NCC):
        wsl = ot % 2
        wf = f2[:, 8192 + 4096 * wsl:8192 + 4096 * wsl + 4096]
        wb = a3[:, 4096 * wsl:4096 * wsl + 4096]
        E("sp", lambda e, d=wf, o=ot:
          e.dma_start(out=d, in_=wprojT[:, 4096 * o:4096 * o + 4096]),
          inc=L[6 + wsl], waits=[free.get(("cwf", wsl)),
                                 mi_ag_y if ot < 2 else None,
                                 mi_V if ot < 2 else None])
        mi_wf = (L[6 + wsl], L[6 + wsl].n)
        E("dve", lambda e, d=wb, s_=wf: e.tensor_copy(out=d, in_=s_),
          inc=t_dve, waits=[mi_wf, free.get(("cwb", wsl))])
        mi_wb = (t_dve, t_dve.n)
        free[("cwf", wsl)] = mi_wb
        bks = zbanks[ot % 2]
        for cc in range(NCC):
            for th in range(2):
                E("pe", lambda e, b=bks[th], w=wb, cc_=cc, t_=th,
                  st=(cc == 0), sp_=(cc == NCC - 1):
                  e.matmul(b, w[:, 128 * cc_:128 * cc_ + 128],
                           a1[:, 16384 * t_ + 512 * cc_:16384 * t_ + 512 * cc_ + 512],
                           start=st, stop=sp_),
                  inc=t_pe if (cc == NCC - 1 and th == 1) else None,
                  waits=[mi_wb, mi_y[0], mi_y[1],
                         free.get(("zb", ot % 2))] if (cc == 0 and th == 0) else [])
        mi_mm = (t_pe, t_pe.n)
        free[("cwb", wsl)] = mi_mm
        free["yh"] = mi_mm
        for g_ in pend_stats:
            g_()
        pend_stats = []
        for th in range(2):
            par = 2 * (ot % 2) + th
            xsl = f3[:, 4096 + 512 * par:4096 + 512 * par + 512]
            x2t = f3[:, 6144 + 512 * par:6144 + 512 * par + 512]
            x2b = a2[:, 512 * par:512 * par + 512]
            E("sp", lambda e, d=xsl, o=ot, t_=th:
              e.dma_start(out=d, in_=xT[128 * o:128 * o + 128, 512 * t_:512 * t_ + 512]),
              inc=L[2 + par], waits=[free.get(("cx", par))])
            mi_x = (L[2 + par], L[2 + par].n)
            E("dve", lambda e, d=x2t, b=bks[th], x=xsl: e.tensor_add(d, b, x),
              inc=t_dve, waits=[mi_mm, mi_x, free.get(("cx2", par)), free.get(("cx2a", par))])
            mi_x2 = (t_dve, t_dve.n)
            free[("cx", par)] = mi_x2
            E("act", lambda e, s_=x2t, o=ot, t_=th:
              e.dma_start(out=x2fm_d[:, 1024 * o + 512 * t_:1024 * o + 512 * t_ + 512], in_=s_),
              inc=Strk[1 + par], waits=[mi_x2])
            free[("cx2a", par)] = (Strk[1 + par], Strk[1 + par].n)
            E("pool", lambda e, d=x2b, s_=x2t: e.tensor_copy(out=d, in_=s_),
              inc=t_pool, waits=[mi_x2, free.get(("cxb", par))])
            mi_xb = (t_pool, t_pool.n)
            free[("cx2", par)] = mi_xb
            sqb = a2[:, 2048 + 512 * th:2048 + 512 * th + 512]
            E("act", lambda e, d=sqb, s_=x2b: e.activation(out=d, in_=s_, func=AF.Square),
              inc=t_act, waits=[mi_xb, free.get(("csq", th))])
            mi_sq = (t_act, t_act.n)
            st, sp_ = (ot == 0), (ot == NCC - 1)
            sbank = SUMS if th == 0 else AUX
            fut_st = _Fut()

            def mk_stats(b=sbank, r1=x2b, r2=sqb, st_=st, sp__=sp_,
                         w1=mi_xb, w2=mi_sq, f=fut_st):
                def go():
                    E("pe", lambda e: e.matmul(b[0:1, :], ones_bf[:, :], r1,
                                               start=st_, stop=sp__),
                      waits=[w1])
                    E("pe", lambda e: e.matmul(b[32:33, :], ones_bf[:, :], r2,
                                               start=st_, stop=sp__),
                      inc=t_pe, waits=[w2])
                    f.trk, f.val = (t_pe, t_pe.n)
                return go
            pend_stats.append(mk_stats())
            free[("cxb", par)] = fut_st
            free[("csq", th)] = fut_st
        free[("zb", ot % 2)] = (t_dve, t_dve.n)
    for g_ in pend_stats:
        g_()
    pend_stats = []
    mi_cstats = (t_pe, t_pe.n)
    mi_x2fm = [(Strk[i], Strk[i].n) for i in range(1, 5)]

    mi_rows2 = ln_rows(mi_cstats, 2048)

    bcm = f3[:, 0:1024]
    bcr = f3[:, 1024:2048]
    mi_bc = None
    for h in range(2):
        m_h = mrow_sb[0:1, 2048 + 512 * h:2048 + 512 * h + 512]
        r_h = mrow_sb[0:1, 3072 + 512 * h:3072 + 512 * h + 512]
        E("pe", lambda e, r=m_h: e.matmul(OA[:, :], onesr_f[:, :], r, start=True, stop=True),
          inc=t_pe, waits=[mi_rows2, free.get(("zb", 0)), free.get(("zb", 1))])
        mi1 = (t_pe, t_pe.n)
        E("pe", lambda e, r=r_h: e.matmul(OB[:, :], onesr_f[:, :], r, start=True, stop=True),
          inc=t_pe)
        mi2 = (t_pe, t_pe.n)
        E("dve", lambda e, d=bcm, h_=h: e.tensor_copy(out=d[:, 512 * h_:512 * h_ + 512],
                                                      in_=OA[:, :]),
          inc=t_dve, waits=[mi1])
        E("dve", lambda e, d=bcr, h_=h: e.tensor_copy(out=d[:, 512 * h_:512 * h_ + 512],
                                                      in_=OB[:, :]),
          inc=t_dve, waits=[mi2])
        mi_bc = (t_dve, t_dve.n)
        free[("zb", 0)] = mi_bc
        free[("zb", 1)] = mi_bc

    for cc in range(NCC):
        sl = cc % 2
        x2s = f3[:, 4096 + 1024 * sl:4096 + 1024 * sl + 1024]
        tmp = f3[:, 2048 + 1024 * sl:2048 + 1024 * sl + 1024]
        E("sp", lambda e, d=x2s, c_=cc:
          e.dma_start(out=d, in_=x2fm_d[:, 1024 * c_:1024 * c_ + 1024]),
          inc=L[2 + sl], waits=(mi_x2fm + [mi_cstats] if cc < 2 else []) + [free.get(("h2x", sl))])
        mi_x = (L[2 + sl], L[2 + sl].n)
        E("pool", lambda e, d=tmp, s=x2s, m=bcm: e.tensor_sub(d, s, m),
          inc=t_pool, waits=[mi_x, mi_bc, free.get(("h2t", sl))])
        mi_t = (t_pool, t_pool.n)
        free[("h2x", sl)] = mi_t
        h2v = a2[:, 1024 * cc:1024 * cc + 1024]
        E("pool", lambda e, d=h2v, s=tmp, r=bcr: e.tensor_mul(d, s, r),
          inc=t_pool, waits=[mi_V if cc == 0 else None])
        mi_h = (t_pool, t_pool.n)
        free[("h2t", sl)] = mi_h
        E("dve", lambda e, d=hsum_sb[:, cc:cc + 1], s=h2v: e.reduce_sum(d, s, axis=AX),
          inc=t_dve, waits=[mi_h])
    mi_h2 = (t_pool, t_pool.n)
    mi_hsum = (t_dve, t_dve.n)

    for tt in range(8):
        sl = tt % 2
        xsl = f1[:, 4096 * sl:4096 * sl + 4096]
        stg = f1[:, 8192 + 4096 * sl:8192 + 4096 * sl + 4096]
        E("sp", lambda e, d=xsl, t_=tt:
          e.dma_start(out=d.rearrange("p (cc t) -> p cc t", t=128),
                      in_=x2fm_d.rearrange("p (cc t) -> p cc t", t=1024)
                      [:, 0:32, 128 * t_:128 * t_ + 128]),
          inc=L[8 + sl], waits=mi_x2fm + [free.get(("xt", sl)),
                            free.get("yh") if tt < 2 else None])
        mi_x = (L[8 + sl], L[8 + sl].n)
        for g in range(8):
            for j in range(4):
                cc = 4 * g + j
                E("pe", lambda e, s=xsl, c_=cc, j_=j:
                  e.transpose(AUX[:, 128 * j_:128 * j_ + 128],
                              s[:, 128 * c_:128 * c_ + 128], ident_sb[:, :]),
                  inc=t_pe if j == 3 else None,
                  waits=[mi_x, mi_rows2, free.get("xaux")] if j == 0 else [])
            mi_tr = (t_pe, t_pe.n)
            E("act", lambda e, d=stg, g_=g: e.copy(out=d[:, 512 * g_:512 * g_ + 512],
                                                   in_=AUX[:, :]),
              inc=t_act, waits=[mi_tr, free.get(("xstg", sl)) if g == 0 else None])
            free["xaux"] = (t_act, t_act.n)
        mi_stg = (t_act, t_act.n)
        free[("xt", sl)] = mi_stg
        E("act", lambda e, s=stg, t_=tt:
          e.dma_start(out=x2tm_d[128 * t_:128 * t_ + 128, :], in_=s),
          inc=Strk[3 + sl], waits=[mi_stg])
        free[("xstg", sl)] = (Strk[3 + sl], Strk[3 + sl].n)
    mi_x2tm = [(Strk[3], Strk[3].n), (Strk[4], Strk[4].n)]

    E("act", lambda e: e.dma_start(out=hsin_d, in_=hsum_sb[:, :]),
      inc=Strk[5], waits=[mi_hsum])
    mi_hs = (Strk[5], Strk[5].n)
    E("pool", lambda e: e.collective_compute(
        "AllReduce", ALU.add, replica_groups=RG, ins=[hsin_d.opt()], outs=[hsout_d.opt()]),
      inc=t_cc, waits=[mi_hs])
    mi_ar = (t_cc, t_cc.n)
    mcol = f3[:, 6144:6176]
    E("sp", lambda e: e.dma_start(out=mcol, in_=hsout_d), inc=L[1], waits=[mi_ar])
    mi_m = (L[1], L[1].n)
    E("dve", lambda e: e.tensor_scalar_mul(mcol, mcol, 1.0 / T), inc=t_dve, waits=[mi_m])
    mi_m2 = (t_dve, t_dve.n)
    wqs = f1[:, 0:4096]
    E("sp", lambda e: e.dma_start(out=wqs, in_=wqT),
      inc=L[3], waits=[free.get(("xt", 0))])
    mi_wq = (L[3], L[3].n)
    for cc in range(NCC):
        E("pe", lambda e, c_=cc, st=(cc == 0), sp_=(cc == NCC - 1):
          e.matmul(AUX[0:1, 0:128], mcol[:, c_:c_ + 1], wqs[:, 128 * c_:128 * c_ + 128],
                   start=st, stop=sp_),
          inc=t_pe if cc == NCC - 1 else None,
          waits=[mi_m2, mi_wq, free.get("xaux"), (t_act, P.tracks["act"].n)] if cc == 0 else [])
    mi_qmm = (t_pe, t_pe.n)
    qrow = f3[:, 6176:6304]
    E("dve", lambda e: e.tensor_copy(out=qrow[0:1, :], in_=AUX[0:1, 0:128]),
      inc=t_dve, waits=[mi_qmm])
    mi_qr = (t_dve, t_dve.n)
    E("pe", lambda e: e.transpose(AUX[:, 256:257], qrow[0:1, :], ident_sb[0:1, 0:1]),
      inc=t_pe, waits=[mi_qr])
    mi_qt = (t_pe, t_pe.n)
    E("dve", lambda e: e.tensor_copy(out=qT_sb[:, :], in_=AUX[:, 256:257]),
      inc=t_dve, waits=[mi_qt])
    mi_qT = (t_dve, t_dve.n)
    E("pe", lambda e: e.matmul(AUX[0:1, 0:8], qT_sb[:, :], keys_sb[:, :],
                               start=True, stop=True),
      inc=t_pe, waits=[mi_qT, mi_keys])
    mi_sc = (t_pe, t_pe.n)
    e8 = f3[:, 6304:6312]
    rt = f3[:, 6312:6320]
    E("dve", lambda e: e.reduce_max(rt[0:1, 0:1], AUX[0:1, 0:8], axis=AX, negate=True),
      inc=t_dve, waits=[mi_sc])
    E("dve", lambda e: e.tensor_scalar_mul(rt[0:1, 0:1], rt[0:1, 0:1], SCALE_MAIN),
      inc=t_dve)
    mi_ng = (t_dve, t_dve.n)
    E("act", lambda e: e.activation(out=e8[0:1, :], in_=AUX[0:1, 0:8], func=AF.Exp,
                                    scale=SCALE_MAIN, bias=rt[0:1, 0:1]),
      inc=t_act, waits=[mi_ng])
    mi_e8 = (t_act, t_act.n)
    E("dve", lambda e: e.reduce_sum(rt[0:1, 1:2], e8[0:1, :], axis=AX),
      inc=t_dve, waits=[mi_e8])
    E("dve", lambda e: e.reciprocal(rt[0:1, 2:3], rt[0:1, 1:2]), inc=t_dve)
    E("dve", lambda e: e.tensor_scalar_mul(e8[0:1, :], e8[0:1, :], rt[0:1, 2:3]),
      inc=t_dve)
    mi_cf = (t_dve, t_dve.n)
    E("pe", lambda e: e.matmul(AUX[:, 0:8], onesr_f[:, :], e8[0:1, :], start=True, stop=True),
      inc=t_pe, waits=[mi_cf])
    mi_cemm = (t_pe, t_pe.n)
    E("dve", lambda e: e.tensor_copy(out=ce_sb[:, :], in_=AUX[:, 0:8]),
      inc=t_dve, waits=[mi_cemm])
    mi_ce = (t_dve, t_dve.n)

    if debug_stage >= 4:
        E("pool", lambda e: e.dma_start(out=dbg["x2"], in_=x2tm_d), inc=t_ld,
          waits=mi_x2tm)
        E("act", lambda e: e.dma_start(
            out=dbg["h2"].rearrange("(cc p) t -> p cc t", p=128),
            in_=a2[:, :].rearrange("p (cc t) -> p cc t", t=1024)), inc=t_ld2,
          waits=[mi_h2])
        E("act", lambda e: e.dma_start(out=dbg["coeffs"], in_=e8[0:1, :]),
          inc=t_ld2, waits=[mi_cf])
    mi_dbg4 = (t_ld2, t_ld2.n) if debug_stage >= 4 else None
    if debug_stage == 4:
        return _finish(nc, P, stack)

    gpack = a3[0:64, 0:8192]
    gbanks = [OA, OB, SUMS, AUX]
    mi_gev = {}
    for e_ in range(8):
        sl = e_ % 2
        dwf = f1[:, 2048 * sl:2048 * sl + 2048]
        dwb = a3[:, 8192 + 2048 * sl:8192 + 2048 * sl + 2048]
        E("sp", lambda e, d=dwf, i=e_:
          e.dma_start(out=d, in_=downT[:, 2048 * i:2048 * i + 2048]),
          inc=L[4 + sl], waits=[free.get(("df", sl)), mi_qmm if e_ < 2 else None])
        mi_w = (L[4 + sl], L[4 + sl].n)
        E("pool", lambda e, d=dwb, s=dwf: e.tensor_copy(out=d, in_=s),
          inc=t_pool, waits=[mi_w, free.get(("db", sl))])
        mi_wb = (t_pool, t_pool.n)
        free[("df", sl)] = mi_wb
        for th in range(2):
            gi = (e_ * 2 + th) % 4
            bank = gbanks[gi]
            E_waits = [mi_wb, mi_h2, mi_gev.get(gi),
                       mi_ce if e_ == 0 else None]
            for cc in range(NCC):
                E("pe", lambda e, b=bank, w=dwb, cc_=cc, t_=th,
                  st=(cc == 0), sp_=(cc == NCC - 1):
                  e.matmul(b[0:64, :], w[:, 64 * cc_:64 * cc_ + 64],
                           a2[:, 1024 * cc_ + 512 * t_:1024 * cc_ + 512 * t_ + 512],
                           start=st, stop=sp_),
                  inc=t_pe if cc == NCC - 1 else None,
                  waits=E_waits if cc == 0 else [])
            mi_mm = (t_pe, t_pe.n)
            E("dve", lambda e, b=bank, i=e_, t_=th:
              e.tensor_scalar_mul(
                  gpack[0:64, 1024 * i + 512 * t_:1024 * i + 512 * t_ + 512],
                  b[0:64, :], ce_sb[0:64, i:i + 1]),
              inc=t_dve, waits=[mi_mm, mi_ce])
            mi_gev[gi] = (t_dve, t_dve.n)
        free[("db", sl)] = (t_pe, t_pe.n)
    mi_g = (t_dve, t_dve.n)
    if debug_stage == 41:
        return _finish(nc, P, stack)

    grp = [(SA, 0), (SA, 512), (SB, 0), (SB, 512),
           (OA, 0), (OB, 0), (SUMS, 0), (AUX, 0)]
    mi_outev = {}
    n_ot = 2 if debug_stage in (42, 43, 44) else 16
    skip_up = (debug_stage == 43)
    wbfH = [a3[:, 8192:12288], a3[:, 12288:16384]]
    wffH = [f1[:, 8192:12288], f1[:, 12288:16384]]
    upb = a1[0:64, 10240:12288]
    upb2 = a1[0:64, 12288:14336]
    mi_wcast = {}
    mi_wread = {}

    def d_wload(o):
        for hfd in range(2):
            E("sp", lambda e, d=wffH[hfd], o_=o, f=hfd:
              e.dma_start(out=d, in_=worgT[:, 8192 * o_ + 4096 * f:
                                           8192 * o_ + 4096 * f + 4096]),
              inc=L[6 + hfd], waits=[mi_wcast.get((o - 1, hfd)),
                                     mi_g if o == 0 else None,
                                     mi_dbg4 if o == 0 else None]
              + (mi_x2tm if o == 0 else []))
            mi_wf = (L[6 + hfd], L[6 + hfd].n)
            E("dve", lambda e, d=wbfH[hfd], s=wffH[hfd]: e.tensor_copy(out=d, in_=s),
              inc=t_dve, waits=[mi_wf, mi_wread.get(hfd)])
            mi_wcast[(o, hfd)] = (t_dve, t_dve.n)

    d_wload(0)
    for ot in range(n_ot):
        mi_wc = {h: mi_wcast[(ot, h)] for h in range(2)}
        usl = ot % 2
        po = 256 * usl
        upx = upb if usl == 0 else upb2
        upff = f1[0:64, 2048 * usl:2048 * usl + 2048]
        E("sp", lambda e, d=upff, o=ot:
          e.dma_start(out=d, in_=upw[:, 2048 * o:2048 * o + 2048]),
          inc=L[8 + usl], waits=[free.get(("uf", usl)), mi_g if ot < 2 else None]
          + (mi_x2tm if ot < 2 else []))
        mi_uf = (L[8 + usl], L[8 + usl].n)
        E("pool", lambda e, d=upx, s=upff: e.tensor_copy(out=d, in_=s),
          inc=t_pool, waits=[mi_uf, free.get(("ub", usl))])
        mi_up = (t_pool, t_pool.n)
        free[("uf", usl)] = mi_up
        mi_xg = {}
        for tt in range(4):
            E("sp", lambda e, d=f1[:, 4096 + 256 * tt:4096 + 256 * tt + 256], o=ot, t_=tt:
              e.dma_start(out=d, in_=x2tm_d[128 * t_:128 * t_ + 128, 256 * o:256 * o + 256]),
              inc=L[tt], waits=(mi_x2tm if ot == 0 else []) + [free.get(("ox", tt))])
            mi_xg[tt] = (L[tt], L[tt].n)
        mi_tt = {}
        for cc in range(NCC):
            hfd = cc // 16
            for tt in range(8):
                bnk, boff = grp[tt]
                w = []
                if cc % 16 == 0 and tt == 0:
                    w.append(mi_wc[hfd])
                if cc == 0:
                    w += [mi_g, mi_outev.get(tt)]
                last = (skip_up and cc == NCC - 1)
                E("pe", lambda e, b=bnk, o_=boff + po, cc_=cc, t_=tt, h_=hfd,
                  st=(cc == 0), sp_=last:
                  e.matmul(b[:, o_:o_ + 256],
                           a2[:, 1024 * cc_ + 128 * t_:1024 * cc_ + 128 * t_ + 128],
                           wbfH[h_][:, 256 * (cc_ % 16):256 * (cc_ % 16) + 256],
                           start=st, stop=sp_),
                  inc=t_pe if ((tt == 7 and cc % 16 == 15) or last) else None,
                  waits=w)
                if last:
                    mi_tt[tt] = (t_pe, t_pe.n)
            if cc % 16 == 15:
                mi_wread[cc // 16] = (t_pe, t_pe.n)
        if not skip_up:
            for e_ in range(8):
                for tt in range(8):
                    bnk, boff = grp[tt]
                    E("pe", lambda e, b=bnk, o_=boff + po, i=e_, t_=tt,
                      sp_=(e_ == 7), u=upx:
                      e.matmul(b[:, o_:o_ + 256],
                               gpack[0:64, 1024 * i + 128 * t_:1024 * i + 128 * t_ + 128],
                               u[0:64, 256 * i:256 * i + 256],
                               start=False, stop=sp_),
                      inc=t_pe if e_ == 7 else None,
                      waits=[mi_up] if (e_ == 0 and tt == 0) else [])
                    if e_ == 7:
                        mi_tt[tt] = (t_pe, t_pe.n)
        if ot + 1 < n_ot:
            d_wload(ot + 1)
        for tt in range(8):
            bnk, boff = grp[tt]
            off = boff + po
            mm_gate = mi_tt[tt]
            xsl = f1[:, 4096 + 256 * (tt % 4):4096 + 256 * (tt % 4) + 256]
            och = f1[:, 7168 + 256 * (tt % 4):7168 + 256 * (tt % 4) + 256]
            mi_x = mi_xg[tt % 4]
            E("dve", lambda e, d=och, b=bnk, o_=off, x=xsl:
              e.tensor_add(d, b[:, o_:o_ + 256], x),
              inc=t_dve, waits=[mm_gate, mi_x, free.get(("oo", tt % 4))])
            mi_o = (t_dve, t_dve.n)
            mi_outev[tt] = mi_o
            if tt < 4:
                E("sp", lambda e, d=xsl, o=ot, t_=tt + 4:
                  e.dma_start(out=d, in_=x2tm_d[128 * t_:128 * t_ + 128,
                                                256 * o:256 * o + 256]),
                  inc=L[tt], waits=[mi_o])
                mi_xg[tt] = (L[tt], L[tt].n)
            else:
                free[("ox", tt % 4)] = mi_o
            E("act", lambda e, s=och, o=ot, t_=tt:
              e.dma_start(out=out[128 * t_:128 * t_ + 128, 256 * o:256 * o + 256], in_=s),
              inc=Strk[tt % 4], waits=[mi_o])
            free[("oo", tt % 4)] = (Strk[tt % 4], Strk[tt % 4].n)
        free[("ub", usl)] = (t_pe, t_pe.n)

    return _finish(nc, P, stack)


def _finish(nc, P, stack):
    semstack = contextlib.ExitStack()
    sems = {name: semstack.enter_context(nc.semaphore(f"sem_{name}"))
            for name in P.tracks}
    final = [(t, t.n) for t in P.tracks.values() if t.n > 0]
    P.emit("sp", lambda e: e.nop(), waits=final)
    with nc.Block() as block:
        P.realize(block, sems)
    semstack.close()
    stack.close()
    return nc



def _prep(x, w_attn, w_proj, adapter_wte, gating, W_org, lora_down, lora_up,
          lora_keys, Wq):
    x = np.asarray(x, dtype=np.float32).reshape(T, C)
    xT = np.ascontiguousarray(x.T)
    w_attn = np.asarray(w_attn, dtype=np.float32)
    theta = 1.0 / (10000.0 ** (np.arange(0, HD, 2, dtype=np.float32) / HD))
    idx = np.outer(np.arange(T, dtype=np.float32), theta).astype(np.float32)
    idx2 = np.concatenate([idx, idx], axis=-1)
    cosT = np.ascontiguousarray(np.cos(idx2).astype(np.float32).T)
    sinST = np.ascontiguousarray(np.sin(idx2).astype(np.float32).T).copy()
    sinST[0:64, :] *= -1.0
    mk = np.zeros((4, 128, S), dtype=np.float32)
    qq = np.arange(S)[None, :]
    kk = np.arange(128)[:, None]
    for i, o in enumerate((0, 128, 256, 384)):
        mk[i] = np.where(qq >= kk + o, 0.0, NEG).astype(np.float32)
    shared = {
        "wprojT": np.ascontiguousarray(
            np.asarray(w_proj, np.float32).T.reshape(32, 128, 32, 128)
            .transpose(1, 2, 0, 3).reshape(128, 32 * 4096)),
        "worgT": np.ascontiguousarray(
            np.asarray(W_org, np.float32).T.reshape(32, 128, 16, 256)
            .transpose(1, 2, 0, 3).reshape(128, 16 * 8192)),
        "downT": np.ascontiguousarray(
            np.asarray(lora_down, np.float32).reshape(8, 32, 128, 64)
            .transpose(2, 0, 1, 3).reshape(128, 8 * 2048)),
        "upw": np.ascontiguousarray(
            np.asarray(lora_up, np.float32).reshape(8, 64, 16, 256)
            .transpose(1, 2, 0, 3).reshape(64, 16 * 2048)),
        "wqT": np.ascontiguousarray(
            np.asarray(Wq, np.float32).T.reshape(32, 128, 128)
            .transpose(1, 0, 2).reshape(128, 4096)),
        "keysT": np.ascontiguousarray(np.asarray(lora_keys, np.float32).T),
        "adT": np.ascontiguousarray(np.asarray(adapter_wte, np.float32).T),
        "gat": np.asarray(gating, np.float32).reshape(1, 1),
        "cosT": cosT,
        "sinS": sinST,
        "masks": mk,
    }
    in_maps = []
    for r in range(NCORES):
        im = dict(shared)
        im["xT"] = np.ascontiguousarray(xT[:, TOWN * r:TOWN * (r + 1)])
        rows = np.concatenate([w_attn[HC * r:HC * (r + 1)],
                               w_attn[C + HC * r:C + HC * (r + 1)],
                               w_attn[2 * C + HC * r:2 * C + HC * (r + 1)]], axis=0)
        im["wqkvT"] = np.ascontiguousarray(
            rows.T.reshape(32, 128, 12, 128).transpose(1, 2, 0, 3).reshape(128, 12 * 4096))
        in_maps.append(im)
    return in_maps


def kernel(L0_attn_x, L0_attn_w_attn, L0_attn_w_proj, L0_attn_adapter_wte,
           L0_attn_gating, L0_ffn_W_org, L0_ffn_lora_down, L0_ffn_lora_up,
           L0_ffn_lora_keys, L0_ffn_Wq):
    in_maps = _prep(L0_attn_x, L0_attn_w_attn, L0_attn_w_proj, L0_attn_adapter_wte,
                    L0_attn_gating, L0_ffn_W_org, L0_ffn_lora_down, L0_ffn_lora_up,
                    L0_ffn_lora_keys, L0_ffn_Wq)
    nc = _build(DEBUG_STAGE)
    res = run_bass_kernel_spmd(nc, in_maps, core_ids=list(range(NCORES)))
    if DEBUG_STAGE:
        kernel.last_results = res
        if "out" not in res.results[0]:
            return None
    outs = [res.results[r]["out"] for r in range(NCORES)]
    return np.concatenate(outs, axis=0).reshape(1, T, C)
